# version 07
# speedup vs baseline: 1.3085x; 1.1127x over previous
import numpy as np
import ml_dtypes
from contextlib import ExitStack

import concourse.bass as bass
import concourse.mybir as mybir
from concourse.bass_utils import run_bass_kernel_spmd

F32 = mybir.dt.float32
BF16 = mybir.dt.bfloat16
I32 = mybir.dt.int32
AF = mybir.ActivationFunctionType
ALU = mybir.AluOpType
AX = mybir.AxisListType
BF = ml_dtypes.bfloat16

NCORES = 8
B, S, D, H, DH = 2, 8192, 1024, 16, 64
CH = 512
NCH = S // CH
TOK = 2048
NOC = D // 128


def _chunks_of(j):
    return [(0, j), (0, 15 - j), (1, j), (1, 15 - j)]


def _rank_half(g):
    return (g, 0) if g < 8 else (15 - g, 1)


def _col_of(b, g):
    _, hf = _rank_half(g)
    return b * 1024 + hf * 512



def build_p1():
    nc = bass.Bass("TRN2", target_bir_lowering=False)
    x = nc.dram_tensor("x", [D, TOK], F32, kind="ExternalInput").ap()
    wts = {w: nc.dram_tensor("w" + w + "t", [D, D], F32, kind="ExternalInput").ap()
           for w in "qkv"}
    qt = nc.dram_tensor("qt", [D, TOK], BF16, kind="ExternalOutput").ap()
    kt = nc.dram_tensor("kt", [D, TOK], BF16, kind="ExternalOutput").ap()
    vt = nc.dram_tensor("vt", [D, TOK], BF16, kind="ExternalOutput").ap()

    groups = [(w, oc, t) for w in "qkv" for oc in range(NOC) for t in range(4)]
    NLD = 8 + 3 * 8

    es = ExitStack()
    with es:
        xbf = [es.enter_context(nc.sbuf_tensor(f"xbf{i}", [128, TOK], BF16)) for i in range(NOC)]
        wsb = {w: [es.enter_context(nc.sbuf_tensor(f"w{w}{i}", [128, D], BF16)) for i in range(NOC)]
               for w in "qkv"}
        osb = {w: [es.enter_context(nc.sbuf_tensor(f"o{w}{i}", [128, TOK], BF16)) for i in range(NOC)]
               for w in "qkv"}
        fv = es.enter_context(nc.sbuf_tensor("fv", [1, 8], F32))
        ps = [es.enter_context(nc.psum_tensor(f"ps{i}", [128, 512], F32)) for i in range(4)]
        ld = es.enter_context(nc.semaphore("ld"))
        mm = es.enter_context(nc.semaphore("mm"))
        ev = es.enter_context(nc.semaphore("ev"))
        st1 = es.enter_context(nc.semaphore("st1"))
        block = es.enter_context(nc.Block())

        out_d = {"q": qt, "k": kt, "v": vt}

        @block.gpsimd
        def _(g):
            for i in range(NOC):
                g.dma_start(out=xbf[i][:, :], in_=x[i * 128:(i + 1) * 128, :]).then_inc(ld, 16)
            for w in "qkv":
                for i in range(NOC):
                    g.dma_start(out=wsb[w][i][:, :], in_=wts[w][i * 128:(i + 1) * 128, :]).then_inc(ld, 16)

        @block.tensor
        def _(t_):
            t_.wait_ge(ld, 16 * NLD)
            for gi, (w, oc, t) in enumerate(groups):
                if gi >= 4:
                    t_.wait_ge(ev, gi - 3)
                for ic in range(NOC):
                    inst = t_.matmul(ps[gi % 4][:, :],
                                     wsb[w][ic][:, oc * 128:(oc + 1) * 128],
                                     xbf[ic][:, t * 512:(t + 1) * 512],
                                     start=(ic == 0), stop=(ic == NOC - 1))
                inst.then_inc(mm, 1)

        @block.vector
        def _(v_):
            for gi, (w, oc, t) in enumerate(groups):
                v_.wait_ge(mm, gi + 1)
                v_.tensor_copy(osb[w][oc][:, t * 512:(t + 1) * 512], ps[gi % 4][:, :])
                v_.tensor_copy(fv[:, 0:4], fv[:, 4:8]).then_inc(ev, 1)

        @block.sync
        def _(s_):
            for w, base in (("q", 0), ("k", 32), ("v", 64)):
                for oc in range(NOC):
                    s_.wait_ge(ev, base + oc * 4 + 4)
                    s_.dma_start(out=out_d[w][oc * 128:(oc + 1) * 128, :],
                                 in_=osb[w][oc][:, :]).then_inc(st1, 16)
            s_.wait_ge(st1, 16 * 24)
    return nc



P2_NCH = NCH


def build_p2():
    nc = bass.Bass("TRN2", target_bir_lowering=False)
    qh = nc.dram_tensor("qh", [4, 64, S], BF16, kind="ExternalInput").ap()
    kh = nc.dram_tensor("kh", [4, 64, S], BF16, kind="ExternalInput").ap()
    vh = nc.dram_tensor("vh", [4, 64, S], BF16, kind="ExternalInput").ap()
    mask = nc.dram_tensor("mask", [128, 2048], F32, kind="ExternalInput").ap()
    ones1 = nc.dram_tensor("ones1", [1, 64], F32, kind="ExternalInput").ap()
    ident2 = nc.dram_tensor("ident2", [128, 64], F32, kind="ExternalInput").ap()
    ao = nc.dram_tensor("ao", [4, 64, S], BF16, kind="ExternalOutput").ap()

    units = []
    epis = []
    si = 0
    nmask = 0
    for hp in range(2):
        for pp in range(2):
            p = 2 * hp + pp
            for qc in range(P2_NCH):
                for g in range(qc + 1):
                    for half in range(2):
                        diag = (g == qc)
                        first = (g == 0 and half == 0)
                        last = (g == qc and half == 1)
                        units.append((si, p, hp, pp, qc, g, half, diag, first, last, nmask))
                        if diag:
                            nmask += 1
                        si += 1
                epis.append((len(epis), p, hp, pp, qc, si - 1))
    NE = len(epis)
    e_of = {(p_, qc_): e_ for (e_, p_, hp_, pp_, qc_, sl_) in epis}
    avh_after_p = {}
    cum = 0
    for (si_, p, hp, pp, qc, g, half, diag, first, last, nm) in units:
        cum += 1
        avh_after_p[p] = cum

    es = ExitStack()
    with es:
        q2 = [es.enter_context(nc.sbuf_tensor(f"q2_{i}", [128, S], BF16)) for i in range(2)]
        k2 = [es.enter_context(nc.sbuf_tensor(f"k2_{i}", [128, S], BF16)) for i in range(2)]
        v2 = [es.enter_context(nc.sbuf_tensor(f"v2_{i}", [128, S], BF16)) for i in range(2)]
        msk = es.enter_context(nc.sbuf_tensor("msk", [128, 2048], BF16))
        on1 = es.enter_context(nc.sbuf_tensor("on1", [128, 64], BF16))
        idn2 = es.enter_context(nc.sbuf_tensor("idn2", [128, 64], BF16))
        pt = [es.enter_context(nc.sbuf_tensor(f"pt{i}", [128, 1024], BF16)) for i in range(4)]
        ptm = [es.enter_context(nc.sbuf_tensor(f"ptm{i}", [128, 1024], BF16)) for i in range(4)]
        vtk = [es.enter_context(nc.sbuf_tensor(f"vtk{i}", [128, 65 * 64], BF16)) for i in range(2)]
        zin = es.enter_context(nc.sbuf_tensor("zin", [128, 512], BF16))
        dsb = es.enter_context(nc.sbuf_tensor("dsb", [128, 512], F32))
        lnt = es.enter_context(nc.sbuf_tensor("lnt", [128, 512], F32))
        avs = [es.enter_context(nc.sbuf_tensor(f"avs{i}", [64, 512], F32)) for i in range(2)]
        aos = [es.enter_context(nc.sbuf_tensor(f"aos{i}", [64, 512], BF16)) for i in range(2)]
        fv = es.enter_context(nc.sbuf_tensor("fv", [1, 8], F32))
        fs = es.enter_context(nc.sbuf_tensor("fs", [1, 8], F32))
        ps_s = [es.enter_context(nc.psum_tensor(f"ps_s{i}", [128, 1024], F32)) for i in range(2)]
        ps_av = es.enter_context(nc.psum_tensor("ps_av", [128, 512], F32))
        ps_zb = es.enter_context(nc.psum_tensor("ps_zb", [64, 512], F32))
        trv = [es.enter_context(nc.psum_tensor(f"trv{i}", [128, 64], F32)) for i in range(2)]
        ld = es.enter_context(nc.semaphore("ld"))
        mm_s = es.enter_context(nc.semaphore("mm_s"))
        expd = es.enter_context(nc.semaphore("expd"))
        mskd = es.enter_context(nc.semaphore("mskd"))
        avh = es.enter_context(nc.semaphore("avh"))
        rcp = es.enter_context(nc.semaphore("rcp"))
        zrd = es.enter_context(nc.semaphore("zrd"))
        mm_zb = es.enter_context(nc.semaphore("mm_zb"))
        aoev = es.enter_context(nc.semaphore("aoev"))
        trm = es.enter_context(nc.semaphore("trm"))
        trvs = es.enter_context(nc.semaphore("trvs"))
        stao = [es.enter_context(nc.semaphore(f"stao{i}")) for i in range(2)]
        block = es.enter_context(nc.Block())

        NLD = 12 + 3

        @block.gpsimd
        def _(g_):
            for hp in range(2):
                g_.dma_start(out=q2[hp][0:64, :], in_=qh[2 * hp, :, :]).then_inc(ld, 16)
                g_.dma_start(out=q2[hp][64:128, :], in_=qh[2 * hp + 1, :, :]).then_inc(ld, 16)
                g_.dma_start(out=k2[hp][0:64, :], in_=kh[2 * hp, :, :]).then_inc(ld, 16)
                g_.dma_start(out=k2[hp][64:128, :], in_=kh[2 * hp + 1, :, :]).then_inc(ld, 16)
                g_.dma_start(out=v2[hp][0:64, :], in_=vh[2 * hp, :, :]).then_inc(ld, 16)
                g_.dma_start(out=v2[hp][64:128, :], in_=vh[2 * hp + 1, :, :]).then_inc(ld, 16)
            g_.dma_start(out=msk[:, :], in_=mask[:, :]).then_inc(ld, 16)
            g_.dma_start(out=on1[64:65, :], in_=ones1[:, :]).then_inc(ld, 16)
            g_.dma_start(out=idn2[:, :], in_=ident2[:, :]).then_inc(ld, 16)

        @block.tensor
        def _(t_):
            t_.wait_ge(ld, 16 * NLD)

            def emit_tr(t_, u):
                (si_, p, hp, pp, qc, g, half, diag, first, last, nm) = u
                rb = 64 * pp
                if p >= 2:
                    t_.wait_ge(avh, avh_after_p[p - 2])
                t_.wait_ge(mskd, 1)
                for kb in range(64):
                    if p * 64 + kb >= 2:
                        t_.wait_ge(trvs, p * 64 + kb - 1)
                    t_.matmul(trv[kb % 2][:, :],
                              v2[hp][rb:rb + 64, kb * 128:(kb + 1) * 128],
                              idn2[rb:rb + 64, :], start=True, stop=True).then_inc(trm, 1)

            def emit_scores(t_, u):
                (si_, p, hp, pp, qc, g, half, diag, first, last, nm) = u
                rb = 64 * pp
                if si_ >= 2:
                    t_.wait_ge(expd, si_ - 1)
                for kb2 in range(2):
                    key0 = g * 512 + (half * 2 + kb2) * 128
                    inst = t_.matmul(ps_s[si_ % 2][:, kb2 * 512:(kb2 + 1) * 512],
                                     k2[hp][rb:rb + 64, key0:key0 + 128],
                                     q2[hp][rb:rb + 64, qc * 512:(qc + 1) * 512],
                                     start=True, stop=True)
                inst.then_inc(mm_s, 1)

            pending_z = []

            def emit_av(t_, u):
                (si_, p, hp, pp, qc, g, half, diag, first, last, nm) = u
                if diag:
                    t_.wait_ge(mskd, nm + 2)
                else:
                    t_.wait_ge(expd, si_ + 2)
                if first:
                    e = e_of[(p, qc)]
                    if e >= 1:
                        t_.wait_ge(rcp, e)
                src_pt = ptm[si_ % 4] if diag else pt[si_ % 4]
                for kb2 in range(2):
                    blk = g * 4 + half * 2 + kb2
                    t_.wait_ge(trvs, p * 64 + blk + 1)
                    inst = t_.matmul(ps_av[0:65, :],
                                     vtk[p % 2][:, blk * 65:blk * 65 + 65],
                                     src_pt[:, kb2 * 512:(kb2 + 1) * 512],
                                     start=(first and kb2 == 0), stop=(last and kb2 == 1))
                inst.then_inc(avh, 1)
                if last:
                    pending_z.append((e_of[(p, qc)], si_))

            def emit_z(t_, e):
                t_.wait_ge(zrd, e + 1)
                if e >= 1:
                    t_.wait_ge(aoev, e)
                t_.matmul(ps_zb[:, :], on1[64:65, :], zin[64:65, :],
                          start=True, stop=True).then_inc(mm_zb, 1)

            U4 = len(units) // 4
            pair_starts = {0, U4, 2 * U4, 3 * U4}
            for idx in range(len(units)):
                emit_scores(t_, units[idx])
                while pending_z and pending_z[0][1] <= idx - 5:
                    emit_z(t_, pending_z.pop(0)[0])
                if idx >= 3:
                    emit_av(t_, units[idx - 3])
                if idx - 2 in pair_starts and idx - 2 >= 0:
                    emit_tr(t_, units[idx - 2])
            for k in (3, 2, 1):
                emit_av(t_, units[-k])
            while pending_z:
                emit_z(t_, pending_z.pop(0)[0])

        @block.scalar
        def _(s_):
            s_.memzero(fs[:, :])
            def emit_act(s_, u):
                (si_, p, hp, pp, qc, g, half, diag, first, last, nm) = u
                s_.wait_ge(mm_s, si_ + 1)
                if si_ >= 4:
                    s_.wait_ge(avh, si_ - 3)
                s_.activation(pt[si_ % 4][:, :], ps_s[si_ % 2][:, :], AF.Exp,
                              scale=0.125).then_inc(expd, 1)

            def emit_zepi(s_, e):
                s_.wait_ge(rcp, e + 1)
                if e >= 1:
                    s_.wait_ge(mm_zb, e)
                s_.activation(lnt[64:65, :], dsb[64:65, :], AF.Ln)
                s_.activation(zin[64:65, :], lnt[64:65, :], AF.Exp, scale=-1.0)
                s_.copy(fs[:, 0:4], fs[:, 4:8]).then_inc(zrd, 1)

            ei = 0
            for idx in range(len(units)):
                emit_act(s_, units[idx])
                if idx >= 1 and units[idx - 1][9]:
                    emit_zepi(s_, ei)
                    ei += 1
            s_.copy(fs[:, 0:4], fs[:, 4:8]).then_inc(expd, 2)
            emit_zepi(s_, ei)

        @block.vector
        def _(v_):
            v_.memset(fv[:, :], 0.0)
            v_.memset(vtk[0][:, :], 1.0)
            v_.memset(vtk[1][:, :], 1.0)
            v_.tensor_copy(fv[:, 0:4], fv[:, 4:8]).then_inc(mskd, 1)

            def emit_mul(v_, e):
                v_.wait_ge(mm_zb, e + 1)
                if e >= 2:
                    v_.wait_ge(stao[e % 2], 16 * ((e - 2) // 2 + 1))
                v_.tensor_mul(aos[e % 2][:, :], avs[e % 2][:, :], ps_zb[:, :])
                v_.tensor_copy(fv[:, 0:4], fv[:, 4:8]).then_inc(aoev, 1)

            nt = 0
            ei = 0
            for (si_, p, hp, pp, qc, g, half, diag, first, last, nm) in units:
                if qc == 0 and g == 0 and half == 0:
                    for kb in range(64):
                        v_.wait_ge(trm, nt + 1)
                        v_.tensor_copy(vtk[p % 2][:, kb * 65:kb * 65 + 64], trv[kb % 2][:, :])
                        v_.tensor_copy(fv[:, 0:4], fv[:, 4:8]).then_inc(trvs, 1)
                        nt += 1
                if diag:
                    v_.wait_ge(expd, si_ + 2)
                    v_.tensor_tensor(ptm[si_ % 4][:, :], pt[si_ % 4][:, :],
                                     msk[:, half * 1024:(half + 1) * 1024], ALU.mult)
                    v_.tensor_copy(fv[:, 0:4], fv[:, 4:8]).then_inc(mskd, 1)
                if last:
                    (e, p_, hp_, pp_, qc_, sl_) = epis[ei]
                    v_.wait_ge(avh, si_ + 1)
                    v_.tensor_copy(dsb[64:65, :], ps_av[64:65, :])
                    v_.tensor_copy(avs[e % 2][:, :], ps_av[0:64, :])
                    v_.tensor_copy(fv[:, 0:4], fv[:, 4:8]).then_inc(rcp, 1)
                    if e >= 1:
                        emit_mul(v_, e - 1)
                    ei += 1
            emit_mul(v_, NE - 1)

        @block.sync
        def _(s_):
            for (e, p, hp, pp, qc, sl_) in epis:
                s_.wait_ge(aoev, e + 1)
                s_.dma_start(out=ao[p, :, qc * 512:(qc + 1) * 512],
                             in_=aos[e % 2][:, :]).then_inc(stao[e % 2], 16)
            s_.wait_ge(stao[0], 16 * (NE // 2))
            s_.wait_ge(stao[1], 16 * (NE // 2))
    return nc



def _stage_xT(xfull):
    out = []
    for j in range(NCORES):
        cols = [np.ascontiguousarray(xfull[b, c * CH:(c + 1) * CH, :].T)
                for (b, c) in _chunks_of(j)]
        out.append(np.ascontiguousarray(np.concatenate(cols, axis=1)))
    return out


def _unstage_xT(slabs):
    xfull = np.empty((B, S, D), np.asarray(slabs[0]).dtype)
    for j in range(NCORES):
        for i, (b, c) in enumerate(_chunks_of(j)):
            xfull[b, c * CH:(c + 1) * CH, :] = np.asarray(slabs[j])[:, i * CH:(i + 1) * CH].T
    return xfull


def _causal_mask_strips():
    mask = np.zeros((128, 2048), np.float32)
    for kb in range(4):
        kk = np.arange(128)[:, None] + kb * 128
        qq = np.arange(512)[None, :]
        mask[:, kb * 512:(kb + 1) * 512] = (kk <= qq).astype(np.float32)
    return mask


def _assemble_bh(qt_all, kt_all, vt_all):
    qh_all, kh_all, vh_all = [], [], []
    for j in range(NCORES):
        qh = np.empty((4, 64, S), BF)
        kh = np.empty((4, 64, S), BF)
        vh = np.empty((4, 64, S), BF)
        for p in range(4):
            bh = 4 * j + p
            b, h = bh // 16, bh % 16
            for g in range(NCH):
                r, _hf = _rank_half(g)
                col = _col_of(b, g)
                qh[p][:, g * CH:(g + 1) * CH] = qt_all[r][h * 64:(h + 1) * 64, col:col + CH]
                kh[p][:, g * CH:(g + 1) * CH] = kt_all[r][h * 64:(h + 1) * 64, col:col + CH]
                vh[p][:, g * CH:(g + 1) * CH] = vt_all[r][h * 64:(h + 1) * 64, col:col + CH]
        qh_all.append(qh)
        kh_all.append(kh)
        vh_all.append(vh)
    return qh_all, kh_all, vh_all


def _assemble_attnT(ao_all):
    out = []
    for j in range(NCORES):
        at = np.empty((D, TOK), BF)
        for i, (b, c) in enumerate(_chunks_of(j)):
            for h in range(H):
                bh = b * 16 + h
                r, p = bh // 4, bh % 4
                at[h * 64:(h + 1) * 64, i * CH:(i + 1) * CH] = \
                    ao_all[r][p][:, c * CH:(c + 1) * CH]
        out.append(at)
    return out


def kernel(**inputs):
    raise NotImplementedError("under construction")



SERIALIZE = False


class Sch:
    ENGS = ("tensor", "vector", "scalar", "gpsimd", "sync")

    def __init__(self, nc, es):
        self.nc = nc
        self.es = es
        self.ops = {e: [] for e in self.ENGS}
        self.cnt = {}
        self.sems = {}
        self.dma_sems = set()

    def _sem(self, name):
        if name not in self.sems:
            self.sems[name] = self.es.enter_context(self.nc.semaphore(name))
            self.cnt[name] = 0
        return self.sems[name]

    def emit(self, eng, fn, waits=(), inc=None, k=1, fence=False):
        if SERIALIZE is True or (SERIALIZE and eng in SERIALIZE):
            waits = list(waits) + [(s, v) for (s, v) in self.cnt.items() if v > 0]
        waits = [(s, self.cnt.get(s, 0) if s in self.dma_sems and v > 0 else v)
                 for (s, v) in waits]
        waits = [(s, v) for (s, v) in waits if v > 0]
        for s, _ in waits:
            self._sem(s)
        if inc is not None:
            self._sem(inc)
            if k >= 16:
                self.dma_sems.add(inc)
            self.cnt[inc] += k
            tok = self.cnt[inc]
        else:
            tok = None
        self.ops[eng].append((waits, fn, inc, k, fence))
        return tok

    def build(self, block):
        fbuf = {}
        for eng in ("vector", "scalar", "gpsimd"):
            fbuf[eng] = self.es.enter_context(
                self.nc.sbuf_tensor(f"_fence_{eng}", [1, 8], F32))

        def mk(eng):
            def body(h):
                if eng == "vector" or eng == "gpsimd":
                    h.memset(fbuf[eng][:, :], 0.0)
                elif eng == "scalar":
                    h.memzero(fbuf[eng][:, :])
                for (waits, fn, inc, k, fence) in self.ops[eng]:
                    for (s, v) in waits:
                        h.wait_ge(self.sems[s], v)
                    inst = fn(h) if fn is not None else None
                    if inc is not None:
                        if fence:
                            if eng == "vector" or eng == "gpsimd":
                                inst = h.tensor_copy(fbuf[eng][:, 0:4], fbuf[eng][:, 4:8])
                            elif eng == "scalar":
                                inst = h.copy(fbuf[eng][:, 0:4], fbuf[eng][:, 4:8])
                        inst.then_inc(self.sems[inc], k)
            return body

        block.tensor(mk("tensor"))
        block.vector(mk("vector"))
        block.scalar(mk("scalar"))
        block.gpsimd(mk("gpsimd"))
        block.sync(mk("sync"))


def _emit_ln(sch, sbf, s2bf, sfull, onescol, onesrow, ps_stat, ps_b, scratch,
             dst_f32, dst_bf, eps, gcol=None, bcol=None, pre_tokens=(),
             bc_waits=(), apply_waits=(), tag="", dbg=None):
    msb, m2, var, rstd, mr, rstd_bf, c_bf = scratch
    t_m = sch.emit("tensor", lambda h: [
        h.matmul(ps_stat[0][0:1, :], onescol[:, 0:1], sbf[ic][:, :],
                 start=(ic == 0), stop=(ic == NOC - 1)) for ic in range(NOC)][-1],
        waits=list(pre_tokens), inc="mst" + tag)
    t_q = sch.emit("tensor", lambda h: [
        h.matmul(ps_stat[1][0:1, :], onescol[:, 0:1], s2bf[ic][:, :],
                 start=(ic == 0), stop=(ic == NOC - 1)) for ic in range(NOC)][-1],
        inc="mst" + tag)

    def fin1(h):
        if dbg is not None:
            h.tensor_copy(dbg[0][0:1, dbg[4] * 512:(dbg[4] + 1) * 512], ps_stat[0][0:1, :])
            h.tensor_copy(dbg[1][0:1, dbg[4] * 512:(dbg[4] + 1) * 512], ps_stat[1][0:1, :])
        h.tensor_scalar_mul(msb[0:1, :], ps_stat[0][0:1, :], 1.0 / D)
        h.tensor_mul(m2[0:1, :], msb[0:1, :], msb[0:1, :])
        h.scalar_tensor_tensor(var[0:1, :], ps_stat[1][0:1, :], 1.0 / D,
                               m2[0:1, :], ALU.mult, ALU.subtract)
        h.tensor_scalar_add(m2[0:1, :], var[0:1, :], float(eps))
        if dbg is not None:
            h.tensor_copy(dbg[2][0:1, dbg[4] * 512:(dbg[4] + 1) * 512], var[0:1, :])
        return h.reciprocal(var[0:1, :], m2[0:1, :])
    t_f1 = sch.emit("vector", fin1, waits=[("mst" + tag, t_q)], inc="stf" + tag, fence=True)

    t_ln = sch.emit("scalar", lambda h: h.activation(rstd[0:1, :], var[0:1, :], AF.Sqrt),
                    waits=[("stf" + tag, t_f1)], inc="sln" + tag, fence=True)

    def fin2(h):
        h.tensor_mul(mr[0:1, :], msb[0:1, :], rstd[0:1, :])
        h.tensor_copy(rstd_bf[0:1, :], rstd[0:1, :])
        return h.tensor_copy(c_bf[0:1, :], mr[0:1, :])
    t_f2 = sch.emit("vector", fin2, waits=[("sln" + tag, t_ln)], inc="stf2" + tag, fence=True)

    def bcast(h):
        h.matmul(ps_b[0][:, :], onesrow[0:1, :], rstd_bf[0:1, :], start=True, stop=True)
        return h.matmul(ps_b[1][:, :], onesrow[0:1, :], c_bf[0:1, :], start=True, stop=True)
    t_bc = sch.emit("tensor", bcast, waits=[("stf2" + tag, t_f2)] + list(bc_waits),
                    inc="mbc" + tag)

    toks = []
    for oc in range(NOC):
        def apply(h, oc=oc):
            h.tensor_mul(dst_f32[oc][:, :], sfull[oc][:, :], ps_b[0][:, :])
            inst = h.tensor_sub(dst_f32[oc][:, :], dst_f32[oc][:, :], ps_b[1][:, :])
            if gcol is not None:
                inst = h.tensor_scalar(dst_f32[oc][:, :], dst_f32[oc][:, :],
                                       gcol[oc], bcol[oc], ALU.mult, ALU.add)
            if dst_bf is not None:
                inst = h.tensor_copy(dst_bf[oc][:, :], dst_f32[oc][:, :])
            return inst
        toks.append(sch.emit("vector", apply,
                             waits=[("mbc" + tag, t_bc)] + list(apply_waits[oc] if apply_waits else []),
                             inc="apw" + tag, fence=True))
    return toks



HID = 3072
NHC = HID // 128


DBG_NT = 4


def build_p3():
    nc = bass.Bass("TRN2", target_bir_lowering=False)
    x = nc.dram_tensor("x", [D, TOK], F32, kind="ExternalInput").ap()
    at = nc.dram_tensor("at", [D, TOK], BF16, kind="ExternalInput").ap()
    wot = nc.dram_tensor("wot", [D, D], F32, kind="ExternalInput").ap()
    wgt = nc.dram_tensor("wgt", [D, HID], I32, kind="ExternalInput").ap()
    wut = nc.dram_tensor("wut", [D, HID], I32, kind="ExternalInput").ap()
    wdt = nc.dram_tensor("wdt", [HID, D], I32, kind="ExternalInput").ap()
    x1 = nc.dram_tensor("x1", [D, TOK], F32, kind="ExternalOutput").ap()

    es = ExitStack()
    with es:
        wo = [es.enter_context(nc.sbuf_tensor(f"wo{i}", [128, D], BF16)) for i in range(NOC)]
        xt = [es.enter_context(nc.sbuf_tensor(f"xt{i}", [128, 512], F32)) for i in range(NOC)]
        att = [es.enter_context(nc.sbuf_tensor(f"att{i}", [128, 512], BF16)) for i in range(NOC)]
        sfull = [es.enter_context(nc.sbuf_tensor(f"sf{i}", [128, 512], F32)) for i in range(NOC)]
        sbf = [es.enter_context(nc.sbuf_tensor(f"sb{i}", [128, 512], BF16)) for i in range(NOC)]
        s2bf = [es.enter_context(nc.sbuf_tensor(f"s2{i}", [128, 512], BF16)) for i in range(NOC)]
        ybuf = [es.enter_context(nc.sbuf_tensor(f"yb{i}", [128, 512], F32)) for i in range(NOC)]
        ybf = [es.enter_context(nc.sbuf_tensor(f"yf{i}", [128, 512], BF16)) for i in range(NOC)]
        hbuf = [es.enter_context(nc.sbuf_tensor(f"hb{i}", [128, 512], BF16)) for i in range(NHC)]
        hsig = [es.enter_context(nc.sbuf_tensor(f"hs{i}", [128, 512], BF16)) for i in range(2)]
        gpan = [es.enter_context(nc.sbuf_tensor(f"gp{i}", [128, D], BF16)) for i in range(2)]
        upan = [es.enter_context(nc.sbuf_tensor(f"up{i}", [128, D], BF16)) for i in range(2)]
        dpan = [es.enter_context(nc.sbuf_tensor(f"dp{i}", [128, HID], BF16)) for i in range(2)]
        x1b = [[es.enter_context(nc.sbuf_tensor(f"x1b{z}_{i}", [128, 512], F32)) for i in range(NOC)]
               for z in range(2)]
        onescol = es.enter_context(nc.sbuf_tensor("onescol", [128, 1], BF16))
        onesrow = es.enter_context(nc.sbuf_tensor("onesrow", [1, 128], BF16))
        lns = [es.enter_context(nc.sbuf_tensor(f"lns{i}", [1, 512], F32)) for i in range(5)]
        lnb = [es.enter_context(nc.sbuf_tensor(f"lnb{i}", [1, 512], BF16)) for i in range(2)]
        scratch = (lns[0], lns[1], lns[2], lns[3], lns[4], lnb[0], lnb[1])
        P = [es.enter_context(nc.psum_tensor(f"P{i}", [128, 512], F32)) for i in range(4)]
        Sst = [es.enter_context(nc.psum_tensor(f"Sst{i}", [1, 512], F32)) for i in range(2)]
        Bb = [es.enter_context(nc.psum_tensor(f"Bb{i}", [128, 512], F32)) for i in range(2)]
        sch = Sch(nc, es)
        block = es.enter_context(nc.Block())

        sch.emit("vector", lambda h: h.memset(onescol[:, :], 1.0))
        t_ones = sch.emit("vector", lambda h: h.memset(onesrow[:, :], 1.0), inc="ones", fence=True)
        t_wo = None
        for i in range(NOC):
            t_wo = sch.emit("gpsimd", lambda h, i=i: h.dma_start(
                out=wo[i][:, :], in_=wot[i * 128:(i + 1) * 128, :]), inc="ldw", k=16)

        prev = {"evsA": {}, "mo": {}, "ldx": {}, "lda": {}, "apwA": {}, "apwB": {},
                "f1A": 0, "f1B": 0, "ev2": {}, "sto": {}}
        last_bc_readers = [("ones", t_ones)]
        last_stat_users = []

        for t in range(DBG_NT):
            tok_x, tok_at = {}, {}
            for oc in range(NOC):
                w = [("evsA", prev["evsA"].get(oc, 0))]
                tok_x[oc] = sch.emit("sync", lambda h, oc=oc, t=t: h.dma_start(
                    out=xt[oc][:, :], in_=x[oc * 128:(oc + 1) * 128, t * 512:(t + 1) * 512]),
                    waits=w, inc="ldx", k=16)
            for ic in range(NOC):
                w = [("mo", prev["mo"].get(7, 0)), ("evsA", prev["evsA"].get(7, 0))]
                tok_at[ic] = sch.emit("sync", lambda h, ic=ic, t=t: h.dma_start(
                    out=att[ic][:, :], in_=at[ic * 128:(ic + 1) * 128, t * 512:(t + 1) * 512]),
                    waits=w, inc="lda", k=16)
            tok_mo, tok_s, tok_q = {}, {}, {}
            for oc in range(NOC):
                w = [("ldw", t_wo), ("lda", tok_at[7])]
                if oc >= 2:
                    w.append(("evsA", tok_s[oc - 2]))
                elif t > 0:
                    w.append(("evsA", prev["evsA"][oc + 6]))
                tok_mo[oc] = sch.emit("tensor", lambda h, oc=oc: [
                    h.matmul(P[oc % 2][:, :], wo[ic][:, oc * 128:(oc + 1) * 128],
                             att[ic][:, :], start=(ic == 0), stop=(ic == NOC - 1))
                    for ic in range(NOC)][-1], waits=w, inc="mo")

                def mks(h, oc=oc):
                    h.tensor_add(sfull[oc][:, :], xt[oc][:, :], P[oc % 2][:, :])
                    return h.tensor_copy(sbf[oc][:, :], sfull[oc][:, :])
                tok_s[oc] = sch.emit("vector", mks,
                                     waits=[("mo", tok_mo[oc]), ("ldx", tok_x[oc]),
                                            ("mstB", prev.get("mstB", 0))],
                                     inc="evsA", fence=True)
                tok_q[oc] = sch.emit("scalar", lambda h, oc=oc: h.activation(
                    s2bf[oc][:, :], sfull[oc][:, :], AF.Square),
                    waits=[("evsA", tok_s[oc])], inc="sqA", fence=True)
            pre = [("evsA", tok_s[7]), ("sqA", tok_q[7]), ("ones", t_ones)] + last_stat_users
            apA = _emit_ln(sch, sbf, s2bf, sfull, onescol, onesrow, Sst, Bb, scratch,
                           ybuf, ybf, 1e-5, pre_tokens=pre,
                           bc_waits=last_bc_readers, tag="A")
            last_bc_readers = [("apwA", apA[7])]
            last_stat_users = [("stfA", sch.cnt["stfA"])]
            mstA_now = sch.cnt["mstA"]

            tok_pan, tok_g, tok_u, tok_sil, tok_h = {}, {}, {}, {}, {}
            for hc in range(NHC):
                w = []
                if hc >= 2:
                    w = [("sil", tok_sil[hc - 2]), ("ehh", tok_h[hc - 2])]

                sch.emit("gpsimd", lambda h, hc=hc: h.dma_start(
                    out=gpan[hc % 2][:, :],
                    in_=wgt[:, hc * 128:(hc + 1) * 128].rearrange("(ic p) c -> p ic c", p=128)),
                    waits=w, inc="ldp", k=16)
                tok_pan[hc] = sch.emit("gpsimd", lambda h, hc=hc: h.dma_start(
                    out=upan[hc % 2][:, :],
                    in_=wut[:, hc * 128:(hc + 1) * 128].rearrange("(ic p) c -> p ic c", p=128)),
                    inc="ldp", k=16)
                wmm = [("ldp", tok_pan[hc]), ("apwA", apA[7])]
                if hc >= 2:
                    wmm += [("sil", tok_sil[hc - 2]), ("ehh", tok_h[hc - 2])]
                tok_g[hc] = sch.emit("tensor", lambda h, hc=hc: [
                    h.matmul(P[2 * (hc % 2)][:, :], gpan[hc % 2][:, ic * 128:(ic + 1) * 128],
                             ybf[ic][:, :], start=(ic == 0), stop=(ic == NOC - 1))
                    for ic in range(NOC)][-1], waits=wmm, inc="mg")
                tok_u[hc] = sch.emit("tensor", lambda h, hc=hc: [
                    h.matmul(P[2 * (hc % 2) + 1][:, :], upan[hc % 2][:, ic * 128:(ic + 1) * 128],
                             ybf[ic][:, :], start=(ic == 0), stop=(ic == NOC - 1))
                    for ic in range(NOC)][-1], inc="mu")
                wsil = [("mg", tok_g[hc])]
                if hc >= 2:
                    wsil.append(("ehh", tok_h[hc - 2]))
                tok_sil[hc] = sch.emit("scalar", lambda h, hc=hc: h.activation(
                    hsig[hc % 2][:, :], P[2 * (hc % 2)][:, :], AF.Silu, scale=1.0 / 127.0),
                    waits=wsil, inc="sil", fence=True)
                w = [("mu", tok_u[hc]), ("sil", tok_sil[hc])]
                if t > 0:
                    w.append(("md", prev_md_last))
                tok_h[hc] = sch.emit("vector", lambda h, hc=hc: h.scalar_tensor_tensor(
                    hbuf[hc][:, :], P[2 * (hc % 2) + 1][:, :], 1.0 / 127.0,
                    hsig[hc % 2][:, :], ALU.mult, ALU.mult),
                    waits=w, inc="ehh", fence=True)

            tok_dp, tok_md, tok_s2, tok_q2 = {}, {}, {}, {}
            for oc in range(NOC):
                w = []
                if oc >= 1:
                    w = [("md", tok_md[oc - 1])]
                tok_dp[oc] = sch.emit("gpsimd", lambda h, oc=oc: h.dma_start(
                    out=dpan[oc % 2][:, :],
                    in_=wdt[:, oc * 128:(oc + 1) * 128].rearrange("(hc p) c -> p hc c", p=128)),
                    waits=w, inc="ldd", k=16)
                wmm = [("ldd", tok_dp[oc]), ("ehh", tok_h[NHC - 1]),
                       ("sil", tok_sil[NHC - 1])]
                if oc >= 2:
                    wmm.append(("ev2", tok_s2[oc - 2]))
                tok_md[oc] = sch.emit("tensor", lambda h, oc=oc: [
                    h.matmul(P[oc % 2][:, :], dpan[oc % 2][:, hc * 128:(hc + 1) * 128],
                             hbuf[hc][:, :], start=(hc == 0), stop=(hc == NHC - 1))
                    for hc in range(NHC)][-1], waits=wmm, inc="md")

                def mks2(h, oc=oc):
                    h.scalar_tensor_tensor(sfull[oc][:, :], P[oc % 2][:, :],
                                           1.0 / 127.0, ybuf[oc][:, :], ALU.mult, ALU.add)
                    return h.tensor_copy(sbf[oc][:, :], sfull[oc][:, :])
                tok_s2[oc] = sch.emit("vector", mks2,
                                      waits=[("md", tok_md[oc]), ("apwA", apA[oc]),
                                             ("mstA", mstA_now)],
                                      inc="ev2", fence=True)
                tok_q2[oc] = sch.emit("scalar", lambda h, oc=oc: h.activation(
                    s2bf[oc][:, :], sfull[oc][:, :], AF.Square),
                    waits=[("ev2", tok_s2[oc])], inc="sqB", fence=True)
            prev_md_last = tok_md[NOC - 1]
            pre = [("ev2", tok_s2[7]), ("sqB", tok_q2[7])] + last_stat_users
            ap_extra = [[("sto", prev["sto"].get(oc, 0))] for oc in range(NOC)]
            apB = _emit_ln(sch, sbf, s2bf, sfull, onescol, onesrow, Sst, Bb, scratch,
                           x1b[t % 2], None, 1e-5, pre_tokens=pre,
                           bc_waits=last_bc_readers, apply_waits=ap_extra, tag="B")
            last_bc_readers = [("apwB", apB[7])]
            last_stat_users = [("stfB", sch.cnt["stfB"])]
            tok_sto = {}
            for oc in range(NOC):
                tok_sto[oc] = sch.emit("sync", lambda h, oc=oc, t=t: h.dma_start(
                    out=x1[oc * 128:(oc + 1) * 128, t * 512:(t + 1) * 512],
                    in_=x1b[t % 2][oc][:, :]),
                    waits=[("apwB", apB[oc])], inc="sto", k=16)
            prev = {"evsA": tok_s, "mo": tok_mo, "ldx": tok_x, "lda": tok_at,
                    "apwA": apA, "apwB": apB, "ev2": tok_s2, "sto": tok_sto,
                    "mstB": sch.cnt["mstB"]}

        sch.emit("sync", None, waits=[("sto", sch.cnt["sto"])])
        sch.build(block)
    return nc



def build_p5():
    nc = bass.Bass("TRN2", target_bir_lowering=False)
    x1h = nc.dram_tensor("x1h", [D, 2080], F32, kind="ExternalInput").ap()
    x1 = nc.dram_tensor("x1", [D, TOK], F32, kind="ExternalInput").ap()
    w1m = nc.dram_tensor("w1m", [32, 5, 128, 128], F32, kind="ExternalInput").ap()
    w2m = nc.dram_tensor("w2m", [8, 4, 5, 128, 128], F32, kind="ExternalInput").ap()
    b1c = nc.dram_tensor("b1c", [128, 32], F32, kind="ExternalInput").ap()
    b2c = nc.dram_tensor("b2c", [128, 8], F32, kind="ExternalInput").ap()
    g0c = nc.dram_tensor("g0c", [128, 8], F32, kind="ExternalInput").ap()
    b0c = nc.dram_tensor("b0c", [128, 8], F32, kind="ExternalInput").ap()
    emk = nc.dram_tensor("emk", [128, 32], F32, kind="ExternalInput").ap()
    h0 = nc.dram_tensor("h0", [D, TOK], F32, kind="ExternalOutput").ap()

    NST = 8
    es = ExitStack()
    with es:
        w1sb = es.enter_context(nc.sbuf_tensor("w1sb", [128, 32 * 5 * 128], BF16))
        w2sb = es.enter_context(nc.sbuf_tensor("w2sb", [128, 8 * 4 * 5 * 128], BF16))
        b1s = es.enter_context(nc.sbuf_tensor("b1s", [128, 32], F32))
        b2s = es.enter_context(nc.sbuf_tensor("b2s", [128, 8], F32))
        g0s = es.enter_context(nc.sbuf_tensor("g0s", [128, 8], F32))
        b0s = es.enter_context(nc.sbuf_tensor("b0s", [128, 8], F32))
        emks = es.enter_context(nc.sbuf_tensor("emks", [128, 32], BF16))
        xw = [es.enter_context(nc.sbuf_tensor(f"xw{i}", [128, 264], BF16)) for i in range(NOC)]
        x1t = [es.enter_context(nc.sbuf_tensor(f"x1t{i}", [128, 256], F32)) for i in range(NOC)]
        h1s = [es.enter_context(nc.sbuf_tensor(f"h1s{i}", [128, 260], BF16)) for i in range(32)]
        sfull = [es.enter_context(nc.sbuf_tensor(f"sf{i}", [128, 256], F32)) for i in range(NOC)]
        sbf = [es.enter_context(nc.sbuf_tensor(f"sb{i}", [128, 256], BF16)) for i in range(NOC)]
        s2bf = [es.enter_context(nc.sbuf_tensor(f"s2{i}", [128, 256], BF16)) for i in range(NOC)]
        h0b = [[es.enter_context(nc.sbuf_tensor(f"h0b{z}_{i}", [128, 256], F32)) for i in range(NOC)]
               for z in range(2)]
        onescol = es.enter_context(nc.sbuf_tensor("onescol", [128, 1], BF16))
        onesrow = es.enter_context(nc.sbuf_tensor("onesrow", [1, 128], BF16))
        lns = [es.enter_context(nc.sbuf_tensor(f"lns{i}", [1, 256], F32)) for i in range(5)]
        lnb = [es.enter_context(nc.sbuf_tensor(f"lnb{i}", [1, 256], BF16)) for i in range(2)]
        scratch = (lns[0], lns[1], lns[2], lns[3], lns[4], lnb[0], lnb[1])
        C = [es.enter_context(nc.psum_tensor(f"C{i}", [128, 260], F32)) for i in range(2)]
        Dp = [es.enter_context(nc.psum_tensor(f"D{i}", [128, 256], F32)) for i in range(2)]
        Sst = [es.enter_context(nc.psum_tensor(f"Sst{i}", [1, 256], F32)) for i in range(2)]
        Bb = [es.enter_context(nc.psum_tensor(f"Bb{i}", [128, 256], F32)) for i in range(2)]
        sch = Sch(nc, es)
        block = es.enter_context(nc.Block())

        sch.emit("vector", lambda h: h.memset(onescol[:, :], 1.0))
        t_ones = sch.emit("vector", lambda h: h.memset(onesrow[:, :], 1.0), inc="ones", fence=True)
        for a in range(32):
            sch.emit("gpsimd", lambda h, a=a: h.dma_start(
                out=w1sb[:, a * 5 * 128:(a + 1) * 5 * 128],
                in_=w1m[a].rearrange("j p c -> p j c")), inc="ldw", k=16)
        for a in range(8):
            sch.emit("gpsimd", lambda h, a=a: h.dma_start(
                out=w2sb[:, a * 20 * 128:(a + 1) * 20 * 128],
                in_=w2m[a].rearrange("q j p c -> p q j c")), inc="ldw", k=16)
        sch.emit("gpsimd", lambda h: h.dma_start(out=emks[:, :], in_=emk[:, :]), inc="ldw", k=16)
        for src, dst in ((b1c, b1s), (b2c, b2s), (g0c, g0s), (b0c, b0s)):
            sch.emit("sync", lambda h, src=src, dst=dst: h.dma_start(
                out=dst[:, :], in_=src[:, :]), inc="ldw2", k=16)
        t_w = ("ldw", sch.cnt["ldw"])
        t_w2 = ("ldw2", sch.cnt["ldw2"])

        def w1tile(a, j):
            return w1sb[:, (a * 5 + j) * 128:(a * 5 + j + 1) * 128]

        def w2tile(a, q, j):
            return w2sb[:, ((a * 4 + q) * 5 + j) * 128:((a * 4 + q) * 5 + j + 1) * 128]

        prev = {"c1": {}, "gel": {}, "evs": {}, "md": {}, "sto": {}, "xw": {}, "x1t": {}}
        last_bc_readers = [("ones", t_ones)]
        last_stat_users = []
        for stg in range(NST):
            i, st = divmod(stg, 2)
            xoff = i * 520 + st * 256
            toff = stg * 256
            tok_xw, tok_x1t = {}, {}
            for ic in range(NOC):
                w = [("c1", prev["c1"].get(31, 0)), ("gel", prev["gel"].get(31, 0))]
                tok_xw[ic] = sch.emit("gpsimd", lambda h, ic=ic, xoff=xoff: h.dma_start(
                    out=xw[ic][:, :], in_=x1h[ic * 128:(ic + 1) * 128, xoff:xoff + 264]),
                    waits=w, inc="lxw", k=16)
                w = [("evs", prev["evs"].get(ic, 0))]
                tok_x1t[ic] = sch.emit("sync", lambda h, ic=ic, toff=toff: h.dma_start(
                    out=x1t[ic][:, :], in_=x1[ic * 128:(ic + 1) * 128, toff:toff + 256]),
                    waits=w, inc="lx1", k=16)
            tok_c1, tok_gel = {}, {}
            for a in range(32):
                w = [t_w, ("lxw", tok_xw[7])]
                if a >= 2:
                    w.append(("gel", tok_gel[a - 2]))
                elif stg > 0:
                    w.append(("gel", prev["gel"][30 + a]))
                tok_c1[a] = sch.emit("tensor", lambda h, a=a: [
                    h.matmul(C[a % 2][:, :], w1tile(a, j), xw[a // 4][:, j:j + 260],
                             start=(j == 0), stop=(j == 4)) for j in range(5)][-1],
                    waits=w, inc="c1")
                w = [("c1", tok_c1[a]), t_w2]
                if stg > 0:
                    w.append(("md", prev["md"].get(7, 0)))
                tok_gel[a] = sch.emit("scalar", lambda h, a=a: h.activation(
                    h1s[a][:, :], C[a % 2][:, :], AF.Gelu, bias=b1s[:, a:a + 1]),
                    waits=w, inc="gel", fence=True)
            tok_em = {}
            for a in range(32):
                def mkem(h, a=a, stg=stg):
                    h.tensor_tensor(h1s[a][:, 0:2], h1s[a][:, 0:2],
                                    emks[:, stg * 4:stg * 4 + 2], ALU.mult)
                    return h.tensor_tensor(h1s[a][:, 258:260], h1s[a][:, 258:260],
                                           emks[:, stg * 4 + 2:stg * 4 + 4], ALU.mult)
                tok_em[a] = sch.emit("vector", mkem, waits=[("gel", tok_gel[a])],
                                     inc="emkd", fence=True)
            tok_md, tok_s, tok_q = {}, {}, {}
            for oc in range(NOC):
                w = [("gel", tok_gel[4 * oc + 3])] + [("emkd", tok_em[4 * oc + 3])]
                if oc >= 2:
                    w.append(("evs", tok_s[oc - 2]))
                elif stg > 0:
                    w.append(("evs", prev["evs"][oc + 6]))
                tok_md[oc] = sch.emit("tensor", lambda h, oc=oc: [
                    h.matmul(Dp[oc % 2][:, :], w2tile(oc, q, j),
                             h1s[4 * oc + q][:, j:j + 256],
                             start=(q == 0 and j == 0), stop=(q == 3 and j == 4))
                    for q in range(4) for j in range(5)][-1], waits=w, inc="md")

                def mks(h, oc=oc):
                    h.scalar_tensor_tensor(sfull[oc][:, :], Dp[oc % 2][:, :],
                                           b2s[:, oc:oc + 1], x1t[oc][:, :],
                                           ALU.add, ALU.add)
                    return h.tensor_copy(sbf[oc][:, :], sfull[oc][:, :])
                tok_s[oc] = sch.emit("vector", mks,
                                     waits=[("md", tok_md[oc]), ("lx1", tok_x1t[oc]),
                                            ("mstN", prev.get("mstN", 0))],
                                     inc="evs", fence=True)
                tok_q[oc] = sch.emit("scalar", lambda h, oc=oc: h.activation(
                    s2bf[oc][:, :], sfull[oc][:, :], AF.Square),
                    waits=[("evs", tok_s[oc])], inc="sq", fence=True)
            pre = [("evs", tok_s[7]), ("sq", tok_q[7]), ("ones", t_ones)] + last_stat_users
            gcol = [g0s[:, oc:oc + 1] for oc in range(NOC)]
            bcol = [b0s[:, oc:oc + 1] for oc in range(NOC)]
            ap_extra = [[("sto", prev["sto"].get(oc, 0))] for oc in range(NOC)]
            ap = _emit_ln(sch, sbf, s2bf, sfull, onescol, onesrow, Sst, Bb, scratch,
                          h0b[stg % 2], None, 1e-12, gcol=gcol, bcol=bcol,
                          pre_tokens=pre, bc_waits=last_bc_readers,
                          apply_waits=ap_extra, tag="N")
            last_bc_readers = [("apwN", ap[7])]
            last_stat_users = [("stfN", sch.cnt["stfN"])]
            tok_sto = {}
            for oc in range(NOC):
                tok_sto[oc] = sch.emit("sync", lambda h, oc=oc, toff=toff, stg=stg: h.dma_start(
                    out=h0[oc * 128:(oc + 1) * 128, toff:toff + 256],
                    in_=h0b[stg % 2][oc][:, :]),
                    waits=[("apwN", ap[oc])], inc="sto", k=16)
            prev = {"c1": tok_c1, "gel": tok_gel, "evs": tok_s, "md": tok_md,
                    "sto": tok_sto, "xw": tok_xw, "x1t": tok_x1t,
                    "mstN": sch.cnt["mstN"]}

        sch.emit("sync", None, waits=[("sto", sch.cnt["sto"])])
        sch.build(block)
    return nc


def _conv_mats(conv1_w, conv2_w):
    w1m = np.zeros((32, 5, 128, 128), np.float32)
    for a in range(32):
        ic = a // 4
        for j in range(5):
            for ol in range(128):
                o = a * 128 + ol
                cin_local = o // 4 - 128 * ic
                w1m[a, j, cin_local, ol] = conv1_w[o, 0, j]
    w2m = np.zeros((8, 4, 5, 128, 128), np.float32)
    for a in range(8):
        for q in range(4):
            hc = 4 * a + q
            for j in range(5):
                for p in range(128):
                    c_local = 32 * q + p // 4
                    m = p % 4
                    w2m[a, q, j, p, c_local] = conv2_w[a * 128 + c_local, m, j]
    return w1m, w2m


def _stage_halo(x1full):
    out = []
    xp = np.zeros((B, S + 8, D), np.float32)
    xp[:, 4:S + 4, :] = x1full
    for j in range(NCORES):
        cols = []
        for (b, c) in _chunks_of(j):
            cols.append(np.ascontiguousarray(xp[b, c * CH:c * CH + 520, :].T))
        out.append(np.ascontiguousarray(np.concatenate(cols, axis=1)))
    return out



def build_p7():
    nc = bass.Bass("TRN2", target_bir_lowering=False)
    hin = nc.dram_tensor("hin", [D, TOK], F32, kind="ExternalInput").ap()
    x1 = nc.dram_tensor("x1", [D, TOK], F32, kind="ExternalInput").ap()
    w1 = nc.dram_tensor("w1", [D, 4096], F32, kind="ExternalInput").ap()
    w2 = nc.dram_tensor("w2", [4096, D], F32, kind="ExternalInput").ap()
    b1r = nc.dram_tensor("b1r", [128, 32], F32, kind="ExternalInput").ap()
    b2r = nc.dram_tensor("b2r", [128, 8], F32, kind="ExternalInput").ap()
    g2r = nc.dram_tensor("g2r", [128, 8], F32, kind="ExternalInput").ap()
    bb2r = nc.dram_tensor("bb2r", [128, 8], F32, kind="ExternalInput").ap()
    x2 = nc.dram_tensor("x2", [D, TOK], F32, kind="ExternalOutput").ap()

    NH1 = 32
    es = ExitStack()
    with es:
        ht = [es.enter_context(nc.sbuf_tensor(f"ht{i}", [128, 512], F32)) for i in range(NOC)]
        hbf = [es.enter_context(nc.sbuf_tensor(f"hbf{i}", [128, 512], BF16)) for i in range(NOC)]
        x1t = [es.enter_context(nc.sbuf_tensor(f"x1t{i}", [128, 512], F32)) for i in range(NOC)]
        f1b = [es.enter_context(nc.sbuf_tensor(f"f1b{i}", [128, 512], BF16)) for i in range(NH1)]
        sfull = [es.enter_context(nc.sbuf_tensor(f"sf{i}", [128, 512], F32)) for i in range(NOC)]
        sbf = [es.enter_context(nc.sbuf_tensor(f"sb{i}", [128, 512], BF16)) for i in range(NOC)]
        s2bf = [es.enter_context(nc.sbuf_tensor(f"s2{i}", [128, 512], BF16)) for i in range(NOC)]
        ybuf = [es.enter_context(nc.sbuf_tensor(f"yb{i}", [128, 512], F32)) for i in range(NOC)]
        w1p = [es.enter_context(nc.sbuf_tensor(f"w1p{i}", [128, D], BF16)) for i in range(2)]
        w2p = [es.enter_context(nc.sbuf_tensor(f"w2p{i}", [128, 4096], BF16)) for i in range(2)]
        x2b = [[es.enter_context(nc.sbuf_tensor(f"x2b{z}_{i}", [128, 512], F32)) for i in range(NOC)]
               for z in range(2)]
        b1s = es.enter_context(nc.sbuf_tensor("b1s", [128, 32], F32))
        b2s = es.enter_context(nc.sbuf_tensor("b2s", [128, 8], F32))
        g2s = es.enter_context(nc.sbuf_tensor("g2s", [128, 8], F32))
        bb2s = es.enter_context(nc.sbuf_tensor("bb2s", [128, 8], F32))
        onescol = es.enter_context(nc.sbuf_tensor("onescol", [128, 1], BF16))
        onesrow = es.enter_context(nc.sbuf_tensor("onesrow", [1, 128], BF16))
        lns = [es.enter_context(nc.sbuf_tensor(f"lns{i}", [1, 512], F32)) for i in range(5)]
        lnb = [es.enter_context(nc.sbuf_tensor(f"lnb{i}", [1, 512], BF16)) for i in range(2)]
        scratch = (lns[0], lns[1], lns[2], lns[3], lns[4], lnb[0], lnb[1])
        P = [es.enter_context(nc.psum_tensor(f"P{i}", [128, 512], F32)) for i in range(4)]
        Sst = [es.enter_context(nc.psum_tensor(f"Sst{i}", [1, 512], F32)) for i in range(2)]
        Bb = [es.enter_context(nc.psum_tensor(f"Bb{i}", [128, 512], F32)) for i in range(2)]
        sch = Sch(nc, es)
        block = es.enter_context(nc.Block())

        sch.emit("vector", lambda h: h.memset(onescol[:, :], 1.0))
        t_ones = sch.emit("vector", lambda h: h.memset(onesrow[:, :], 1.0), inc="ones", fence=True)
        for src, dst in ((b1r, b1s), (b2r, b2s), (g2r, g2s), (bb2r, bb2s)):
            sch.emit("sync", lambda h, src=src, dst=dst: h.dma_start(
                out=dst[:, :], in_=src[:, :]), inc="ldc", k=16)
        t_c = ("ldc", sch.cnt["ldc"])

        prev = {"mf": {}, "gelf": {}, "evs": {}, "m2": {}, "sC": {}, "sD": {},
                "sto": {}, "lh": {}, "lx": {}}
        last_bc_readers = [("ones", t_ones)]
        last_stat_users = []
        for t in range(4):
            tok_lh, tok_lhb, tok_lx = {}, {}, {}
            for ic in range(NOC):
                w = [("sC", prev["sC"].get(ic, 0))]
                tok_lh[ic] = sch.emit("sync", lambda h, ic=ic, t=t: h.dma_start(
                    out=ht[ic][:, :], in_=hin[ic * 128:(ic + 1) * 128, t * 512:(t + 1) * 512]),
                    waits=w, inc="lh", k=16)
                w = [("mf", prev["mf"].get(31, 0)), ("m2", prev["m2"].get(7, 0)),
                     ("gelf", prev["gelf"].get(31, 0))]
                tok_lhb[ic] = sch.emit("gpsimd", lambda h, ic=ic, t=t: h.dma_start(
                    out=hbf[ic][:, :], in_=hin[ic * 128:(ic + 1) * 128, t * 512:(t + 1) * 512]),
                    waits=w, inc="lhb", k=16)
                w = [("sD", prev["sD"].get(ic, 0))]
                tok_lx[ic] = sch.emit("sync", lambda h, ic=ic, t=t: h.dma_start(
                    out=x1t[ic][:, :], in_=x1[ic * 128:(ic + 1) * 128, t * 512:(t + 1) * 512]),
                    waits=w, inc="lx", k=16)
            tok_mf, tok_gf = {}, {}
            for hc in range(NH1):
                w = []
                if hc >= 1:
                    w = [("mf", tok_mf[hc - 1])]
                tok_p = sch.emit("gpsimd", lambda h, hc=hc: h.dma_start(
                    out=w1p[hc % 2][:, :],
                    in_=w1[:, hc * 128:(hc + 1) * 128].rearrange("(ic p) c -> p ic c", p=128)),
                    waits=w, inc="lw1", k=16)
                wmm = [("lw1", tok_p), ("lhb", tok_lhb[7])]
                if hc >= 2:
                    wmm.append(("gelf", tok_gf[hc - 2]))
                tok_mf[hc] = sch.emit("tensor", lambda h, hc=hc: [
                    h.matmul(P[hc % 2][:, :], w1p[hc % 2][:, ic * 128:(ic + 1) * 128],
                             hbf[ic][:, :], start=(ic == 0), stop=(ic == NOC - 1))
                    for ic in range(NOC)][-1], waits=wmm, inc="mf")
                w = [("mf", tok_mf[hc]), t_c]
                if t > 0:
                    w.append(("m2", prev["m2"].get(7, 0)))
                tok_gf[hc] = sch.emit("scalar", lambda h, hc=hc: h.activation(
                    f1b[hc][:, :], P[hc % 2][:, :], AF.Gelu, bias=b1s[:, hc:hc + 1]),
                    waits=w, inc="gelf", fence=True)
            tok_m2, tok_sC, tok_qC = {}, {}, {}
            for oc in range(NOC):
                w = []
                if oc >= 1:
                    w = [("m2", tok_m2[oc - 1])]
                tok_p = sch.emit("gpsimd", lambda h, oc=oc: h.dma_start(
                    out=w2p[oc % 2][:, :],
                    in_=w2[:, oc * 128:(oc + 1) * 128].rearrange("(hc p) c -> p hc c", p=128)),
                    waits=w, inc="lw2", k=16)
                wmm = [("lw2", tok_p), ("gelf", tok_gf[NH1 - 1])]
                if oc >= 2:
                    wmm.append(("sC", tok_sC[oc - 2]))
                tok_m2[oc] = sch.emit("tensor", lambda h, oc=oc: [
                    h.matmul(P[2 + oc % 2][:, :], w2p[oc % 2][:, hc * 128:(hc + 1) * 128],
                             f1b[hc][:, :], start=(hc == 0), stop=(hc == NH1 - 1))
                    for hc in range(NH1)][-1], waits=wmm, inc="m2")

                def mksC(h, oc=oc):
                    h.scalar_tensor_tensor(sfull[oc][:, :], P[2 + oc % 2][:, :],
                                           b2s[:, oc:oc + 1], ht[oc][:, :],
                                           ALU.add, ALU.add)
                    return h.tensor_copy(sbf[oc][:, :], sfull[oc][:, :])
                tok_sC[oc] = sch.emit("vector", mksC,
                                      waits=[("m2", tok_m2[oc]), ("lh", tok_lh[oc]),
                                             ("mstDd", prev.get("mstDd", 0))],
                                      inc="sC", fence=True)
                tok_qC[oc] = sch.emit("scalar", lambda h, oc=oc: h.activation(
                    s2bf[oc][:, :], sfull[oc][:, :], AF.Square),
                    waits=[("sC", tok_sC[oc])], inc="qC", fence=True)
            pre = [("sC", tok_sC[7]), ("qC", tok_qC[7]), ("ones", t_ones)] + last_stat_users
            gcol = [g2s[:, oc:oc + 1] for oc in range(NOC)]
            bcol = [bb2s[:, oc:oc + 1] for oc in range(NOC)]
            apC = _emit_ln(sch, sbf, s2bf, sfull, onescol, onesrow, Sst, Bb, scratch,
                           ybuf, None, 1e-12, gcol=gcol, bcol=bcol, pre_tokens=pre,
                           bc_waits=last_bc_readers, tag="C")
            last_bc_readers = [("apwC", apC[7])]
            last_stat_users = [("stfC", sch.cnt["stfC"])]
            mstC_now = sch.cnt["mstC"]
            tok_sD, tok_qD = {}, {}
            for oc in range(NOC):
                def mksD(h, oc=oc):
                    h.tensor_add(sfull[oc][:, :], ybuf[oc][:, :], x1t[oc][:, :])
                    return h.tensor_copy(sbf[oc][:, :], sfull[oc][:, :])
                tok_sD[oc] = sch.emit("vector", mksD,
                                      waits=[("apwC", apC[oc]), ("lx", tok_lx[oc]),
                                             ("mstC", mstC_now)],
                                      inc="sD", fence=True)
                tok_qD[oc] = sch.emit("scalar", lambda h, oc=oc: h.activation(
                    s2bf[oc][:, :], sfull[oc][:, :], AF.Square),
                    waits=[("sD", tok_sD[oc])], inc="qD", fence=True)
            pre = [("sD", tok_sD[7]), ("qD", tok_qD[7])] + last_stat_users
            ap_extra = [[("sto", prev["sto"].get(oc, 0))] for oc in range(NOC)]
            apD = _emit_ln(sch, sbf, s2bf, sfull, onescol, onesrow, Sst, Bb, scratch,
                           x2b[t % 2], None, 1e-5, pre_tokens=pre,
                           bc_waits=last_bc_readers, apply_waits=ap_extra, tag="Dd")
            last_bc_readers = [("apwDd", apD[7])]
            last_stat_users = [("stfDd", sch.cnt["stfDd"])]
            tok_sto = {}
            for oc in range(NOC):
                tok_sto[oc] = sch.emit("sync", lambda h, oc=oc, t=t: h.dma_start(
                    out=x2[oc * 128:(oc + 1) * 128, t * 512:(t + 1) * 512],
                    in_=x2b[t % 2][oc][:, :]),
                    waits=[("apwDd", apD[oc])], inc="sto", k=16)
            prev = {"mf": tok_mf, "gelf": tok_gf, "m2": tok_m2, "sC": tok_sC,
                    "sD": tok_sD, "sto": tok_sto, "lh": tok_lh, "lx": tok_lx,
                    "evs": {}, "mstDd": sch.cnt["mstDd"]}

        sch.emit("sync", None, waits=[("sto", sch.cnt["sto"])])
        sch.build(block)
    return nc



def build_p8():
    nc = bass.Bass("TRN2", target_bir_lowering=False)
    x2 = nc.dram_tensor("x2", [D, TOK], F32, kind="ExternalInput").ap()
    wst = nc.dram_tensor("wst", [D, D], F32, kind="ExternalInput").ap()
    out = nc.dram_tensor("out", [D, TOK], F32, kind="ExternalOutput").ap()

    es = ExitStack()
    with es:
        wraw = [es.enter_context(nc.sbuf_tensor(f"wr{i}", [128, D], BF16)) for i in range(NOC)]
        wsg = [es.enter_context(nc.sbuf_tensor(f"ws{i}", [128, D], BF16)) for i in range(NOC)]
        x2t = [es.enter_context(nc.sbuf_tensor(f"x2t{i}", [128, 512], F32)) for i in range(NOC)]
        x2bf = [es.enter_context(nc.sbuf_tensor(f"x2f{i}", [128, 512], BF16)) for i in range(NOC)]
        sfull = [es.enter_context(nc.sbuf_tensor(f"sf{i}", [128, 512], F32)) for i in range(NOC)]
        sbf = [es.enter_context(nc.sbuf_tensor(f"sb{i}", [128, 512], BF16)) for i in range(NOC)]
        s2bf = [es.enter_context(nc.sbuf_tensor(f"s2{i}", [128, 512], BF16)) for i in range(NOC)]
        ob = [[es.enter_context(nc.sbuf_tensor(f"ob{z}_{i}", [128, 512], F32)) for i in range(NOC)]
              for z in range(2)]
        onescol = es.enter_context(nc.sbuf_tensor("onescol", [128, 1], BF16))
        onesrow = es.enter_context(nc.sbuf_tensor("onesrow", [1, 128], BF16))
        lns = [es.enter_context(nc.sbuf_tensor(f"lns{i}", [1, 512], F32)) for i in range(5)]
        lnb = [es.enter_context(nc.sbuf_tensor(f"lnb{i}", [1, 512], BF16)) for i in range(2)]
        scratch = (lns[0], lns[1], lns[2], lns[3], lns[4], lnb[0], lnb[1])
        P = [es.enter_context(nc.psum_tensor(f"P{i}", [128, 512], F32)) for i in range(2)]
        Sst = [es.enter_context(nc.psum_tensor(f"Sst{i}", [1, 512], F32)) for i in range(2)]
        Bb = [es.enter_context(nc.psum_tensor(f"Bb{i}", [128, 512], F32)) for i in range(2)]
        sch = Sch(nc, es)
        block = es.enter_context(nc.Block())

        sch.emit("vector", lambda h: h.memset(onescol[:, :], 1.0))
        t_ones = sch.emit("vector", lambda h: h.memset(onesrow[:, :], 1.0), inc="ones", fence=True)
        tok_lw = {}
        for i in range(NOC):
            tok_lw[i] = sch.emit("gpsimd", lambda h, i=i: h.dma_start(
                out=wraw[i][:, :], in_=wst[i * 128:(i + 1) * 128, :]), inc="lw", k=16)
        tok_sg = {}
        for i in range(NOC):
            tok_sg[i] = sch.emit("scalar", lambda h, i=i: h.activation(
                wsg[i][:, :], wraw[i][:, :], AF.Sign),
                waits=[("lw", tok_lw[i])], inc="sg", fence=True)

        prev = {"evs": {}, "mm": {}, "sto": {}, "lx": {}}
        last_bc_readers = [("ones", t_ones)]
        last_stat_users = []
        for t in range(4):
            tok_lx, tok_lxb = {}, {}
            for ic in range(NOC):
                w = [("evs", prev["evs"].get(ic, 0))]
                tok_lx[ic] = sch.emit("sync", lambda h, ic=ic, t=t: h.dma_start(
                    out=x2t[ic][:, :], in_=x2[ic * 128:(ic + 1) * 128, t * 512:(t + 1) * 512]),
                    waits=w, inc="lx", k=16)
                w = [("mm", prev["mm"].get(7, 0)), ("evs", prev["evs"].get(7, 0))]
                tok_lxb[ic] = sch.emit("gpsimd", lambda h, ic=ic, t=t: h.dma_start(
                    out=x2bf[ic][:, :], in_=x2[ic * 128:(ic + 1) * 128, t * 512:(t + 1) * 512]),
                    waits=w, inc="lxb", k=16)
            tok_mm, tok_s, tok_q = {}, {}, {}
            for oc in range(NOC):
                w = [("sg", tok_sg[NOC - 1]), ("lxb", tok_lxb[7])]
                if oc >= 2:
                    w.append(("evs", tok_s[oc - 2]))
                elif t > 0:
                    w.append(("evs", prev["evs"][oc + 6]))
                tok_mm[oc] = sch.emit("tensor", lambda h, oc=oc: [
                    h.matmul(P[oc % 2][:, :], wsg[ic][:, oc * 128:(oc + 1) * 128],
                             x2bf[ic][:, :], start=(ic == 0), stop=(ic == NOC - 1))
                    for ic in range(NOC)][-1], waits=w, inc="mm")

                def mks(h, oc=oc):
                    h.tensor_add(sfull[oc][:, :], x2t[oc][:, :], P[oc % 2][:, :])
                    return h.tensor_copy(sbf[oc][:, :], sfull[oc][:, :])
                tok_s[oc] = sch.emit("vector", mks,
                                     waits=[("mm", tok_mm[oc]), ("lx", tok_lx[oc]),
                                            ("mstF", prev.get("mstF", 0))],
                                     inc="evs", fence=True)
                tok_q[oc] = sch.emit("scalar", lambda h, oc=oc: h.activation(
                    s2bf[oc][:, :], sfull[oc][:, :], AF.Square),
                    waits=[("evs", tok_s[oc])], inc="sq", fence=True)
            pre = [("evs", tok_s[7]), ("sq", tok_q[7]), ("ones", t_ones)] + last_stat_users
            ap_extra = [[("sto", prev["sto"].get(oc, 0))] for oc in range(NOC)]
            ap = _emit_ln(sch, sbf, s2bf, sfull, onescol, onesrow, Sst, Bb, scratch,
                          ob[t % 2], None, 1e-5, pre_tokens=pre,
                          bc_waits=last_bc_readers, apply_waits=ap_extra, tag="F")
            last_bc_readers = [("apwF", ap[7])]
            last_stat_users = [("stfF", sch.cnt["stfF"])]
            tok_sto = {}
            for oc in range(NOC):
                tok_sto[oc] = sch.emit("sync", lambda h, oc=oc, t=t: h.dma_start(
                    out=out[oc * 128:(oc + 1) * 128, t * 512:(t + 1) * 512],
                    in_=ob[t % 2][oc][:, :]),
                    waits=[("apwF", ap[oc])], inc="sto", k=16)
            prev = {"evs": tok_s, "mm": tok_mm, "sto": tok_sto, "lx": tok_lx,
                    "mstF": sch.cnt["mstF"]}

        sch.emit("sync", None, waits=[("sto", sch.cnt["sto"])])
        sch.build(block)
    return nc


def build_p6(mode):
    first, mid, last = mode == "first", mode == "mid", mode == "last"
    do_fin = mid or last
    do_proj = first or mid
    nc = bass.Bass("TRN2", target_bir_lowering=False)
    hin = nc.dram_tensor("hin", [D, TOK], F32, kind="ExternalInput").ap()
    if do_fin:
        parts = nc.dram_tensor("parts", [8, 128, 1040], F32, kind="ExternalInput").ap()
        qp_i = nc.dram_tensor("qp_i", [D, TOK], BF16, kind="ExternalInput").ap()
        qr_i = nc.dram_tensor("qr_i", [D, TOK], BF16, kind="ExternalInput").ap()
        lnag = nc.dram_tensor("lnag", [128, 8], F32, kind="ExternalInput").ap()
        lnab = nc.dram_tensor("lnab", [128, 8], F32, kind="ExternalInput").ap()
        e2 = nc.dram_tensor("e2", [2, 128], F32, kind="ExternalInput").ap()
        hn = nc.dram_tensor("hn", [D, TOK], F32, kind="ExternalOutput").ap()
    if do_proj:
        wts = {w: nc.dram_tensor("w" + w, [D, D], F32, kind="ExternalInput").ap()
               for w in "qkv"}
        bcols = {w: nc.dram_tensor("b" + w, [128, 8], F32, kind="ExternalInput").ap()
                 for w in "qkv"}
        cosb = nc.dram_tensor("cosb", [D, TOK], BF16, kind="ExternalInput").ap()
        sinb = nc.dram_tensor("sinb", [D, TOK], BF16, kind="ExternalInput").ap()
        rmat = nc.dram_tensor("rmat", [128, 128], F32, kind="ExternalInput").ap()
        ident = nc.dram_tensor("ident", [128, 128], F32, kind="ExternalInput").ap()
        qp_o = nc.dram_tensor("qp_o", [D, TOK], BF16, kind="ExternalOutput").ap()
        qr_o = nc.dram_tensor("qr_o", [D, TOK], BF16, kind="ExternalOutput").ap()
        part_o = nc.dram_tensor("part_o", [128, 1040], F32, kind="ExternalOutput").ap()

    es = ExitStack()
    with es:
        ht = [es.enter_context(nc.sbuf_tensor(f"ht{i}", [128, 512], F32)) for i in range(NOC)]
        if do_fin:
            psb = [es.enter_context(nc.sbuf_tensor("psb0", [128, 1040], F32))]
            ptot = es.enter_context(nc.sbuf_tensor("ptot", [128, 1040], F32))
            kvbf = es.enter_context(nc.sbuf_tensor("kvbf", [128, 1024], BF16))
            kmbf = es.enter_context(nc.sbuf_tensor("kmbf", [128, 16], BF16))
            km2 = es.enter_context(nc.sbuf_tensor("km2", [128, 32], BF16))
            qpt = [es.enter_context(nc.sbuf_tensor(f"qpt{i}", [128, 512], BF16)) for i in range(2)]
            zden = es.enter_context(nc.sbuf_tensor("zden", [2, 4096], F32))
            zfb = es.enter_context(nc.sbuf_tensor("zfb", [2, 4096], BF16))
            e2s = es.enter_context(nc.sbuf_tensor("e2s", [2, 128], BF16))
            ln8c = es.enter_context(nc.sbuf_tensor("ln8c", [2, 1], F32))
            qrt = [es.enter_context(nc.sbuf_tensor(f"qrt{i}", [128, 512], BF16)) for i in range(2)]
            zbsb = es.enter_context(nc.sbuf_tensor("zbsb", [128, 512], F32))
            sfull = [es.enter_context(nc.sbuf_tensor(f"sf{i}", [128, 512], F32)) for i in range(NOC)]
            sbf = [es.enter_context(nc.sbuf_tensor(f"sb{i}", [128, 512], BF16)) for i in range(NOC)]
            s2bf = [es.enter_context(nc.sbuf_tensor(f"s2{i}", [128, 512], BF16)) for i in range(NOC)]
            hnew = [es.enter_context(nc.sbuf_tensor(f"hw{i}", [128, 512], F32)) for i in range(NOC)]
            gs = es.enter_context(nc.sbuf_tensor("gs", [128, 8], F32))
            bs = es.enter_context(nc.sbuf_tensor("bs", [128, 8], F32))
            onescol = es.enter_context(nc.sbuf_tensor("onescol", [128, 1], BF16))
            onesrow = es.enter_context(nc.sbuf_tensor("onesrow", [1, 128], BF16))
            lns = [es.enter_context(nc.sbuf_tensor(f"lns{i}", [1, 512], F32)) for i in range(5)]
            lnb = [es.enter_context(nc.sbuf_tensor(f"lnb{i}", [1, 512], BF16)) for i in range(2)]
            scratch = (lns[0], lns[1], lns[2], lns[3], lns[4], lnb[0], lnb[1])
        if do_proj:
            hbnew = [es.enter_context(nc.sbuf_tensor(f"hf{i}", [128, 512], BF16)) for i in range(NOC)]
            wsb = {w: [es.enter_context(nc.sbuf_tensor(f"w{w}{i}", [128, D], BF16)) for i in range(NOC)]
                   for w in "qkv"}
            bsb = {w: es.enter_context(nc.sbuf_tensor(f"bc{w}", [128, 8], F32)) for w in "qkv"}
            coss = [es.enter_context(nc.sbuf_tensor(f"cs{i}", [128, 512], BF16)) for i in range(2)]
            sins = [es.enter_context(nc.sbuf_tensor(f"sn{i}", [128, 512], BF16)) for i in range(2)]
            rms = es.enter_context(nc.sbuf_tensor("rms", [128, 128], BF16))
            idn = es.enter_context(nc.sbuf_tensor("idn", [128, 128], BF16))
            qpf = [es.enter_context(nc.sbuf_tensor(f"qpf{i}", [128, 512], F32)) for i in range(2)]
            kpf = [es.enter_context(nc.sbuf_tensor(f"kpf{i}", [128, 512], F32)) for i in range(2)]
            qpb = [es.enter_context(nc.sbuf_tensor(f"qpb{i}", [128, 512], BF16)) for i in range(2)]
            kpb = [es.enter_context(nc.sbuf_tensor(f"kpb{i}", [128, 512], BF16)) for i in range(2)]
            qcosf = es.enter_context(nc.sbuf_tensor("qcosf", [128, 512], F32))
            rsinf = es.enter_context(nc.sbuf_tensor("rsinf", [128, 512], F32))
            qrb = [es.enter_context(nc.sbuf_tensor(f"qrb{i}", [128, 512], BF16)) for i in range(2)]
            krb = [es.enter_context(nc.sbuf_tensor(f"krb{i}", [128, 512], BF16)) for i in range(2)]
            vbt = [es.enter_context(nc.sbuf_tensor(f"vbt{i}", [128, 512], BF16)) for i in range(2)]
            krtok = [es.enter_context(nc.sbuf_tensor(f"krt{i}", [128, 128], BF16)) for i in range(4)]
            vtok = [es.enter_context(nc.sbuf_tensor(f"vt{i}", [128, 128], BF16)) for i in range(4)]
            kvsb = es.enter_context(nc.sbuf_tensor("kvsb", [128, 1024], F32))
            kmtmp = es.enter_context(nc.sbuf_tensor("kmtmp", [128, 32], F32))
            km16 = es.enter_context(nc.sbuf_tensor("km16", [128, 16], F32))
        PP = [es.enter_context(nc.psum_tensor(f"PP{i}", [128, 512], F32)) for i in range(2)]
        OZ = [es.enter_context(nc.psum_tensor(f"OZ{i}", [128, 512], F32)) for i in range(2)]
        if do_proj:
            KV = [es.enter_context(nc.psum_tensor(f"KV{i}", [128, 512], F32)) for i in range(2)]
            RQ = es.enter_context(nc.psum_tensor("RQ", [128, 512], F32))
            RK = es.enter_context(nc.psum_tensor("RK", [128, 512], F32))
        sch = Sch(nc, es)
        C = sch.cnt
        block = es.enter_context(nc.Block())

        t_ones = 0
        if do_fin:
            sch.emit("vector", lambda h: h.memset(onescol[:, :], 1.0))
            sch.emit("vector", lambda h: h.memset(ln8c[:, :], -float(np.log(8192.0))))
            t_ones = sch.emit("vector", lambda h: h.memset(onesrow[:, :], 1.0), inc="ones", fence=True)
            for s_, d_ in ((lnag, gs), (lnab, bs)):
                sch.emit("sync", lambda h, s_=s_, d_=d_: h.dma_start(out=d_[:, :], in_=s_[:, :]),
                         inc="ldc", k=16)
            sch.emit("gpsimd", lambda h: h.dma_start(out=e2s[:, :], in_=e2[:, :]), inc="ldc2", k=16)
        if do_proj:
            for w in "qkv":
                for i in range(NOC):
                    sch.emit("gpsimd", lambda h, w=w, i=i: h.dma_start(
                        out=wsb[w][i][:, :], in_=wts[w][i * 128:(i + 1) * 128, :]),
                        inc="ldw", k=16)
                sch.emit("sync", lambda h, w=w: h.dma_start(
                    out=bsb[w][:, :], in_=bcols[w][:, :]), inc="ldc3", k=16)
            sch.emit("gpsimd", lambda h: h.dma_start(out=rms[:, :], in_=rmat[:, :]), inc="ldw", k=16)
            sch.emit("gpsimd", lambda h: h.dma_start(out=idn[:, :], in_=ident[:, :]), inc="ldw", k=16)
            sch.emit("vector", lambda h: h.memset(kvsb[:, :], 0.0))
            sch.emit("vector", lambda h: h.memset(km16[:, :], 0.0), inc="z0", fence=True)

        t_zf = 0
        t_kv = 0
        if do_fin:
            for r in range(8):
                tpl = sch.emit("sync", lambda h, r=r: h.dma_start(
                    out=psb[0][:, :], in_=parts[r]),
                    waits=[("psum8", r)], inc="ldp", k=16)
                if r == 0:
                    sch.emit("vector", lambda h: h.tensor_copy(ptot[:, :], psb[0][:, :]),
                             waits=[("ldp", tpl)], inc="psum8", fence=True)
                else:
                    sch.emit("vector", lambda h, r=r: h.tensor_add(
                        ptot[:, :], ptot[:, :], psb[0][:, :]),
                        waits=[("ldp", tpl)], inc="psum8", fence=True)

            def mkkv(h):
                h.tensor_copy(kvbf[:, :], ptot[:, 0:1024])
                h.tensor_copy(kmbf[:, :], ptot[:, 1024:1040])
                return h.memset(km2[:, :], 0.0)
            t_kv = sch.emit("vector", mkkv, waits=[("psum8", C["psum8"])], inc="kvr", fence=True)

            def mkkm2(h):
                for b_ in range(2):
                    for oc in range(NOC):
                        c = b_ * 8 + oc
                        h.tensor_copy(km2[0:64, 2 * c:2 * c + 1], kmbf[0:64, c:c + 1])
                        inst = h.tensor_copy(km2[64:128, 2 * c + 1:2 * c + 2], kmbf[64:128, c:c + 1])
                return inst
            t_km2 = sch.emit("vector", mkkm2, waits=[("kvr", t_kv)], inc="km2d", fence=True)


        prev = {}
        pst = {"m": 0, "gel": {}}
        last_bc_readers = [("ones", t_ones)]
        last_stat_users = []
        for tt in range(4):
            b = tt // 2
            tok_lh = {}
            for ic in range(NOC):
                if do_fin:
                    w = [("sA", prev.get("sA", {}).get(ic, 0))]
                else:
                    w = [("hbc", prev.get("hbc", {}).get(ic, 0))]
                tok_lh[ic] = sch.emit("sync", lambda h, ic=ic, tt=tt: h.dma_start(
                    out=ht[ic][:, :], in_=hin[ic * 128:(ic + 1) * 128, tt * 512:(tt + 1) * 512]),
                    waits=w, inc="lh", k=16)
            if do_fin:
                tok_zmm = {}
                for oc in range(NOC):
                    tqp = sch.emit("sync", lambda h, oc=oc, tt=tt: h.dma_start(
                        out=qpt[oc % 2][:, :],
                        in_=qp_i[oc * 128:(oc + 1) * 128, tt * 512:(tt + 1) * 512]),
                        waits=[("zmm", max(0, C.get("zmm", 0) - 1))], inc="ldq", k=16)
                    c = b * 8 + oc
                    wz = [("km2d", t_km2), ("ldq", tqp),
                          ("zev", max(0, C.get("zev", 0) - 1)),
                          ("gvd", prev.get("gvd", 0)), ("ropk", prev.get("ropk", 0)),
                          ("apwA", prev.get("apwA", 0))]
                    tok_zmm[oc] = sch.emit("tensor", lambda h, c=c, oc=oc: h.matmul(
                        PP[oc % 2][0:2, :], km2[:, 2 * c:2 * c + 2],
                        qpt[oc % 2][:, :], start=True, stop=True),
                        waits=wz, inc="zmm")
                    sch.emit("vector", lambda h, oc=oc: h.tensor_copy(
                        zden[0:2, oc * 512:(oc + 1) * 512], PP[oc % 2][0:2, :]),
                        waits=[("zmm", tok_zmm[oc]), ("zbm", prev.get("zbm", 0))],
                        inc="zev", fence=True)

                t_zs = sch.emit("vector", lambda h: h.tensor_scalar(
                    zden[:, :], zden[:, :], 1.0 / 8192.0, 1e-6, ALU.mult, ALU.add),
                    waits=[("zev", C["zev"])], inc="zsc", fence=True)

                def mkzf(h):
                    h.activation(zden[:, :], zden[:, :], AF.Ln)
                    return h.activation(zfb[:, :], zden[:, :], AF.Exp,
                                        scale=-1.0, bias=ln8c[0:2, 0:1])
                t_zf = sch.emit("scalar", mkzf,
                                waits=[("zsc", t_zs), ("ones", t_ones)],
                                inc="zfd", fence=True)

                tok_ao = {}
                for oc in range(NOC):
                    tq = sch.emit("sync", lambda h, oc=oc, tt=tt: h.dma_start(
                        out=qrt[oc % 2][:, :],
                        in_=qr_i[oc * 128:(oc + 1) * 128, tt * 512:(tt + 1) * 512]),
                        waits=[("omm", tok_ao.get(oc - 2, 0) and 0) if False else
                               ("omm", max(0, C.get("omm", 0) - 1))], inc="lqr", k=16)
                    woz = [("stfA", prev.get("stfA", 0)),
                           ("trev", prev.get("trev", 0))]
                    if oc >= 2:
                        woz.append(("aoe", tok_ao[oc - 2]))
                    else:
                        woz.append(("aoe", prev.get("aoe", {}).get(oc + 6, 0)))

                    def mko(h, oc=oc, b=b):
                        h.matmul(OZ[oc % 2][0:64, :],
                                 kvbf[0:64, b * 512 + 64 * oc:b * 512 + 64 * oc + 64],
                                 qrt[oc % 2][0:64, :], start=True, stop=True)
                        return h.matmul(OZ[oc % 2][64:128, :],
                                        kvbf[64:128, b * 512 + 64 * oc:b * 512 + 64 * oc + 64],
                                        qrt[oc % 2][64:128, :], start=True, stop=True)
                    t_omm = sch.emit("tensor", mko,
                                     waits=[("kvr", t_kv), ("lqr", tq), ("zfd", t_zf)] + woz,
                                     inc="omm")
                    wzb = [("ldc2", 16), ("zfd", t_zf),
                           ("gvd", prev.get("gvd", 0)), ("ropk", prev.get("ropk", 0)),
                           ("apwA", prev.get("apwA", 0))]
                    if oc >= 2:
                        wzb.append(("aoe", tok_ao[oc - 2]))
                    t_zb = sch.emit("tensor", lambda h, oc=oc: h.matmul(
                        PP[oc % 2][:, :], e2s[:, :], zfb[0:2, oc * 512:(oc + 1) * 512],
                        start=True, stop=True), waits=wzb, inc="zbm")

                    def mkao(h, oc=oc):
                        h.tensor_copy(zbsb[:, :], PP[oc % 2][:, :])
                        return h.tensor_mul(sfull[oc][:, :], zbsb[:, :], OZ[oc % 2][:, :])
                    tok_ao[oc] = sch.emit("vector", mkao,
                                          waits=[("omm", t_omm), ("zbm", t_zb),
                                                 ("mstA", prev.get("mstA", 0))],
                                          inc="aoe", fence=True)
                prev["zbm"] = C["zbm"]
                tok_s, tok_q = {}, {}
                for oc in range(NOC):
                    def mks(h, oc=oc):
                        h.tensor_add(sfull[oc][:, :], sfull[oc][:, :], ht[oc][:, :])
                        return h.tensor_copy(sbf[oc][:, :], sfull[oc][:, :])
                    tok_s[oc] = sch.emit("vector", mks,
                                         waits=[("aoe", tok_ao[oc]), ("lh", tok_lh[oc])],
                                         inc="sA", fence=True)
                    tok_q[oc] = sch.emit("scalar", lambda h, oc=oc: h.activation(
                        s2bf[oc][:, :], sfull[oc][:, :], AF.Square),
                        waits=[("sA", tok_s[oc])], inc="qA", fence=True)
                pre = [("sA", tok_s[7]), ("qA", tok_q[7]), ("ones", t_ones),
                       ("aoe", C["aoe"]), ("ldc", 32)] + last_stat_users
                gcol = [gs[:, oc:oc + 1] for oc in range(NOC)]
                bcol = [bs[:, oc:oc + 1] for oc in range(NOC)]
                bcw = list(last_bc_readers) + [("aoe", C["aoe"]),
                                               ("gvd", prev.get("gvd", 0)),
                                               ("ropk", prev.get("ropk", 0))]
                ap_extra = [[("sthn", prev.get("sthn", {}).get(oc, 0)),
                             ("hbc", prev.get("hbc", {}).get(oc, 0))] for oc in range(NOC)]
                ps_stat = [OZ[0][0:1, :], OZ[1][0:1, :]]
                ps_b = [PP[0][:, :], PP[1][:, :]]
                apA = _emit_ln(sch, sbf, s2bf, sfull, onescol, onesrow, ps_stat, ps_b,
                               scratch, hnew, None, 1e-12, gcol=gcol, bcol=bcol,
                               pre_tokens=pre, bc_waits=bcw,
                               apply_waits=ap_extra, tag="A")
                last_bc_readers = [("apwA", apA[7])]
                last_stat_users = [("stfA", C["stfA"])]
                prev["mstA"] = C["mstA"]
                prev["stfA"] = C["stfA"]
                prev["apwA"] = C["apwA"]
                tok_hn = {}
                for oc in range(NOC):
                    tok_hn[oc] = sch.emit("sync", lambda h, oc=oc, tt=tt: h.dma_start(
                        out=hn[oc * 128:(oc + 1) * 128, tt * 512:(tt + 1) * 512],
                        in_=hnew[oc][:, :]),
                        waits=[("apwA", apA[oc])], inc="sthn", k=16)
                prev["sthn"] = tok_hn
                prev["aoe"] = tok_ao
                prev["sA"] = tok_s
                srcf = hnew
                src_gate = [("apwA", apA[oc]) for oc in range(NOC)]
            else:
                srcf = ht
                src_gate = [("lh", tok_lh[oc]) for oc in range(NOC)]

            if do_proj:
                tok_hb = {}
                for ic in range(NOC):
                    w = [src_gate[ic], ("pmv", prev.get("pmv", 0))]
                    tok_hb[ic] = sch.emit("vector", lambda h, ic=ic, srcf=srcf: h.tensor_copy(
                        hbnew[ic][:, :], srcf[ic][:, :]), waits=w, inc="hbc", fence=True)
                prev["hbc"] = tok_hb
                t_qpd, t_kpd, t_rok, t_gv2 = {}, {}, {}, {}
                for it in range(NOC + 2):
                    oc2 = it - 2
                    if oc2 >= 0:
                        for tb in range(4):
                            t_t1 = sch.emit("tensor", lambda h, tb=tb, oc2=oc2: h.transpose(
                                OZ[tb % 2].bitcast(BF16)[:, 0:128],
                                krb[oc2 % 2][:, tb * 128:(tb + 1) * 128], idn[:, :]),
                                waits=[("ropk", t_rok[oc2]), ("trev", max(0, C.get("trev", 0) - 1)),
                                       ("ldw", C["ldw"]), ("stfA", C.get("stfA", 0))],
                                inc="trm")
                            sch.emit("vector", lambda h, tb=tb: h.tensor_copy(
                                krtok[tb][:, :], OZ[tb % 2].bitcast(BF16)[:, 0:128]),
                                waits=[("trm", t_t1), ("kvm", C.get("kvm", 0))],
                                inc="trev", fence=True)
                        for tb in range(4):
                            t_t2 = sch.emit("tensor", lambda h, tb=tb, oc2=oc2: h.transpose(
                                OZ[tb % 2].bitcast(BF16)[:, 0:128],
                                vbt[oc2 % 2][:, tb * 128:(tb + 1) * 128], idn[:, :]),
                                waits=[("gvd", t_gv2[oc2]), ("trev", max(0, C.get("trev", 0) - 1))],
                                inc="trm")
                            sch.emit("vector", lambda h, tb=tb: h.tensor_copy(
                                vtok[tb][:, :], OZ[tb % 2].bitcast(BF16)[:, 0:128]),
                                waits=[("trm", t_t2), ("kvm", C.get("kvm", 0))],
                                inc="trev", fence=True)

                        def mkkvmm(h, oc=oc2, b=b):
                            for hh in range(2):
                                for tb in range(4):
                                    inst = h.matmul(KV[b][64 * hh:64 * hh + 64,
                                                          64 * oc:64 * oc + 64],
                                                    krtok[tb][:, 64 * hh:64 * hh + 64],
                                                    vtok[tb][:, 64 * hh:64 * hh + 64],
                                                    start=(tb == 0), stop=(tb == 3))
                            return inst
                        t_kvm = sch.emit("tensor", mkkvmm,
                                         waits=[("trev", C["trev"]), ("kve", C.get("kve", 0))],
                                         inc="kvm")

                        def mkkve(h, oc=oc2, b=b, tt=tt):
                            reg = kvsb[:, b * 512 + 64 * oc: b * 512 + 64 * oc + 64]
                            src = KV[b][:, 64 * oc:64 * oc + 64]
                            if tt % 2 == 0:
                                return h.tensor_copy(reg, src)
                            return h.tensor_add(reg, reg, src)
                        sch.emit("vector", mkkve, waits=[("kvm", t_kvm), ("z0", 1)],
                                 inc="kve", fence=True)
                    oc1 = it - 1
                    if 0 <= oc1 < NOC:
                        t_rq = sch.emit("tensor", lambda h, oc=oc1: h.matmul(
                            RQ[:, :], rms[:, :], qpb[oc % 2][:, :], start=True, stop=True),
                            waits=[("qpd", t_qpd[oc1]), ("ropq", C.get("ropq", 0))], inc="rq")

                        def mkrope_q(h, oc=oc1):
                            h.tensor_mul(qcosf[:, :], qpf[oc % 2][:, :], coss[oc % 2][:, :])
                            h.tensor_mul(rsinf[:, :], RQ[:, :], sins[oc % 2][:, :])
                            return h.tensor_add(qrb[oc % 2][:, :], qcosf[:, :], rsinf[:, :])
                        t_ro = sch.emit("vector", mkrope_q,
                                        waits=[("rq", t_rq), ("lcs", C["lcs"]),
                                               ("stqr", C.get("stqr", 0))],
                                        inc="ropq", fence=True)
                        sch.emit("sync", lambda h, oc=oc1, tt=tt: h.dma_start(
                            out=qr_o[oc * 128:(oc + 1) * 128, tt * 512:(tt + 1) * 512],
                            in_=qrb[oc % 2][:, :]), waits=[("ropq", t_ro)], inc="stqr", k=16)
                        sch.emit("sync", lambda h, oc=oc1, tt=tt: h.dma_start(
                            out=qp_o[oc * 128:(oc + 1) * 128, tt * 512:(tt + 1) * 512],
                            in_=qpb[oc % 2][:, :]), waits=[("qpd", t_qpd[oc1])], inc="stqp", k=16)
                        t_rk = sch.emit("tensor", lambda h, oc=oc1: h.matmul(
                            RK[:, :], rms[:, :], kpb[oc % 2][:, :], start=True, stop=True),
                            waits=[("kpd", t_kpd[oc1]), ("ropk", C.get("ropk", 0))], inc="rk")

                        def mkrope_k(h, oc=oc1):
                            h.tensor_mul(qcosf[:, :], kpf[oc % 2][:, :], coss[oc % 2][:, :])
                            h.tensor_mul(rsinf[:, :], RK[:, :], sins[oc % 2][:, :])
                            return h.tensor_add(krb[oc % 2][:, :], qcosf[:, :], rsinf[:, :])
                        t_rok[oc1] = sch.emit("vector", mkrope_k,
                                              waits=[("rk", t_rk), ("lcs", C["lcs"]),
                                                     ("trm", C.get("trm", 0))],
                                              inc="ropk", fence=True)
                    oc = it
                    if oc < NOC:
                        sch.emit("sync", lambda h, oc=oc, tt=tt: h.dma_start(
                            out=coss[oc % 2][:, :],
                            in_=cosb[oc * 128:(oc + 1) * 128, tt * 512:(tt + 1) * 512]),
                            waits=[("ropk", max(0, C.get("ropk", 0) - 1))], inc="lcs", k=16)
                        sch.emit("sync", lambda h, oc=oc, tt=tt: h.dma_start(
                            out=sins[oc % 2][:, :],
                            in_=sinb[oc * 128:(oc + 1) * 128, tt * 512:(tt + 1) * 512]),
                            inc="lcs", k=16)
                        m0 = pst["m"]

                        def mkproj(w_, m):
                            def mm(h, w_=w_, oc=oc, m=m):
                                for ic in range(NOC):
                                    inst = h.matmul(PP[m % 2][:, :],
                                                    wsb[w_][ic][:, oc * 128:(oc + 1) * 128],
                                                    hbnew[ic][:, :],
                                                    start=(ic == 0), stop=(ic == NOC - 1))
                                return inst
                            wts_ = [("ldw", C["ldw"]), ("hbc", tok_hb[7])]
                            if m - 2 in pst["gel"]:
                                wts_.append(pst["gel"][m - 2])
                            return sch.emit("tensor", mm, waits=wts_, inc="pm" + w_)
                        t_q = mkproj("q", m0)
                        t_gq = sch.emit("scalar", lambda h, oc=oc, m=m0: h.activation(
                            qpf[oc % 2][:, :], PP[m % 2][:, :], AF.Gelu,
                            bias=bsb["q"][:, oc:oc + 1]),
                            waits=[("pmq", t_q), ("ldc3", 48), ("qpd", C.get("qpd", 0)),
                                   ("ropq", C.get("ropq", 0))],
                            inc="gqd", fence=True)
                        pst["gel"][m0] = ("gqd", t_gq)

                        def mkqp2(h, oc=oc):
                            h.tensor_scalar_add(qpf[oc % 2][:, :], qpf[oc % 2][:, :], 0.21)
                            return h.tensor_copy(qpb[oc % 2][:, :], qpf[oc % 2][:, :])
                        t_qpd[oc] = sch.emit("vector", mkqp2,
                                             waits=[("gqd", t_gq), ("rq", C.get("rq", 0)),
                                                    ("stqp", C.get("stqp", 0))],
                                             inc="qpd", fence=True)
                        t_k = mkproj("k", m0 + 1)
                        t_gk = sch.emit("scalar", lambda h, oc=oc, m=m0 + 1: h.activation(
                            kpf[oc % 2][:, :], PP[m % 2][:, :], AF.Gelu,
                            bias=bsb["k"][:, oc:oc + 1]),
                            waits=[("pmk", t_k), ("kpd", C.get("kpd", 0)),
                                   ("ropk", C.get("ropk", 0))],
                            inc="gkd", fence=True)
                        pst["gel"][m0 + 1] = ("gkd", t_gk)

                        def mkkp(h, oc=oc, tt=tt):
                            h.tensor_scalar_add(kpf[oc % 2][:, :], kpf[oc % 2][:, :], 0.21)
                            h.tensor_copy(kpb[oc % 2][:, :], kpf[oc % 2][:, :])
                            return h.reduce_sum(kmtmp[:, tt * 8 + oc:tt * 8 + oc + 1],
                                                kpf[oc % 2][:, :], axis=AX.X)
                        t_kpd[oc] = sch.emit("vector", mkkp,
                                             waits=[("gkd", t_gk), ("rk", C.get("rk", 0))],
                                             inc="kpd", fence=True)
                        t_v = mkproj("v", m0 + 2)
                        t_gv2[oc] = sch.emit("scalar", lambda h, oc=oc, m=m0 + 2: h.activation(
                            vbt[oc % 2][:, :], PP[m % 2][:, :], AF.Identity,
                            bias=bsb["v"][:, oc:oc + 1]),
                            waits=[("pmv", t_v), ("trm", C.get("trm", 0))],
                            inc="gvd", fence=True)
                        pst["gel"][m0 + 2] = ("gvd", t_gv2[oc])
                        pst["m"] = m0 + 3
                prev["pmv"] = C["pmv"]
                prev["gvd"] = C["gvd"]
                prev["ropk"] = C["ropk"]
                prev["trev"] = C["trev"]
        if do_proj:
            def mkpart(h):
                for b_ in range(2):
                    for oc in range(NOC):
                        c = b_ * 8 + oc
                        inst = h.tensor_add(km16[:, c:c + 1],
                                            kmtmp[:, (2 * b_) * 8 + oc:(2 * b_) * 8 + oc + 1],
                                            kmtmp[:, (2 * b_ + 1) * 8 + oc:(2 * b_ + 1) * 8 + oc + 1])
                return inst
            t_pt = sch.emit("vector", mkpart,
                            waits=[("kpd", C["kpd"]), ("z0", 1)],
                            inc="ptd", fence=True)
            sch.emit("sync", lambda h: h.dma_start(out=part_o[:, 0:1024], in_=kvsb[:, :]),
                     waits=[("kve", C["kve"])], inc="stpt", k=16)
            sch.emit("sync", lambda h: h.dma_start(out=part_o[:, 1024:1040], in_=km16[:, :]),
                     waits=[("ptd", t_pt)], inc="stpt", k=16)
            sch.emit("sync", None, waits=[("stpt", 32), ("stqr", C["stqr"]),
                                          ("stqp", C["stqp"])])
        if do_fin:
            sch.emit("sync", None, waits=[("sthn", C["sthn"])])
        sch.build(block)
    return nc


def _rope_tables(j):
    inv = (1.0 / 10000.0 ** (np.arange(0, D, 2, dtype=np.float32) / np.float32(D))).astype(np.float32)
    inv2 = np.repeat(inv, 2)
    cos = np.empty((D, TOK), np.float32)
    sin = np.empty((D, TOK), np.float32)
    for i, (b, c) in enumerate(_chunks_of(j)):
        pos = np.arange(c * CH, (c + 1) * CH, dtype=np.float32)
        fr = pos[None, :] * inv2[:, None]
        cos[:, i * CH:(i + 1) * CH] = np.cos(fr)
        sin[:, i * CH:(i + 1) * CH] = np.sin(fr)
    return cos.astype(BF), sin.astype(BF)


def _rmat():
    r = np.zeros((128, 128), np.float32)
    for i in range(64):
        r[2 * i + 1, 2 * i] = -1.0
        r[2 * i, 2 * i + 1] = 1.0
    return r


def _e2():
    e = np.zeros((2, 128), np.float32)
    e[0, 0:64] = 1.0
    e[1, 64:128] = 1.0
    return e


def _cols128(v):
    n = v.shape[0] // 128
    return np.ascontiguousarray(np.asarray(v).reshape(n, 128).T)



def kernel(**inputs):
    g = {k: np.ascontiguousarray(np.asarray(v)) for k, v in inputs.items()}
    cores = list(range(NCORES))
    ident = np.eye(128, dtype=np.float32)

    xs = _stage_xT(g["L0_attn_in_features"])
    wq = np.ascontiguousarray(g["L0_attn_Wq"].T)
    wk = np.ascontiguousarray(g["L0_attn_Wk"].T)
    wv = np.ascontiguousarray(g["L0_attn_Wv"].T)
    res = run_bass_kernel_spmd(build_p1(), [
        {"x": xs[j], "wqt": wq, "wkt": wk, "wvt": wv} for j in cores], cores)
    qt = [np.asarray(res.results[j]["qt"]) for j in cores]
    kt = [np.asarray(res.results[j]["kt"]) for j in cores]
    vt = [np.asarray(res.results[j]["vt"]) for j in cores]

    qh, kh, vh = _assemble_bh(qt, kt, vt)
    mask = _causal_mask_strips()
    ones1 = np.ones((1, 64), np.float32)
    ident2 = np.concatenate([np.eye(64), np.eye(64)], axis=0).astype(np.float32)
    res = run_bass_kernel_spmd(build_p2(), [
        {"qh": qh[j], "kh": kh[j], "vh": vh[j], "mask": mask, "ones1": ones1,
         "ident2": ident2} for j in cores], cores)
    ao = [np.asarray(res.results[j]["ao"]) for j in cores]

    ats = _assemble_attnT(ao)
    res = run_bass_kernel_spmd(build_p3(), [
        {"x": xs[j], "at": ats[j], "wot": np.ascontiguousarray(g["L0_attn_Wo"].T),
         "wgt": np.ascontiguousarray(g["L0_ffn_w_gate"].T),
         "wut": np.ascontiguousarray(g["L0_ffn_w_up"].T),
         "wdt": np.ascontiguousarray(g["L0_ffn_w_down"].T)} for j in cores], cores)
    x1s = [np.asarray(res.results[j]["x1"], np.float32) for j in cores]
    x1full = _unstage_xT(x1s)

    x1h = _stage_halo(x1full)
    w1m, w2m = _conv_mats(g["L1_attn_conv1_w"], g["L1_attn_conv2_w"])
    emks = []
    for j in cores:
        emk = np.ones((128, 32), np.float32)
        for stg in range(8):
            i, st = divmod(stg, 2)
            b, c = _chunks_of(j)[i]
            if c == 0 and st == 0:
                emk[:, stg * 4:stg * 4 + 2] = 0.0
            if c == 15 and st == 1:
                emk[:, stg * 4 + 2:stg * 4 + 4] = 0.0
        emks.append(emk)
    res = run_bass_kernel_spmd(build_p5(), [
        {"x1h": x1h[j], "x1": x1s[j], "w1m": w1m, "w2m": w2m,
         "b1c": _cols128(g["L1_attn_conv1_b"]), "b2c": _cols128(g["L1_attn_conv2_b"]),
         "g0c": _cols128(g["L1_attn_ln0_g"]), "b0c": _cols128(g["L1_attn_ln0_b"]),
         "emk": emks[j]} for j in cores], cores)
    hs = [np.asarray(res.results[j]["h0"], np.float32) for j in cores]

    tables = [_rope_tables(j) for j in cores]
    rm = _rmat()
    e2 = _e2()

    def proj_in(r):
        return {"wq": np.ascontiguousarray(g["L1_attn_Wq"][r]),
                "wk": np.ascontiguousarray(g["L1_attn_Wk"][r]),
                "wv": np.ascontiguousarray(g["L1_attn_Wv"][r]),
                "bq": _cols128(g["L1_attn_bq"][r]), "bk": _cols128(g["L1_attn_bk"][r]),
                "bv": _cols128(g["L1_attn_bv"][r]), "rmat": rm, "ident": ident}

    def fin_in(r, parts, qp_j, qr_j):
        return {"parts": parts, "qp_i": qp_j, "qr_i": qr_j,
                "lnag": _cols128(g["L1_attn_lnA_g"][r]),
                "lnab": _cols128(g["L1_attn_lnA_b"][r]), "e2": e2}

    res = run_bass_kernel_spmd(build_p6("first"), [
        {"hin": hs[j], "cosb": tables[j][0], "sinb": tables[j][1], **proj_in(0)}
        for j in cores], cores)
    qp = [np.asarray(res.results[j]["qp_o"]) for j in cores]
    qr = [np.asarray(res.results[j]["qr_o"]) for j in cores]
    parts = np.stack([np.asarray(res.results[j]["part_o"], np.float32) for j in cores])

    nc_mid = None
    for r in range(4):
        if r < 3:
            if nc_mid is None:
                nc_mid = build_p6("mid")
            res = run_bass_kernel_spmd(nc_mid, [
                {"hin": hs[j], "cosb": tables[j][0], "sinb": tables[j][1],
                 **fin_in(r, parts, qp[j], qr[j]), **proj_in(r + 1)} for j in cores], cores)
            hs = [np.asarray(res.results[j]["hn"], np.float32) for j in cores]
            qp = [np.asarray(res.results[j]["qp_o"]) for j in cores]
            qr = [np.asarray(res.results[j]["qr_o"]) for j in cores]
            parts = np.stack([np.asarray(res.results[j]["part_o"], np.float32) for j in cores])
        else:
            res = run_bass_kernel_spmd(build_p6("last"), [
                {"hin": hs[j], **fin_in(r, parts, qp[j], qr[j])} for j in cores], cores)
            hs = [np.asarray(res.results[j]["hn"], np.float32) for j in cores]

    res = run_bass_kernel_spmd(build_p7(), [
        {"hin": hs[j], "x1": x1s[j], "w1": np.ascontiguousarray(g["L1_attn_W1"]),
         "w2": np.ascontiguousarray(g["L1_attn_W2"]),
         "b1r": _cols128(g["L1_attn_b1"]), "b2r": _cols128(g["L1_attn_b2"]),
         "g2r": _cols128(g["L1_attn_ln2_g"]), "bb2r": _cols128(g["L1_attn_ln2_b"])}
        for j in cores], cores)
    x2s = [np.asarray(res.results[j]["x2"], np.float32) for j in cores]

    res = run_bass_kernel_spmd(build_p8(), [
        {"x2": x2s[j], "wst": np.ascontiguousarray(g["L1_ffn_weight"].T)}
        for j in cores], cores)
    outs = [np.asarray(res.results[j]["out"], np.float32) for j in cores]
    return _unstage_xT(outs)


# version 09 (final)
# speedup vs baseline: 1.3393x; 1.0235x over previous
import numpy as np
import ml_dtypes
from contextlib import ExitStack

import concourse.bass as bass
import concourse.mybir as mybir
from concourse.bass_utils import run_bass_kernel_spmd

F32 = mybir.dt.float32
BF16 = mybir.dt.bfloat16
I32 = mybir.dt.int32
AF = mybir.ActivationFunctionType
ALU = mybir.AluOpType
AX = mybir.AxisListType
BF = ml_dtypes.bfloat16

NCORES = 8
B, S, D, H, DH = 2, 8192, 1024, 16, 64
CH = 512
NCH = S // CH
TOK = 2048
NOC = D // 128


def _chunks_of(j):
    return [(0, j), (0, 15 - j), (1, j), (1, 15 - j)]


def _rank_half(g):
    return (g, 0) if g < 8 else (15 - g, 1)


def _col_of(b, g):
    _, hf = _rank_half(g)
    return b * 1024 + hf * 512



def build_p1():
    nc = bass.Bass("TRN2", target_bir_lowering=False)
    x = nc.dram_tensor("x", [D, TOK], F32, kind="ExternalInput").ap()
    wts = {w: nc.dram_tensor("w" + w + "t", [D, D], F32, kind="ExternalInput").ap()
           for w in "qkv"}
    qt = nc.dram_tensor("qt", [D, TOK], BF16, kind="ExternalOutput").ap()
    kt = nc.dram_tensor("kt", [D, TOK], BF16, kind="ExternalOutput").ap()
    vt = nc.dram_tensor("vt", [D, TOK], BF16, kind="ExternalOutput").ap()

    groups = [(w, oc, t) for w in "qkv" for oc in range(NOC) for t in range(4)]
    NLD = 8 + 3 * 8

    es = ExitStack()
    with es:
        xbf = [es.enter_context(nc.sbuf_tensor(f"xbf{i}", [128, TOK], BF16)) for i in range(NOC)]
        wsb = {w: [es.enter_context(nc.sbuf_tensor(f"w{w}{i}", [128, D], BF16)) for i in range(NOC)]
               for w in "qkv"}
        osb = {w: [es.enter_context(nc.sbuf_tensor(f"o{w}{i}", [128, TOK], BF16)) for i in range(NOC)]
               for w in "qkv"}
        fv = es.enter_context(nc.sbuf_tensor("fv", [1, 8], F32))
        ps = [es.enter_context(nc.psum_tensor(f"ps{i}", [128, 512], F32)) for i in range(4)]
        ld = es.enter_context(nc.semaphore("ld"))
        mm = es.enter_context(nc.semaphore("mm"))
        ev = es.enter_context(nc.semaphore("ev"))
        st1 = es.enter_context(nc.semaphore("st1"))
        block = es.enter_context(nc.Block())

        out_d = {"q": qt, "k": kt, "v": vt}

        @block.gpsimd
        def _(g):
            for i in range(NOC):
                g.dma_start(out=xbf[i][:, :], in_=x[i * 128:(i + 1) * 128, :]).then_inc(ld, 16)
            for w in "qkv":
                for i in range(NOC):
                    g.dma_start(out=wsb[w][i][:, :], in_=wts[w][i * 128:(i + 1) * 128, :]).then_inc(ld, 16)

        @block.tensor
        def _(t_):
            t_.wait_ge(ld, 16 * NLD)
            for gi, (w, oc, t) in enumerate(groups):
                if gi >= 4:
                    t_.wait_ge(ev, gi - 3)
                for ic in range(NOC):
                    inst = t_.matmul(ps[gi % 4][:, :],
                                     wsb[w][ic][:, oc * 128:(oc + 1) * 128],
                                     xbf[ic][:, t * 512:(t + 1) * 512],
                                     start=(ic == 0), stop=(ic == NOC - 1))
                inst.then_inc(mm, 1)

        @block.vector
        def _(v_):
            for gi, (w, oc, t) in enumerate(groups):
                v_.wait_ge(mm, gi + 1)
                v_.tensor_copy(osb[w][oc][:, t * 512:(t + 1) * 512], ps[gi % 4][:, :])
                v_.tensor_copy(fv[:, 0:4], fv[:, 4:8]).then_inc(ev, 1)

        @block.sync
        def _(s_):
            for w, base in (("q", 0), ("k", 32), ("v", 64)):
                for oc in range(NOC):
                    s_.wait_ge(ev, base + oc * 4 + 4)
                    s_.dma_start(out=out_d[w][oc * 128:(oc + 1) * 128, :],
                                 in_=osb[w][oc][:, :]).then_inc(st1, 16)
            s_.wait_ge(st1, 16 * 24)
    return nc



P2_NCH = NCH


def build_p2():
    nc = bass.Bass("TRN2", target_bir_lowering=False)
    qh = nc.dram_tensor("qh", [4, 64, S], BF16, kind="ExternalInput").ap()
    kh = nc.dram_tensor("kh", [4, 64, S], BF16, kind="ExternalInput").ap()
    vh = nc.dram_tensor("vh", [4, 64, S], BF16, kind="ExternalInput").ap()
    mask = nc.dram_tensor("mask", [128, 2048], F32, kind="ExternalInput").ap()
    ones1 = nc.dram_tensor("ones1", [1, 64], F32, kind="ExternalInput").ap()
    ident2 = nc.dram_tensor("ident2", [128, 64], F32, kind="ExternalInput").ap()
    ao = nc.dram_tensor("ao", [4, 64, S], BF16, kind="ExternalOutput").ap()

    units = []
    epis = []
    si = 0
    nmask = 0
    for hp in range(2):
        for pp in range(2):
            p = 2 * hp + pp
            for qc in range(P2_NCH):
                for g in range(qc + 1):
                    for half in range(2):
                        diag = (g == qc)
                        first = (g == 0 and half == 0)
                        last = (g == qc and half == 1)
                        units.append((si, p, hp, pp, qc, g, half, diag, first, last, nmask))
                        if diag:
                            nmask += 1
                        si += 1
                epis.append((len(epis), p, hp, pp, qc, si - 1))
    NE = len(epis)
    e_of = {(p_, qc_): e_ for (e_, p_, hp_, pp_, qc_, sl_) in epis}
    avh_after_p = {}
    cum = 0
    for (si_, p, hp, pp, qc, g, half, diag, first, last, nm) in units:
        cum += 1
        avh_after_p[p] = cum

    es = ExitStack()
    with es:
        q2 = [es.enter_context(nc.sbuf_tensor(f"q2_{i}", [128, S], BF16)) for i in range(2)]
        k2 = [es.enter_context(nc.sbuf_tensor(f"k2_{i}", [128, S], BF16)) for i in range(2)]
        v2 = [es.enter_context(nc.sbuf_tensor(f"v2_{i}", [128, S], BF16)) for i in range(2)]
        msk = es.enter_context(nc.sbuf_tensor("msk", [128, 2048], BF16))
        on1 = es.enter_context(nc.sbuf_tensor("on1", [128, 64], BF16))
        idn2 = es.enter_context(nc.sbuf_tensor("idn2", [128, 64], BF16))
        pt = [es.enter_context(nc.sbuf_tensor(f"pt{i}", [128, 1024], BF16)) for i in range(4)]
        ptm = [es.enter_context(nc.sbuf_tensor(f"ptm{i}", [128, 1024], BF16)) for i in range(4)]
        vtk = [es.enter_context(nc.sbuf_tensor(f"vtk{i}", [128, 65 * 64], BF16)) for i in range(2)]
        zin = es.enter_context(nc.sbuf_tensor("zin", [128, 512], BF16))
        dsb = es.enter_context(nc.sbuf_tensor("dsb", [128, 512], F32))
        lnt = es.enter_context(nc.sbuf_tensor("lnt", [128, 512], F32))
        avs = [es.enter_context(nc.sbuf_tensor(f"avs{i}", [64, 512], F32)) for i in range(2)]
        aos = [es.enter_context(nc.sbuf_tensor(f"aos{i}", [64, 512], BF16)) for i in range(2)]
        fv = es.enter_context(nc.sbuf_tensor("fv", [1, 8], F32))
        fs = es.enter_context(nc.sbuf_tensor("fs", [1, 8], F32))
        ps_s = [es.enter_context(nc.psum_tensor(f"ps_s{i}", [128, 1024], F32)) for i in range(2)]
        ps_av = es.enter_context(nc.psum_tensor("ps_av", [128, 512], F32))
        ps_zb = es.enter_context(nc.psum_tensor("ps_zb", [64, 512], F32))
        trv = [es.enter_context(nc.psum_tensor(f"trv{i}", [128, 64], F32)) for i in range(2)]
        ld = es.enter_context(nc.semaphore("ld"))
        mm_s = es.enter_context(nc.semaphore("mm_s"))
        expd = es.enter_context(nc.semaphore("expd"))
        mskd = es.enter_context(nc.semaphore("mskd"))
        avh = es.enter_context(nc.semaphore("avh"))
        rcp = es.enter_context(nc.semaphore("rcp"))
        zrd = es.enter_context(nc.semaphore("zrd"))
        mm_zb = es.enter_context(nc.semaphore("mm_zb"))
        aoev = es.enter_context(nc.semaphore("aoev"))
        trm = es.enter_context(nc.semaphore("trm"))
        trvs = es.enter_context(nc.semaphore("trvs"))
        stao = [es.enter_context(nc.semaphore(f"stao{i}")) for i in range(2)]
        block = es.enter_context(nc.Block())

        NLD = 12 + 3

        @block.gpsimd
        def _(g_):
            for hp in range(2):
                g_.dma_start(out=q2[hp][0:64, :], in_=qh[2 * hp, :, :]).then_inc(ld, 16)
                g_.dma_start(out=q2[hp][64:128, :], in_=qh[2 * hp + 1, :, :]).then_inc(ld, 16)
                g_.dma_start(out=k2[hp][0:64, :], in_=kh[2 * hp, :, :]).then_inc(ld, 16)
                g_.dma_start(out=k2[hp][64:128, :], in_=kh[2 * hp + 1, :, :]).then_inc(ld, 16)
                g_.dma_start(out=v2[hp][0:64, :], in_=vh[2 * hp, :, :]).then_inc(ld, 16)
                g_.dma_start(out=v2[hp][64:128, :], in_=vh[2 * hp + 1, :, :]).then_inc(ld, 16)
            g_.dma_start(out=msk[:, :], in_=mask[:, :]).then_inc(ld, 16)
            g_.dma_start(out=on1[64:65, :], in_=ones1[:, :]).then_inc(ld, 16)
            g_.dma_start(out=idn2[:, :], in_=ident2[:, :]).then_inc(ld, 16)

        @block.tensor
        def _(t_):
            t_.wait_ge(ld, 16 * NLD)

            def emit_tr(t_, u):
                (si_, p, hp, pp, qc, g, half, diag, first, last, nm) = u
                rb = 64 * pp
                if p >= 2:
                    t_.wait_ge(avh, avh_after_p[p - 2])
                t_.wait_ge(mskd, 1)
                for kb in range(64):
                    if p * 64 + kb >= 2:
                        t_.wait_ge(trvs, p * 64 + kb - 1)
                    t_.matmul(trv[kb % 2][:, :],
                              v2[hp][rb:rb + 64, kb * 128:(kb + 1) * 128],
                              idn2[rb:rb + 64, :], start=True, stop=True).then_inc(trm, 1)

            def emit_scores(t_, u):
                (si_, p, hp, pp, qc, g, half, diag, first, last, nm) = u
                rb = 64 * pp
                if si_ >= 2:
                    t_.wait_ge(expd, si_ - 1)
                for kb2 in range(2):
                    key0 = g * 512 + (half * 2 + kb2) * 128
                    inst = t_.matmul(ps_s[si_ % 2][:, kb2 * 512:(kb2 + 1) * 512],
                                     k2[hp][rb:rb + 64, key0:key0 + 128],
                                     q2[hp][rb:rb + 64, qc * 512:(qc + 1) * 512],
                                     start=True, stop=True)
                inst.then_inc(mm_s, 1)

            pending_z = []

            def emit_av(t_, u):
                (si_, p, hp, pp, qc, g, half, diag, first, last, nm) = u
                if diag:
                    t_.wait_ge(mskd, nm + 2)
                else:
                    t_.wait_ge(expd, si_ + 2)
                if first:
                    e = e_of[(p, qc)]
                    if e >= 1:
                        t_.wait_ge(rcp, e)
                src_pt = ptm[si_ % 4] if diag else pt[si_ % 4]
                for kb2 in range(2):
                    blk = g * 4 + half * 2 + kb2
                    t_.wait_ge(trvs, p * 64 + blk + 1)
                    inst = t_.matmul(ps_av[0:65, :],
                                     vtk[p % 2][:, blk * 65:blk * 65 + 65],
                                     src_pt[:, kb2 * 512:(kb2 + 1) * 512],
                                     start=(first and kb2 == 0), stop=(last and kb2 == 1))
                inst.then_inc(avh, 1)
                if last:
                    pending_z.append((e_of[(p, qc)], si_))

            def emit_z(t_, e):
                t_.wait_ge(zrd, e + 1)
                if e >= 1:
                    t_.wait_ge(aoev, e)
                t_.matmul(ps_zb[:, :], on1[64:65, :], zin[64:65, :],
                          start=True, stop=True).then_inc(mm_zb, 1)

            U4 = len(units) // 4
            pair_starts = {0, U4, 2 * U4, 3 * U4}
            for idx in range(len(units)):
                emit_scores(t_, units[idx])
                while pending_z and pending_z[0][1] <= idx - 5:
                    emit_z(t_, pending_z.pop(0)[0])
                if idx >= 3:
                    emit_av(t_, units[idx - 3])
                if idx - 2 in pair_starts and idx - 2 >= 0:
                    emit_tr(t_, units[idx - 2])
            for k in (3, 2, 1):
                emit_av(t_, units[-k])
            while pending_z:
                emit_z(t_, pending_z.pop(0)[0])

        @block.scalar
        def _(s_):
            s_.memzero(fs[:, :])
            def emit_act(s_, u):
                (si_, p, hp, pp, qc, g, half, diag, first, last, nm) = u
                s_.wait_ge(mm_s, si_ + 1)
                if si_ >= 4:
                    s_.wait_ge(avh, si_ - 3)
                s_.activation(pt[si_ % 4][:, :], ps_s[si_ % 2][:, :], AF.Exp,
                              scale=0.125).then_inc(expd, 1)

            def emit_zepi(s_, e):
                s_.wait_ge(rcp, e + 1)
                if e >= 1:
                    s_.wait_ge(mm_zb, e)
                s_.activation(lnt[64:65, :], dsb[64:65, :], AF.Ln)
                s_.activation(zin[64:65, :], lnt[64:65, :], AF.Exp, scale=-1.0)
                s_.copy(fs[:, 0:4], fs[:, 4:8]).then_inc(zrd, 1)

            ei = 0
            for idx in range(len(units)):
                emit_act(s_, units[idx])
                if idx >= 1 and units[idx - 1][9]:
                    emit_zepi(s_, ei)
                    ei += 1
            s_.copy(fs[:, 0:4], fs[:, 4:8]).then_inc(expd, 2)
            emit_zepi(s_, ei)

        @block.vector
        def _(v_):
            v_.memset(fv[:, :], 0.0)
            v_.memset(vtk[0][:, :], 1.0)
            v_.memset(vtk[1][:, :], 1.0)
            v_.tensor_copy(fv[:, 0:4], fv[:, 4:8]).then_inc(mskd, 1)

            def emit_mul(v_, e):
                v_.wait_ge(mm_zb, e + 1)
                if e >= 2:
                    v_.wait_ge(stao[e % 2], 16 * ((e - 2) // 2 + 1))
                v_.tensor_mul(aos[e % 2][:, :], avs[e % 2][:, :], ps_zb[:, :])
                v_.tensor_copy(fv[:, 0:4], fv[:, 4:8]).then_inc(aoev, 1)

            nt = 0
            ei = 0
            for (si_, p, hp, pp, qc, g, half, diag, first, last, nm) in units:
                if qc == 0 and g == 0 and half == 0:
                    for kb in range(64):
                        v_.wait_ge(trm, nt + 1)
                        v_.tensor_copy(vtk[p % 2][:, kb * 65:kb * 65 + 64], trv[kb % 2][:, :])
                        v_.tensor_copy(fv[:, 0:4], fv[:, 4:8]).then_inc(trvs, 1)
                        nt += 1
                if diag:
                    v_.wait_ge(expd, si_ + 2)
                    v_.tensor_tensor(ptm[si_ % 4][:, :], pt[si_ % 4][:, :],
                                     msk[:, half * 1024:(half + 1) * 1024], ALU.mult)
                    v_.tensor_copy(fv[:, 0:4], fv[:, 4:8]).then_inc(mskd, 1)
                if last:
                    (e, p_, hp_, pp_, qc_, sl_) = epis[ei]
                    v_.wait_ge(avh, si_ + 1)
                    v_.tensor_copy(dsb[64:65, :], ps_av[64:65, :])
                    v_.tensor_copy(avs[e % 2][:, :], ps_av[0:64, :])
                    v_.tensor_copy(fv[:, 0:4], fv[:, 4:8]).then_inc(rcp, 1)
                    if e >= 1:
                        emit_mul(v_, e - 1)
                    ei += 1
            emit_mul(v_, NE - 1)

        @block.sync
        def _(s_):
            for (e, p, hp, pp, qc, sl_) in epis:
                s_.wait_ge(aoev, e + 1)
                s_.dma_start(out=ao[p, :, qc * 512:(qc + 1) * 512],
                             in_=aos[e % 2][:, :]).then_inc(stao[e % 2], 16)
            s_.wait_ge(stao[0], 16 * (NE // 2))
            s_.wait_ge(stao[1], 16 * (NE // 2))
    return nc



def _stage_xT(xfull):
    out = []
    for j in range(NCORES):
        cols = [np.ascontiguousarray(xfull[b, c * CH:(c + 1) * CH, :].T)
                for (b, c) in _chunks_of(j)]
        out.append(np.ascontiguousarray(np.concatenate(cols, axis=1)))
    return out


def _unstage_xT(slabs):
    xfull = np.empty((B, S, D), np.asarray(slabs[0]).dtype)
    for j in range(NCORES):
        for i, (b, c) in enumerate(_chunks_of(j)):
            xfull[b, c * CH:(c + 1) * CH, :] = np.asarray(slabs[j])[:, i * CH:(i + 1) * CH].T
    return xfull


def _causal_mask_strips():
    mask = np.zeros((128, 2048), np.float32)
    for kb in range(4):
        kk = np.arange(128)[:, None] + kb * 128
        qq = np.arange(512)[None, :]
        mask[:, kb * 512:(kb + 1) * 512] = (kk <= qq).astype(np.float32)
    return mask


def _assemble_bh(qt_all, kt_all, vt_all):
    qh_all, kh_all, vh_all = [], [], []
    for j in range(NCORES):
        qh = np.empty((4, 64, S), BF)
        kh = np.empty((4, 64, S), BF)
        vh = np.empty((4, 64, S), BF)
        for p in range(4):
            bh = 4 * j + p
            b, h = bh // 16, bh % 16
            for g in range(NCH):
                r, _hf = _rank_half(g)
                col = _col_of(b, g)
                qh[p][:, g * CH:(g + 1) * CH] = qt_all[r][h * 64:(h + 1) * 64, col:col + CH]
                kh[p][:, g * CH:(g + 1) * CH] = kt_all[r][h * 64:(h + 1) * 64, col:col + CH]
                vh[p][:, g * CH:(g + 1) * CH] = vt_all[r][h * 64:(h + 1) * 64, col:col + CH]
        qh_all.append(qh)
        kh_all.append(kh)
        vh_all.append(vh)
    return qh_all, kh_all, vh_all


def _assemble_attnT(ao_all):
    out = []
    for j in range(NCORES):
        at = np.empty((D, TOK), BF)
        for i, (b, c) in enumerate(_chunks_of(j)):
            for h in range(H):
                bh = b * 16 + h
                r, p = bh // 4, bh % 4
                at[h * 64:(h + 1) * 64, i * CH:(i + 1) * CH] = \
                    ao_all[r][p][:, c * CH:(c + 1) * CH]
        out.append(at)
    return out


def kernel(**inputs):
    raise NotImplementedError("under construction")



SERIALIZE = False


class Sch:
    ENGS = ("tensor", "vector", "scalar", "gpsimd", "sync")

    def __init__(self, nc, es):
        self.nc = nc
        self.es = es
        self.ops = {e: [] for e in self.ENGS}
        self.cnt = {}
        self.sems = {}
        self.dma_sems = set()

    def _sem(self, name):
        if name not in self.sems:
            self.sems[name] = self.es.enter_context(self.nc.semaphore(name))
            self.cnt[name] = 0
        return self.sems[name]

    def emit(self, eng, fn, waits=(), inc=None, k=1, fence=False):
        if SERIALIZE is True or (SERIALIZE and eng in SERIALIZE):
            waits = list(waits) + [(s, v) for (s, v) in self.cnt.items() if v > 0]
        waits = [(s, self.cnt.get(s, 0) if s in self.dma_sems and v > 0 else v)
                 for (s, v) in waits]
        waits = [(s, v) for (s, v) in waits if v > 0]
        for s, _ in waits:
            self._sem(s)
        if inc is not None:
            self._sem(inc)
            if k >= 16:
                self.dma_sems.add(inc)
            self.cnt[inc] += k
            tok = self.cnt[inc]
        else:
            tok = None
        self.ops[eng].append((waits, fn, inc, k, fence))
        return tok

    def build(self, block):
        fbuf = {}
        for eng in ("vector", "scalar", "gpsimd"):
            fbuf[eng] = self.es.enter_context(
                self.nc.sbuf_tensor(f"_fence_{eng}", [1, 8], F32))

        def mk(eng):
            def body(h):
                if eng == "vector" or eng == "gpsimd":
                    h.memset(fbuf[eng][:, :], 0.0)
                elif eng == "scalar":
                    h.memzero(fbuf[eng][:, :])
                for (waits, fn, inc, k, fence) in self.ops[eng]:
                    for (s, v) in waits:
                        h.wait_ge(self.sems[s], v)
                    inst = fn(h) if fn is not None else None
                    if inc is not None:
                        if fence:
                            if eng == "vector" or eng == "gpsimd":
                                inst = h.tensor_copy(fbuf[eng][:, 0:4], fbuf[eng][:, 4:8])
                            elif eng == "scalar":
                                inst = h.copy(fbuf[eng][:, 0:4], fbuf[eng][:, 4:8])
                        inst.then_inc(self.sems[inc], k)
            return body

        block.tensor(mk("tensor"))
        block.vector(mk("vector"))
        block.scalar(mk("scalar"))
        block.gpsimd(mk("gpsimd"))
        block.sync(mk("sync"))


def _emit_ln(sch, sbf, s2bf, sfull, onescol, onesrow, ps_stat, ps_b, scratch,
             dst_f32, dst_bf, eps, gcol=None, bcol=None, pre_tokens=(),
             bc_waits=(), apply_waits=(), tag="", dbg=None):
    msb, m2, var, rstd, mr, rstd_bf, c_bf = scratch
    t_m = sch.emit("tensor", lambda h: [
        h.matmul(ps_stat[0][0:1, :], onescol[:, 0:1], sbf[ic][:, :],
                 start=(ic == 0), stop=(ic == NOC - 1)) for ic in range(NOC)][-1],
        waits=list(pre_tokens), inc="mst" + tag)
    t_q = sch.emit("tensor", lambda h: [
        h.matmul(ps_stat[1][0:1, :], onescol[:, 0:1], s2bf[ic][:, :],
                 start=(ic == 0), stop=(ic == NOC - 1)) for ic in range(NOC)][-1],
        inc="mst" + tag)

    def fin1(h):
        if dbg is not None:
            h.tensor_copy(dbg[0][0:1, dbg[4] * 512:(dbg[4] + 1) * 512], ps_stat[0][0:1, :])
            h.tensor_copy(dbg[1][0:1, dbg[4] * 512:(dbg[4] + 1) * 512], ps_stat[1][0:1, :])
        h.tensor_scalar_mul(msb[0:1, :], ps_stat[0][0:1, :], 1.0 / D)
        h.tensor_mul(m2[0:1, :], msb[0:1, :], msb[0:1, :])
        h.scalar_tensor_tensor(var[0:1, :], ps_stat[1][0:1, :], 1.0 / D,
                               m2[0:1, :], ALU.mult, ALU.subtract)
        h.tensor_scalar_add(m2[0:1, :], var[0:1, :], float(eps))
        if dbg is not None:
            h.tensor_copy(dbg[2][0:1, dbg[4] * 512:(dbg[4] + 1) * 512], var[0:1, :])
        return h.reciprocal(var[0:1, :], m2[0:1, :])
    t_f1 = sch.emit("vector", fin1, waits=[("mst" + tag, t_q)], inc="stf" + tag, fence=True)

    t_ln = sch.emit("scalar", lambda h: h.activation(rstd[0:1, :], var[0:1, :], AF.Sqrt),
                    waits=[("stf" + tag, t_f1)], inc="sln" + tag, fence=True)

    def fin2(h):
        h.tensor_mul(mr[0:1, :], msb[0:1, :], rstd[0:1, :])
        h.tensor_copy(rstd_bf[0:1, :], rstd[0:1, :])
        return h.tensor_copy(c_bf[0:1, :], mr[0:1, :])
    t_f2 = sch.emit("vector", fin2, waits=[("sln" + tag, t_ln)], inc="stf2" + tag, fence=True)

    def bcast(h):
        h.matmul(ps_b[0][:, :], onesrow[0:1, :], rstd_bf[0:1, :], start=True, stop=True)
        return h.matmul(ps_b[1][:, :], onesrow[0:1, :], c_bf[0:1, :], start=True, stop=True)
    t_bc = sch.emit("tensor", bcast, waits=[("stf2" + tag, t_f2)] + list(bc_waits),
                    inc="mbc" + tag)

    toks = []
    for oc in range(NOC):
        def apply(h, oc=oc):
            h.tensor_mul(dst_f32[oc][:, :], sfull[oc][:, :], ps_b[0][:, :])
            inst = h.tensor_sub(dst_f32[oc][:, :], dst_f32[oc][:, :], ps_b[1][:, :])
            if gcol is not None:
                inst = h.tensor_scalar(dst_f32[oc][:, :], dst_f32[oc][:, :],
                                       gcol[oc], bcol[oc], ALU.mult, ALU.add)
            if dst_bf is not None:
                inst = h.tensor_copy(dst_bf[oc][:, :], dst_f32[oc][:, :])
            return inst
        toks.append(sch.emit("vector", apply,
                             waits=[("mbc" + tag, t_bc)] + list(apply_waits[oc] if apply_waits else []),
                             inc="apw" + tag, fence=True))
    return toks



HID = 3072
NHC = HID // 128


DBG_NT = 4


def build_p3():
    nc = bass.Bass("TRN2", target_bir_lowering=False)
    x = nc.dram_tensor("x", [D, TOK], F32, kind="ExternalInput").ap()
    at = nc.dram_tensor("at", [D, TOK], BF16, kind="ExternalInput").ap()
    wot = nc.dram_tensor("wot", [D, D], F32, kind="ExternalInput").ap()
    wgt = nc.dram_tensor("wgt", [128, NHC * 1024], I32, kind="ExternalInput").ap()
    wut = nc.dram_tensor("wut", [128, NHC * 1024], I32, kind="ExternalInput").ap()
    wdt = nc.dram_tensor("wdt", [128, NOC * HID], I32, kind="ExternalInput").ap()
    x1 = nc.dram_tensor("x1", [D, TOK], F32, kind="ExternalOutput").ap()

    es = ExitStack()
    with es:
        wo = [es.enter_context(nc.sbuf_tensor(f"wo{i}", [128, D], BF16)) for i in range(NOC)]
        xt = [es.enter_context(nc.sbuf_tensor(f"xt{i}", [128, 512], F32)) for i in range(NOC)]
        att = [es.enter_context(nc.sbuf_tensor(f"att{i}", [128, 512], BF16)) for i in range(NOC)]
        sfull = [es.enter_context(nc.sbuf_tensor(f"sf{i}", [128, 512], F32)) for i in range(NOC)]
        sbf = [es.enter_context(nc.sbuf_tensor(f"sb{i}", [128, 512], BF16)) for i in range(NOC)]
        s2bf = [es.enter_context(nc.sbuf_tensor(f"s2{i}", [128, 512], BF16)) for i in range(NOC)]
        ybuf = [es.enter_context(nc.sbuf_tensor(f"yb{i}", [128, 512], F32)) for i in range(NOC)]
        ybf = [es.enter_context(nc.sbuf_tensor(f"yf{i}", [128, 512], BF16)) for i in range(NOC)]
        hbuf = [es.enter_context(nc.sbuf_tensor(f"hb{i}", [128, 512], BF16)) for i in range(NHC)]
        hsig = [es.enter_context(nc.sbuf_tensor(f"hs{i}", [128, 512], BF16)) for i in range(2)]
        gpan = [es.enter_context(nc.sbuf_tensor(f"gp{i}", [128, D], BF16)) for i in range(2)]
        upan = [es.enter_context(nc.sbuf_tensor(f"up{i}", [128, D], BF16)) for i in range(2)]
        dpan = [es.enter_context(nc.sbuf_tensor(f"dp{i}", [128, HID], BF16)) for i in range(2)]
        x1b = [[es.enter_context(nc.sbuf_tensor(f"x1b{z}_{i}", [128, 512], F32)) for i in range(NOC)]
               for z in range(2)]
        onescol = es.enter_context(nc.sbuf_tensor("onescol", [128, 1], BF16))
        onesrow = es.enter_context(nc.sbuf_tensor("onesrow", [1, 128], BF16))
        lns = [es.enter_context(nc.sbuf_tensor(f"lns{i}", [1, 512], F32)) for i in range(5)]
        lnb = [es.enter_context(nc.sbuf_tensor(f"lnb{i}", [1, 512], BF16)) for i in range(2)]
        scratch = (lns[0], lns[1], lns[2], lns[3], lns[4], lnb[0], lnb[1])
        P = [es.enter_context(nc.psum_tensor(f"P{i}", [128, 512], F32)) for i in range(4)]
        Sst = [es.enter_context(nc.psum_tensor(f"Sst{i}", [1, 512], F32)) for i in range(2)]
        Bb = [es.enter_context(nc.psum_tensor(f"Bb{i}", [128, 512], F32)) for i in range(2)]
        sch = Sch(nc, es)
        block = es.enter_context(nc.Block())

        sch.emit("vector", lambda h: h.memset(onescol[:, :], 1.0))
        t_ones = sch.emit("vector", lambda h: h.memset(onesrow[:, :], 1.0), inc="ones", fence=True)
        t_wo = None
        for i in range(NOC):
            t_wo = sch.emit("gpsimd", lambda h, i=i: h.dma_start(
                out=wo[i][:, :], in_=wot[i * 128:(i + 1) * 128, :]), inc="ldw", k=16)

        prev = {"evsA": {}, "mo": {}, "ldx": {}, "lda": {}, "apwA": {}, "apwB": {},
                "f1A": 0, "f1B": 0, "ev2": {}, "sto": {}}
        last_bc_readers = [("ones", t_ones)]
        last_stat_users = []

        for t in range(DBG_NT):
            tok_x, tok_at = {}, {}
            for oc in range(NOC):
                w = [("evsA", prev["evsA"].get(oc, 0))]
                tok_x[oc] = sch.emit("sync", lambda h, oc=oc, t=t: h.dma_start(
                    out=xt[oc][:, :], in_=x[oc * 128:(oc + 1) * 128, t * 512:(t + 1) * 512]),
                    waits=w, inc="ldx", k=16)
            for ic in range(NOC):
                w = [("mo", prev["mo"].get(7, 0)), ("evsA", prev["evsA"].get(7, 0))]
                tok_at[ic] = sch.emit("sync", lambda h, ic=ic, t=t: h.dma_start(
                    out=att[ic][:, :], in_=at[ic * 128:(ic + 1) * 128, t * 512:(t + 1) * 512]),
                    waits=w, inc="lda", k=16)
            tok_mo, tok_s, tok_q = {}, {}, {}
            for oc in range(NOC):
                w = [("ldw", t_wo), ("lda", tok_at[7])]
                if oc >= 2:
                    w.append(("evsA", tok_s[oc - 2]))
                elif t > 0:
                    w.append(("evsA", prev["evsA"][oc + 6]))
                tok_mo[oc] = sch.emit("tensor", lambda h, oc=oc: [
                    h.matmul(P[oc % 2][:, :], wo[ic][:, oc * 128:(oc + 1) * 128],
                             att[ic][:, :], start=(ic == 0), stop=(ic == NOC - 1))
                    for ic in range(NOC)][-1], waits=w, inc="mo")

                def mks(h, oc=oc):
                    h.tensor_add(sfull[oc][:, :], xt[oc][:, :], P[oc % 2][:, :])
                    return h.tensor_copy(sbf[oc][:, :], sfull[oc][:, :])
                tok_s[oc] = sch.emit("vector", mks,
                                     waits=[("mo", tok_mo[oc]), ("ldx", tok_x[oc]),
                                            ("mstB", prev.get("mstB", 0))],
                                     inc="evsA", fence=True)
                tok_q[oc] = sch.emit("scalar", lambda h, oc=oc: h.activation(
                    s2bf[oc][:, :], sfull[oc][:, :], AF.Square),
                    waits=[("evsA", tok_s[oc])], inc="sqA", fence=True)
            pre = [("evsA", tok_s[7]), ("sqA", tok_q[7]), ("ones", t_ones)] + last_stat_users
            apA = _emit_ln(sch, sbf, s2bf, sfull, onescol, onesrow, Sst, Bb, scratch,
                           ybuf, ybf, 1e-5, pre_tokens=pre,
                           bc_waits=last_bc_readers, tag="A")
            last_bc_readers = [("apwA", apA[7])]
            last_stat_users = [("stfA", sch.cnt["stfA"])]
            mstA_now = sch.cnt["mstA"]

            tok_pan, tok_g, tok_u, tok_sil, tok_h = {}, {}, {}, {}, {}
            for hc in range(NHC):
                w = []
                if hc >= 2:
                    w = [("sil", tok_sil[hc - 2]), ("ehh", tok_h[hc - 2])]

                sch.emit("gpsimd", lambda h, hc=hc: h.dma_start(
                    out=gpan[hc % 2][:, :],
                    in_=wgt[:, hc * 1024:(hc + 1) * 1024]),
                    waits=w, inc="ldp", k=16)
                tok_pan[hc] = sch.emit("gpsimd", lambda h, hc=hc: h.dma_start(
                    out=upan[hc % 2][:, :],
                    in_=wut[:, hc * 1024:(hc + 1) * 1024]),
                    inc="ldp", k=16)
                wmm = [("ldp", tok_pan[hc]), ("apwA", apA[7])]
                if hc >= 2:
                    wmm += [("sil", tok_sil[hc - 2]), ("ehh", tok_h[hc - 2])]
                tok_g[hc] = sch.emit("tensor", lambda h, hc=hc: [
                    h.matmul(P[2 * (hc % 2)][:, :], gpan[hc % 2][:, ic * 128:(ic + 1) * 128],
                             ybf[ic][:, :], start=(ic == 0), stop=(ic == NOC - 1))
                    for ic in range(NOC)][-1], waits=wmm, inc="mg")
                tok_u[hc] = sch.emit("tensor", lambda h, hc=hc: [
                    h.matmul(P[2 * (hc % 2) + 1][:, :], upan[hc % 2][:, ic * 128:(ic + 1) * 128],
                             ybf[ic][:, :], start=(ic == 0), stop=(ic == NOC - 1))
                    for ic in range(NOC)][-1], inc="mu")
                wsil = [("mg", tok_g[hc])]
                if hc >= 2:
                    wsil.append(("ehh", tok_h[hc - 2]))
                tok_sil[hc] = sch.emit("scalar", lambda h, hc=hc: h.activation(
                    hsig[hc % 2][:, :], P[2 * (hc % 2)][:, :], AF.Silu, scale=1.0 / 127.0),
                    waits=wsil, inc="sil", fence=True)
                w = [("mu", tok_u[hc]), ("sil", tok_sil[hc])]
                if t > 0:
                    w.append(("md", prev_md_last))
                tok_h[hc] = sch.emit("vector", lambda h, hc=hc: h.scalar_tensor_tensor(
                    hbuf[hc][:, :], P[2 * (hc % 2) + 1][:, :], 1.0 / 127.0,
                    hsig[hc % 2][:, :], ALU.mult, ALU.mult),
                    waits=w, inc="ehh", fence=True)

            tok_dp, tok_md, tok_s2, tok_q2 = {}, {}, {}, {}
            for oc in range(NOC):
                w = []
                if oc >= 1:
                    w = [("md", tok_md[oc - 1])]
                tok_dp[oc] = sch.emit("gpsimd", lambda h, oc=oc: h.dma_start(
                    out=dpan[oc % 2][:, :],
                    in_=wdt[:, oc * HID:(oc + 1) * HID]),
                    waits=w, inc="ldd", k=16)
                wmm = [("ldd", tok_dp[oc]), ("ehh", tok_h[NHC - 1]),
                       ("sil", tok_sil[NHC - 1])]
                if oc >= 2:
                    wmm.append(("ev2", tok_s2[oc - 2]))
                tok_md[oc] = sch.emit("tensor", lambda h, oc=oc: [
                    h.matmul(P[oc % 2][:, :], dpan[oc % 2][:, hc * 128:(hc + 1) * 128],
                             hbuf[hc][:, :], start=(hc == 0), stop=(hc == NHC - 1))
                    for hc in range(NHC)][-1], waits=wmm, inc="md")

                def mks2(h, oc=oc):
                    h.scalar_tensor_tensor(sfull[oc][:, :], P[oc % 2][:, :],
                                           1.0 / 127.0, ybuf[oc][:, :], ALU.mult, ALU.add)
                    return h.tensor_copy(sbf[oc][:, :], sfull[oc][:, :])
                tok_s2[oc] = sch.emit("vector", mks2,
                                      waits=[("md", tok_md[oc]), ("apwA", apA[oc]),
                                             ("mstA", mstA_now)],
                                      inc="ev2", fence=True)
                tok_q2[oc] = sch.emit("scalar", lambda h, oc=oc: h.activation(
                    s2bf[oc][:, :], sfull[oc][:, :], AF.Square),
                    waits=[("ev2", tok_s2[oc])], inc="sqB", fence=True)
            prev_md_last = tok_md[NOC - 1]
            pre = [("ev2", tok_s2[7]), ("sqB", tok_q2[7])] + last_stat_users
            ap_extra = [[("sto", prev["sto"].get(oc, 0))] for oc in range(NOC)]
            apB = _emit_ln(sch, sbf, s2bf, sfull, onescol, onesrow, Sst, Bb, scratch,
                           x1b[t % 2], None, 1e-5, pre_tokens=pre,
                           bc_waits=last_bc_readers, apply_waits=ap_extra, tag="B")
            last_bc_readers = [("apwB", apB[7])]
            last_stat_users = [("stfB", sch.cnt["stfB"])]
            tok_sto = {}
            for oc in range(NOC):
                tok_sto[oc] = sch.emit("sync", lambda h, oc=oc, t=t: h.dma_start(
                    out=x1[oc * 128:(oc + 1) * 128, t * 512:(t + 1) * 512],
                    in_=x1b[t % 2][oc][:, :]),
                    waits=[("apwB", apB[oc])], inc="sto", k=16)
            prev = {"evsA": tok_s, "mo": tok_mo, "ldx": tok_x, "lda": tok_at,
                    "apwA": apA, "apwB": apB, "ev2": tok_s2, "sto": tok_sto,
                    "mstB": sch.cnt["mstB"]}

        sch.emit("sync", None, waits=[("sto", sch.cnt["sto"])])
        sch.build(block)
    return nc



def build_p5():
    nc = bass.Bass("TRN2", target_bir_lowering=False)
    x1h = nc.dram_tensor("x1h", [D, 2080], F32, kind="ExternalInput").ap()
    x1 = nc.dram_tensor("x1", [D, TOK], F32, kind="ExternalInput").ap()
    w1m = nc.dram_tensor("w1m", [32, 5, 128, 128], F32, kind="ExternalInput").ap()
    w2m = nc.dram_tensor("w2m", [8, 4, 5, 128, 128], F32, kind="ExternalInput").ap()
    b1c = nc.dram_tensor("b1c", [128, 32], F32, kind="ExternalInput").ap()
    b2c = nc.dram_tensor("b2c", [128, 8], F32, kind="ExternalInput").ap()
    g0c = nc.dram_tensor("g0c", [128, 8], F32, kind="ExternalInput").ap()
    b0c = nc.dram_tensor("b0c", [128, 8], F32, kind="ExternalInput").ap()
    emk = nc.dram_tensor("emk", [128, 32], F32, kind="ExternalInput").ap()
    h0 = nc.dram_tensor("h0", [D, TOK], F32, kind="ExternalOutput").ap()

    NST = 8
    es = ExitStack()
    with es:
        w1sb = es.enter_context(nc.sbuf_tensor("w1sb", [128, 32 * 5 * 128], BF16))
        w2sb = es.enter_context(nc.sbuf_tensor("w2sb", [128, 8 * 4 * 5 * 128], BF16))
        b1s = es.enter_context(nc.sbuf_tensor("b1s", [128, 32], F32))
        b2s = es.enter_context(nc.sbuf_tensor("b2s", [128, 8], F32))
        g0s = es.enter_context(nc.sbuf_tensor("g0s", [128, 8], F32))
        b0s = es.enter_context(nc.sbuf_tensor("b0s", [128, 8], F32))
        emks = es.enter_context(nc.sbuf_tensor("emks", [128, 32], BF16))
        xw = [es.enter_context(nc.sbuf_tensor(f"xw{i}", [128, 264], BF16)) for i in range(NOC)]
        x1t = [es.enter_context(nc.sbuf_tensor(f"x1t{i}", [128, 256], F32)) for i in range(NOC)]
        h1s = [es.enter_context(nc.sbuf_tensor(f"h1s{i}", [128, 260], BF16)) for i in range(32)]
        sfull = [es.enter_context(nc.sbuf_tensor(f"sf{i}", [128, 256], F32)) for i in range(NOC)]
        sbf = [es.enter_context(nc.sbuf_tensor(f"sb{i}", [128, 256], BF16)) for i in range(NOC)]
        s2bf = [es.enter_context(nc.sbuf_tensor(f"s2{i}", [128, 256], BF16)) for i in range(NOC)]
        h0b = [[es.enter_context(nc.sbuf_tensor(f"h0b{z}_{i}", [128, 256], F32)) for i in range(NOC)]
               for z in range(2)]
        onescol = es.enter_context(nc.sbuf_tensor("onescol", [128, 1], BF16))
        onesrow = es.enter_context(nc.sbuf_tensor("onesrow", [1, 128], BF16))
        lns = [es.enter_context(nc.sbuf_tensor(f"lns{i}", [1, 256], F32)) for i in range(5)]
        lnb = [es.enter_context(nc.sbuf_tensor(f"lnb{i}", [1, 256], BF16)) for i in range(2)]
        scratch = (lns[0], lns[1], lns[2], lns[3], lns[4], lnb[0], lnb[1])
        C = [es.enter_context(nc.psum_tensor(f"C{i}", [128, 260], F32)) for i in range(2)]
        Dp = [es.enter_context(nc.psum_tensor(f"D{i}", [128, 256], F32)) for i in range(2)]
        Sst = [es.enter_context(nc.psum_tensor(f"Sst{i}", [1, 256], F32)) for i in range(2)]
        Bb = [es.enter_context(nc.psum_tensor(f"Bb{i}", [128, 256], F32)) for i in range(2)]
        sch = Sch(nc, es)
        block = es.enter_context(nc.Block())

        sch.emit("vector", lambda h: h.memset(onescol[:, :], 1.0))
        t_ones = sch.emit("vector", lambda h: h.memset(onesrow[:, :], 1.0), inc="ones", fence=True)
        for a in range(32):
            sch.emit("gpsimd", lambda h, a=a: h.dma_start(
                out=w1sb[:, a * 5 * 128:(a + 1) * 5 * 128],
                in_=w1m[a].rearrange("j p c -> p j c")), inc="ldw", k=16)
        for a in range(8):
            sch.emit("gpsimd", lambda h, a=a: h.dma_start(
                out=w2sb[:, a * 20 * 128:(a + 1) * 20 * 128],
                in_=w2m[a].rearrange("q j p c -> p q j c")), inc="ldw", k=16)
        sch.emit("gpsimd", lambda h: h.dma_start(out=emks[:, :], in_=emk[:, :]), inc="ldw", k=16)
        for src, dst in ((b1c, b1s), (b2c, b2s), (g0c, g0s), (b0c, b0s)):
            sch.emit("sync", lambda h, src=src, dst=dst: h.dma_start(
                out=dst[:, :], in_=src[:, :]), inc="ldw2", k=16)
        t_w = ("ldw", sch.cnt["ldw"])
        t_w2 = ("ldw2", sch.cnt["ldw2"])

        def w1tile(a, j):
            return w1sb[:, (a * 5 + j) * 128:(a * 5 + j + 1) * 128]

        def w2tile(a, q, j):
            return w2sb[:, ((a * 4 + q) * 5 + j) * 128:((a * 4 + q) * 5 + j + 1) * 128]

        prev = {"c1": {}, "gel": {}, "evs": {}, "md": {}, "sto": {}, "xw": {}, "x1t": {}}
        last_bc_readers = [("ones", t_ones)]
        last_stat_users = []
        for stg in range(NST):
            i, st = divmod(stg, 2)
            xoff = i * 520 + st * 256
            toff = stg * 256
            tok_xw, tok_x1t = {}, {}
            for ic in range(NOC):
                w = [("c1", prev["c1"].get(31, 0)), ("gel", prev["gel"].get(31, 0))]
                tok_xw[ic] = sch.emit("gpsimd", lambda h, ic=ic, xoff=xoff: h.dma_start(
                    out=xw[ic][:, :], in_=x1h[ic * 128:(ic + 1) * 128, xoff:xoff + 264]),
                    waits=w, inc="lxw", k=16)
                w = [("evs", prev["evs"].get(ic, 0))]
                tok_x1t[ic] = sch.emit("sync", lambda h, ic=ic, toff=toff: h.dma_start(
                    out=x1t[ic][:, :], in_=x1[ic * 128:(ic + 1) * 128, toff:toff + 256]),
                    waits=w, inc="lx1", k=16)
            tok_c1, tok_gel = {}, {}
            for a in range(32):
                w = [t_w, ("lxw", tok_xw[7])]
                if a >= 2:
                    w.append(("gel", tok_gel[a - 2]))
                elif stg > 0:
                    w.append(("gel", prev["gel"][30 + a]))
                tok_c1[a] = sch.emit("tensor", lambda h, a=a: [
                    h.matmul(C[a % 2][:, :], w1tile(a, j), xw[a // 4][:, j:j + 260],
                             start=(j == 0), stop=(j == 4)) for j in range(5)][-1],
                    waits=w, inc="c1")
                w = [("c1", tok_c1[a]), t_w2]
                if stg > 0:
                    w.append(("md", prev["md"].get(7, 0)))
                tok_gel[a] = sch.emit("scalar", lambda h, a=a: h.activation(
                    h1s[a][:, :], C[a % 2][:, :], AF.Gelu, bias=b1s[:, a:a + 1]),
                    waits=w, inc="gel", fence=True)
            tok_em = {}
            for a in range(32):
                def mkem(h, a=a, stg=stg):
                    h.tensor_tensor(h1s[a][:, 0:2], h1s[a][:, 0:2],
                                    emks[:, stg * 4:stg * 4 + 2], ALU.mult)
                    return h.tensor_tensor(h1s[a][:, 258:260], h1s[a][:, 258:260],
                                           emks[:, stg * 4 + 2:stg * 4 + 4], ALU.mult)
                tok_em[a] = sch.emit("vector", mkem, waits=[("gel", tok_gel[a])],
                                     inc="emkd", fence=True)
            tok_md, tok_s, tok_q = {}, {}, {}
            for oc in range(NOC):
                w = [("gel", tok_gel[4 * oc + 3])] + [("emkd", tok_em[4 * oc + 3])]
                if oc >= 2:
                    w.append(("evs", tok_s[oc - 2]))
                elif stg > 0:
                    w.append(("evs", prev["evs"][oc + 6]))
                tok_md[oc] = sch.emit("tensor", lambda h, oc=oc: [
                    h.matmul(Dp[oc % 2][:, :], w2tile(oc, q, j),
                             h1s[4 * oc + q][:, j:j + 256],
                             start=(q == 0 and j == 0), stop=(q == 3 and j == 4))
                    for q in range(4) for j in range(5)][-1], waits=w, inc="md")

                def mks(h, oc=oc):
                    h.scalar_tensor_tensor(sfull[oc][:, :], Dp[oc % 2][:, :],
                                           b2s[:, oc:oc + 1], x1t[oc][:, :],
                                           ALU.add, ALU.add)
                    return h.tensor_copy(sbf[oc][:, :], sfull[oc][:, :])
                tok_s[oc] = sch.emit("vector", mks,
                                     waits=[("md", tok_md[oc]), ("lx1", tok_x1t[oc]),
                                            ("mstN", prev.get("mstN", 0))],
                                     inc="evs", fence=True)
                tok_q[oc] = sch.emit("scalar", lambda h, oc=oc: h.activation(
                    s2bf[oc][:, :], sfull[oc][:, :], AF.Square),
                    waits=[("evs", tok_s[oc])], inc="sq", fence=True)
            pre = [("evs", tok_s[7]), ("sq", tok_q[7]), ("ones", t_ones)] + last_stat_users
            gcol = [g0s[:, oc:oc + 1] for oc in range(NOC)]
            bcol = [b0s[:, oc:oc + 1] for oc in range(NOC)]
            ap_extra = [[("sto", prev["sto"].get(oc, 0))] for oc in range(NOC)]
            ap = _emit_ln(sch, sbf, s2bf, sfull, onescol, onesrow, Sst, Bb, scratch,
                          h0b[stg % 2], None, 1e-12, gcol=gcol, bcol=bcol,
                          pre_tokens=pre, bc_waits=last_bc_readers,
                          apply_waits=ap_extra, tag="N")
            last_bc_readers = [("apwN", ap[7])]
            last_stat_users = [("stfN", sch.cnt["stfN"])]
            tok_sto = {}
            for oc in range(NOC):
                tok_sto[oc] = sch.emit("sync", lambda h, oc=oc, toff=toff, stg=stg: h.dma_start(
                    out=h0[oc * 128:(oc + 1) * 128, toff:toff + 256],
                    in_=h0b[stg % 2][oc][:, :]),
                    waits=[("apwN", ap[oc])], inc="sto", k=16)
            prev = {"c1": tok_c1, "gel": tok_gel, "evs": tok_s, "md": tok_md,
                    "sto": tok_sto, "xw": tok_xw, "x1t": tok_x1t,
                    "mstN": sch.cnt["mstN"]}

        sch.emit("sync", None, waits=[("sto", sch.cnt["sto"])])
        sch.build(block)
    return nc


def _conv_mats(conv1_w, conv2_w):
    w1m = np.zeros((32, 5, 128, 128), np.float32)
    for a in range(32):
        ic = a // 4
        for j in range(5):
            for ol in range(128):
                o = a * 128 + ol
                cin_local = o // 4 - 128 * ic
                w1m[a, j, cin_local, ol] = conv1_w[o, 0, j]
    w2m = np.zeros((8, 4, 5, 128, 128), np.float32)
    for a in range(8):
        for q in range(4):
            hc = 4 * a + q
            for j in range(5):
                for p in range(128):
                    c_local = 32 * q + p // 4
                    m = p % 4
                    w2m[a, q, j, p, c_local] = conv2_w[a * 128 + c_local, m, j]
    return w1m, w2m


def _stage_halo(x1full):
    out = []
    xp = np.zeros((B, S + 8, D), np.float32)
    xp[:, 4:S + 4, :] = x1full
    for j in range(NCORES):
        cols = []
        for (b, c) in _chunks_of(j):
            cols.append(np.ascontiguousarray(xp[b, c * CH:c * CH + 520, :].T))
        out.append(np.ascontiguousarray(np.concatenate(cols, axis=1)))
    return out



def build_p7():
    nc = bass.Bass("TRN2", target_bir_lowering=False)
    hin = nc.dram_tensor("hin", [D, TOK], F32, kind="ExternalInput").ap()
    x1 = nc.dram_tensor("x1", [D, TOK], F32, kind="ExternalInput").ap()
    w1 = nc.dram_tensor("w1", [128, 32768], F32, kind="ExternalInput").ap()
    w2 = nc.dram_tensor("w2", [128, 32768], F32, kind="ExternalInput").ap()
    b1r = nc.dram_tensor("b1r", [128, 32], F32, kind="ExternalInput").ap()
    b2r = nc.dram_tensor("b2r", [128, 8], F32, kind="ExternalInput").ap()
    g2r = nc.dram_tensor("g2r", [128, 8], F32, kind="ExternalInput").ap()
    bb2r = nc.dram_tensor("bb2r", [128, 8], F32, kind="ExternalInput").ap()
    x2 = nc.dram_tensor("x2", [D, TOK], F32, kind="ExternalOutput").ap()

    NH1 = 32
    es = ExitStack()
    with es:
        ht = [es.enter_context(nc.sbuf_tensor(f"ht{i}", [128, 512], F32)) for i in range(NOC)]
        hbf = [es.enter_context(nc.sbuf_tensor(f"hbf{i}", [128, 512], BF16)) for i in range(NOC)]
        x1t = [es.enter_context(nc.sbuf_tensor(f"x1t{i}", [128, 512], F32)) for i in range(NOC)]
        f1b = [es.enter_context(nc.sbuf_tensor(f"f1b{i}", [128, 512], BF16)) for i in range(NH1)]
        sfull = [es.enter_context(nc.sbuf_tensor(f"sf{i}", [128, 512], F32)) for i in range(NOC)]
        sbf = [es.enter_context(nc.sbuf_tensor(f"sb{i}", [128, 512], BF16)) for i in range(NOC)]
        s2bf = [es.enter_context(nc.sbuf_tensor(f"s2{i}", [128, 512], BF16)) for i in range(NOC)]
        ybuf = [es.enter_context(nc.sbuf_tensor(f"yb{i}", [128, 512], F32)) for i in range(NOC)]
        w1p = [es.enter_context(nc.sbuf_tensor(f"w1p{i}", [128, D], BF16)) for i in range(2)]
        w2p = [es.enter_context(nc.sbuf_tensor(f"w2p{i}", [128, 4096], BF16)) for i in range(2)]
        x2b = [[es.enter_context(nc.sbuf_tensor(f"x2b{z}_{i}", [128, 512], F32)) for i in range(NOC)]
               for z in range(2)]
        b1s = es.enter_context(nc.sbuf_tensor("b1s", [128, 32], F32))
        b2s = es.enter_context(nc.sbuf_tensor("b2s", [128, 8], F32))
        g2s = es.enter_context(nc.sbuf_tensor("g2s", [128, 8], F32))
        bb2s = es.enter_context(nc.sbuf_tensor("bb2s", [128, 8], F32))
        onescol = es.enter_context(nc.sbuf_tensor("onescol", [128, 1], BF16))
        onesrow = es.enter_context(nc.sbuf_tensor("onesrow", [1, 128], BF16))
        lns = [es.enter_context(nc.sbuf_tensor(f"lns{i}", [1, 512], F32)) for i in range(5)]
        lnb = [es.enter_context(nc.sbuf_tensor(f"lnb{i}", [1, 512], BF16)) for i in range(2)]
        scratch = (lns[0], lns[1], lns[2], lns[3], lns[4], lnb[0], lnb[1])
        P = [es.enter_context(nc.psum_tensor(f"P{i}", [128, 512], F32)) for i in range(4)]
        Sst = [es.enter_context(nc.psum_tensor(f"Sst{i}", [1, 512], F32)) for i in range(2)]
        Bb = [es.enter_context(nc.psum_tensor(f"Bb{i}", [128, 512], F32)) for i in range(2)]
        sch = Sch(nc, es)
        block = es.enter_context(nc.Block())

        sch.emit("vector", lambda h: h.memset(onescol[:, :], 1.0))
        t_ones = sch.emit("vector", lambda h: h.memset(onesrow[:, :], 1.0), inc="ones", fence=True)
        for src, dst in ((b1r, b1s), (b2r, b2s), (g2r, g2s), (bb2r, bb2s)):
            sch.emit("sync", lambda h, src=src, dst=dst: h.dma_start(
                out=dst[:, :], in_=src[:, :]), inc="ldc", k=16)
        t_c = ("ldc", sch.cnt["ldc"])

        prev = {"mf": {}, "gelf": {}, "evs": {}, "m2": {}, "sC": {}, "sD": {},
                "sto": {}, "lh": {}, "lx": {}}
        last_bc_readers = [("ones", t_ones)]
        last_stat_users = []
        for t in range(4):
            tok_lh, tok_lhb, tok_lx = {}, {}, {}
            for ic in range(NOC):
                w = [("sC", prev["sC"].get(ic, 0))]
                tok_lh[ic] = sch.emit("sync", lambda h, ic=ic, t=t: h.dma_start(
                    out=ht[ic][:, :], in_=hin[ic * 128:(ic + 1) * 128, t * 512:(t + 1) * 512]),
                    waits=w, inc="lh", k=16)
                w = [("mf", prev["mf"].get(31, 0)), ("m2", prev["m2"].get(7, 0)),
                     ("gelf", prev["gelf"].get(31, 0))]
                tok_lhb[ic] = sch.emit("gpsimd", lambda h, ic=ic, t=t: h.dma_start(
                    out=hbf[ic][:, :], in_=hin[ic * 128:(ic + 1) * 128, t * 512:(t + 1) * 512]),
                    waits=w, inc="lhb", k=16)
                w = [("sD", prev["sD"].get(ic, 0))]
                tok_lx[ic] = sch.emit("sync", lambda h, ic=ic, t=t: h.dma_start(
                    out=x1t[ic][:, :], in_=x1[ic * 128:(ic + 1) * 128, t * 512:(t + 1) * 512]),
                    waits=w, inc="lx", k=16)
            tok_mf, tok_gf = {}, {}
            for hc in range(NH1):
                w = []
                if hc >= 1:
                    w = [("mf", tok_mf[hc - 1])]
                tok_p = sch.emit("gpsimd", lambda h, hc=hc: h.dma_start(
                    out=w1p[hc % 2][:, :],
                    in_=w1[:, hc * 1024:(hc + 1) * 1024]),
                    waits=w, inc="lw1", k=16)
                wmm = [("lw1", tok_p), ("lhb", tok_lhb[7])]
                if hc >= 2:
                    wmm.append(("gelf", tok_gf[hc - 2]))
                tok_mf[hc] = sch.emit("tensor", lambda h, hc=hc: [
                    h.matmul(P[hc % 2][:, :], w1p[hc % 2][:, ic * 128:(ic + 1) * 128],
                             hbf[ic][:, :], start=(ic == 0), stop=(ic == NOC - 1))
                    for ic in range(NOC)][-1], waits=wmm, inc="mf")
                w = [("mf", tok_mf[hc]), t_c]
                if t > 0:
                    w.append(("m2", prev["m2"].get(7, 0)))
                tok_gf[hc] = sch.emit("scalar", lambda h, hc=hc: h.activation(
                    f1b[hc][:, :], P[hc % 2][:, :], AF.Gelu, bias=b1s[:, hc:hc + 1]),
                    waits=w, inc="gelf", fence=True)
            tok_m2, tok_sC, tok_qC = {}, {}, {}
            for oc in range(NOC):
                w = []
                if oc >= 1:
                    w = [("m2", tok_m2[oc - 1])]
                tok_p = sch.emit("gpsimd", lambda h, oc=oc: h.dma_start(
                    out=w2p[oc % 2][:, :],
                    in_=w2[:, oc * 4096:(oc + 1) * 4096]),
                    waits=w, inc="lw2", k=16)
                wmm = [("lw2", tok_p), ("gelf", tok_gf[NH1 - 1])]
                if oc >= 2:
                    wmm.append(("sC", tok_sC[oc - 2]))
                tok_m2[oc] = sch.emit("tensor", lambda h, oc=oc: [
                    h.matmul(P[2 + oc % 2][:, :], w2p[oc % 2][:, hc * 128:(hc + 1) * 128],
                             f1b[hc][:, :], start=(hc == 0), stop=(hc == NH1 - 1))
                    for hc in range(NH1)][-1], waits=wmm, inc="m2")

                def mksC(h, oc=oc):
                    h.scalar_tensor_tensor(sfull[oc][:, :], P[2 + oc % 2][:, :],
                                           b2s[:, oc:oc + 1], ht[oc][:, :],
                                           ALU.add, ALU.add)
                    return h.tensor_copy(sbf[oc][:, :], sfull[oc][:, :])
                tok_sC[oc] = sch.emit("vector", mksC,
                                      waits=[("m2", tok_m2[oc]), ("lh", tok_lh[oc]),
                                             ("mstDd", prev.get("mstDd", 0))],
                                      inc="sC", fence=True)
                tok_qC[oc] = sch.emit("scalar", lambda h, oc=oc: h.activation(
                    s2bf[oc][:, :], sfull[oc][:, :], AF.Square),
                    waits=[("sC", tok_sC[oc])], inc="qC", fence=True)
            pre = [("sC", tok_sC[7]), ("qC", tok_qC[7]), ("ones", t_ones)] + last_stat_users
            gcol = [g2s[:, oc:oc + 1] for oc in range(NOC)]
            bcol = [bb2s[:, oc:oc + 1] for oc in range(NOC)]
            apC = _emit_ln(sch, sbf, s2bf, sfull, onescol, onesrow, Sst, Bb, scratch,
                           ybuf, None, 1e-12, gcol=gcol, bcol=bcol, pre_tokens=pre,
                           bc_waits=last_bc_readers, tag="C")
            last_bc_readers = [("apwC", apC[7])]
            last_stat_users = [("stfC", sch.cnt["stfC"])]
            mstC_now = sch.cnt["mstC"]
            tok_sD, tok_qD = {}, {}
            for oc in range(NOC):
                def mksD(h, oc=oc):
                    h.tensor_add(sfull[oc][:, :], ybuf[oc][:, :], x1t[oc][:, :])
                    return h.tensor_copy(sbf[oc][:, :], sfull[oc][:, :])
                tok_sD[oc] = sch.emit("vector", mksD,
                                      waits=[("apwC", apC[oc]), ("lx", tok_lx[oc]),
                                             ("mstC", mstC_now)],
                                      inc="sD", fence=True)
                tok_qD[oc] = sch.emit("scalar", lambda h, oc=oc: h.activation(
                    s2bf[oc][:, :], sfull[oc][:, :], AF.Square),
                    waits=[("sD", tok_sD[oc])], inc="qD", fence=True)
            pre = [("sD", tok_sD[7]), ("qD", tok_qD[7])] + last_stat_users
            ap_extra = [[("sto", prev["sto"].get(oc, 0))] for oc in range(NOC)]
            apD = _emit_ln(sch, sbf, s2bf, sfull, onescol, onesrow, Sst, Bb, scratch,
                           x2b[t % 2], None, 1e-5, pre_tokens=pre,
                           bc_waits=last_bc_readers, apply_waits=ap_extra, tag="Dd")
            last_bc_readers = [("apwDd", apD[7])]
            last_stat_users = [("stfDd", sch.cnt["stfDd"])]
            tok_sto = {}
            for oc in range(NOC):
                tok_sto[oc] = sch.emit("sync", lambda h, oc=oc, t=t: h.dma_start(
                    out=x2[oc * 128:(oc + 1) * 128, t * 512:(t + 1) * 512],
                    in_=x2b[t % 2][oc][:, :]),
                    waits=[("apwDd", apD[oc])], inc="sto", k=16)
            prev = {"mf": tok_mf, "gelf": tok_gf, "m2": tok_m2, "sC": tok_sC,
                    "sD": tok_sD, "sto": tok_sto, "lh": tok_lh, "lx": tok_lx,
                    "evs": {}, "mstDd": sch.cnt["mstDd"]}

        sch.emit("sync", None, waits=[("sto", sch.cnt["sto"])])
        sch.build(block)
    return nc



def build_p8():
    nc = bass.Bass("TRN2", target_bir_lowering=False)
    x2 = nc.dram_tensor("x2", [D, TOK], F32, kind="ExternalInput").ap()
    wst = nc.dram_tensor("wst", [D, D], F32, kind="ExternalInput").ap()
    out = nc.dram_tensor("out", [D, TOK], F32, kind="ExternalOutput").ap()

    es = ExitStack()
    with es:
        wraw = [es.enter_context(nc.sbuf_tensor(f"wr{i}", [128, D], BF16)) for i in range(NOC)]
        wsg = [es.enter_context(nc.sbuf_tensor(f"ws{i}", [128, D], BF16)) for i in range(NOC)]
        x2t = [es.enter_context(nc.sbuf_tensor(f"x2t{i}", [128, 512], F32)) for i in range(NOC)]
        x2bf = [es.enter_context(nc.sbuf_tensor(f"x2f{i}", [128, 512], BF16)) for i in range(NOC)]
        sfull = [es.enter_context(nc.sbuf_tensor(f"sf{i}", [128, 512], F32)) for i in range(NOC)]
        sbf = [es.enter_context(nc.sbuf_tensor(f"sb{i}", [128, 512], BF16)) for i in range(NOC)]
        s2bf = [es.enter_context(nc.sbuf_tensor(f"s2{i}", [128, 512], BF16)) for i in range(NOC)]
        ob = [[es.enter_context(nc.sbuf_tensor(f"ob{z}_{i}", [128, 512], F32)) for i in range(NOC)]
              for z in range(2)]
        onescol = es.enter_context(nc.sbuf_tensor("onescol", [128, 1], BF16))
        onesrow = es.enter_context(nc.sbuf_tensor("onesrow", [1, 128], BF16))
        lns = [es.enter_context(nc.sbuf_tensor(f"lns{i}", [1, 512], F32)) for i in range(5)]
        lnb = [es.enter_context(nc.sbuf_tensor(f"lnb{i}", [1, 512], BF16)) for i in range(2)]
        scratch = (lns[0], lns[1], lns[2], lns[3], lns[4], lnb[0], lnb[1])
        P = [es.enter_context(nc.psum_tensor(f"P{i}", [128, 512], F32)) for i in range(2)]
        Sst = [es.enter_context(nc.psum_tensor(f"Sst{i}", [1, 512], F32)) for i in range(2)]
        Bb = [es.enter_context(nc.psum_tensor(f"Bb{i}", [128, 512], F32)) for i in range(2)]
        sch = Sch(nc, es)
        block = es.enter_context(nc.Block())

        sch.emit("vector", lambda h: h.memset(onescol[:, :], 1.0))
        t_ones = sch.emit("vector", lambda h: h.memset(onesrow[:, :], 1.0), inc="ones", fence=True)
        tok_lw = {}
        for i in range(NOC):
            tok_lw[i] = sch.emit("gpsimd", lambda h, i=i: h.dma_start(
                out=wraw[i][:, :], in_=wst[i * 128:(i + 1) * 128, :]), inc="lw", k=16)
        tok_sg = {}
        for i in range(NOC):
            tok_sg[i] = sch.emit("scalar", lambda h, i=i: h.activation(
                wsg[i][:, :], wraw[i][:, :], AF.Sign),
                waits=[("lw", tok_lw[i])], inc="sg", fence=True)

        prev = {"evs": {}, "mm": {}, "sto": {}, "lx": {}}
        last_bc_readers = [("ones", t_ones)]
        last_stat_users = []
        for t in range(4):
            tok_lx, tok_lxb = {}, {}
            for ic in range(NOC):
                w = [("evs", prev["evs"].get(ic, 0))]
                tok_lx[ic] = sch.emit("sync", lambda h, ic=ic, t=t: h.dma_start(
                    out=x2t[ic][:, :], in_=x2[ic * 128:(ic + 1) * 128, t * 512:(t + 1) * 512]),
                    waits=w, inc="lx", k=16)
                w = [("mm", prev["mm"].get(7, 0)), ("evs", prev["evs"].get(7, 0))]
                tok_lxb[ic] = sch.emit("gpsimd", lambda h, ic=ic, t=t: h.dma_start(
                    out=x2bf[ic][:, :], in_=x2[ic * 128:(ic + 1) * 128, t * 512:(t + 1) * 512]),
                    waits=w, inc="lxb", k=16)
            tok_mm, tok_s, tok_q = {}, {}, {}
            for oc in range(NOC):
                w = [("sg", tok_sg[NOC - 1]), ("lxb", tok_lxb[7])]
                if oc >= 2:
                    w.append(("evs", tok_s[oc - 2]))
                elif t > 0:
                    w.append(("evs", prev["evs"][oc + 6]))
                tok_mm[oc] = sch.emit("tensor", lambda h, oc=oc: [
                    h.matmul(P[oc % 2][:, :], wsg[ic][:, oc * 128:(oc + 1) * 128],
                             x2bf[ic][:, :], start=(ic == 0), stop=(ic == NOC - 1))
                    for ic in range(NOC)][-1], waits=w, inc="mm")

                def mks(h, oc=oc):
                    h.tensor_add(sfull[oc][:, :], x2t[oc][:, :], P[oc % 2][:, :])
                    return h.tensor_copy(sbf[oc][:, :], sfull[oc][:, :])
                tok_s[oc] = sch.emit("vector", mks,
                                     waits=[("mm", tok_mm[oc]), ("lx", tok_lx[oc]),
                                            ("mstF", prev.get("mstF", 0))],
                                     inc="evs", fence=True)
                tok_q[oc] = sch.emit("scalar", lambda h, oc=oc: h.activation(
                    s2bf[oc][:, :], sfull[oc][:, :], AF.Square),
                    waits=[("evs", tok_s[oc])], inc="sq", fence=True)
            pre = [("evs", tok_s[7]), ("sq", tok_q[7]), ("ones", t_ones)] + last_stat_users
            ap_extra = [[("sto", prev["sto"].get(oc, 0))] for oc in range(NOC)]
            ap = _emit_ln(sch, sbf, s2bf, sfull, onescol, onesrow, Sst, Bb, scratch,
                          ob[t % 2], None, 1e-5, pre_tokens=pre,
                          bc_waits=last_bc_readers, apply_waits=ap_extra, tag="F")
            last_bc_readers = [("apwF", ap[7])]
            last_stat_users = [("stfF", sch.cnt["stfF"])]
            tok_sto = {}
            for oc in range(NOC):
                tok_sto[oc] = sch.emit("sync", lambda h, oc=oc, t=t: h.dma_start(
                    out=out[oc * 128:(oc + 1) * 128, t * 512:(t + 1) * 512],
                    in_=ob[t % 2][oc][:, :]),
                    waits=[("apwF", ap[oc])], inc="sto", k=16)
            prev = {"evs": tok_s, "mm": tok_mm, "sto": tok_sto, "lx": tok_lx,
                    "mstF": sch.cnt["mstF"]}

        sch.emit("sync", None, waits=[("sto", sch.cnt["sto"])])
        sch.build(block)
    return nc


def build_p6(mode):
    first, mid, last = mode == "first", mode == "mid", mode == "last"
    do_fin = mid or last
    do_proj = first or mid
    nc = bass.Bass("TRN2", target_bir_lowering=False)
    hin = nc.dram_tensor("hin", [D, TOK], F32, kind="ExternalInput").ap()
    if do_fin:
        parts = nc.dram_tensor("parts", [8, 128, 1040], F32, kind="ExternalInput").ap()
        qp_i = nc.dram_tensor("qp_i", [D, TOK], BF16, kind="ExternalInput").ap()
        qr_i = nc.dram_tensor("qr_i", [D, TOK], BF16, kind="ExternalInput").ap()
        lnag = nc.dram_tensor("lnag", [128, 8], F32, kind="ExternalInput").ap()
        lnab = nc.dram_tensor("lnab", [128, 8], F32, kind="ExternalInput").ap()
        e2 = nc.dram_tensor("e2", [2, 128], F32, kind="ExternalInput").ap()
        hn = nc.dram_tensor("hn", [D, TOK], F32, kind="ExternalOutput").ap()
    if do_proj:
        wts = {w: nc.dram_tensor("w" + w, [D, D], F32, kind="ExternalInput").ap()
               for w in "qkv"}
        bcols = {w: nc.dram_tensor("b" + w, [128, 8], F32, kind="ExternalInput").ap()
                 for w in "qkv"}
        cosb = nc.dram_tensor("cosb", [D, TOK], BF16, kind="ExternalInput").ap()
        sinb = nc.dram_tensor("sinb", [D, TOK], BF16, kind="ExternalInput").ap()
        rmat = nc.dram_tensor("rmat", [128, 128], F32, kind="ExternalInput").ap()
        ident = nc.dram_tensor("ident", [128, 128], F32, kind="ExternalInput").ap()
        qp_o = nc.dram_tensor("qp_o", [D, TOK], BF16, kind="ExternalOutput").ap()
        qr_o = nc.dram_tensor("qr_o", [D, TOK], BF16, kind="ExternalOutput").ap()
        part_o = nc.dram_tensor("part_o", [128, 1040], F32, kind="ExternalOutput").ap()

    es = ExitStack()
    with es:
        ht = [es.enter_context(nc.sbuf_tensor(f"ht{i}", [128, 512], F32)) for i in range(NOC)]
        if do_fin:
            psb = [es.enter_context(nc.sbuf_tensor("psb0", [128, 1040], F32))]
            ptot = es.enter_context(nc.sbuf_tensor("ptot", [128, 1040], F32))
            kvbf = es.enter_context(nc.sbuf_tensor("kvbf", [128, 1024], BF16))
            kmbf = es.enter_context(nc.sbuf_tensor("kmbf", [128, 16], BF16))
            km2 = es.enter_context(nc.sbuf_tensor("km2", [128, 32], BF16))
            qpt = [es.enter_context(nc.sbuf_tensor(f"qpt{i}", [128, 512], BF16)) for i in range(2)]
            zden = es.enter_context(nc.sbuf_tensor("zden", [2, 4096], F32))
            zfb = es.enter_context(nc.sbuf_tensor("zfb", [2, 4096], BF16))
            e2s = es.enter_context(nc.sbuf_tensor("e2s", [2, 128], BF16))
            ln8c = es.enter_context(nc.sbuf_tensor("ln8c", [2, 1], F32))
            qrt = [es.enter_context(nc.sbuf_tensor(f"qrt{i}", [128, 512], BF16)) for i in range(2)]
            zbsb = es.enter_context(nc.sbuf_tensor("zbsb", [128, 512], F32))
            sfull = [es.enter_context(nc.sbuf_tensor(f"sf{i}", [128, 512], F32)) for i in range(NOC)]
            sbf = [es.enter_context(nc.sbuf_tensor(f"sb{i}", [128, 512], BF16)) for i in range(NOC)]
            s2bf = [es.enter_context(nc.sbuf_tensor(f"s2{i}", [128, 512], BF16)) for i in range(NOC)]
            hnew = [es.enter_context(nc.sbuf_tensor(f"hw{i}", [128, 512], F32)) for i in range(NOC)]
            gs = es.enter_context(nc.sbuf_tensor("gs", [128, 8], F32))
            bs = es.enter_context(nc.sbuf_tensor("bs", [128, 8], F32))
            onescol = es.enter_context(nc.sbuf_tensor("onescol", [128, 1], BF16))
            onesrow = es.enter_context(nc.sbuf_tensor("onesrow", [1, 128], BF16))
            lns = [es.enter_context(nc.sbuf_tensor(f"lns{i}", [1, 512], F32)) for i in range(5)]
            lnb = [es.enter_context(nc.sbuf_tensor(f"lnb{i}", [1, 512], BF16)) for i in range(2)]
            scratch = (lns[0], lns[1], lns[2], lns[3], lns[4], lnb[0], lnb[1])
        if do_proj:
            hbnew = [es.enter_context(nc.sbuf_tensor(f"hf{i}", [128, 512], BF16)) for i in range(NOC)]
            wsb = {w: [es.enter_context(nc.sbuf_tensor(f"w{w}{i}", [128, D], BF16)) for i in range(NOC)]
                   for w in "qkv"}
            bsb = {w: es.enter_context(nc.sbuf_tensor(f"bc{w}", [128, 8], F32)) for w in "qkv"}
            coss = [es.enter_context(nc.sbuf_tensor(f"cs{i}", [128, 512], BF16)) for i in range(2)]
            sins = [es.enter_context(nc.sbuf_tensor(f"sn{i}", [128, 512], BF16)) for i in range(2)]
            rms = es.enter_context(nc.sbuf_tensor("rms", [128, 128], BF16))
            idn = es.enter_context(nc.sbuf_tensor("idn", [128, 128], BF16))
            qpf = [es.enter_context(nc.sbuf_tensor(f"qpf{i}", [128, 512], F32)) for i in range(2)]
            kpf = [es.enter_context(nc.sbuf_tensor(f"kpf{i}", [128, 512], F32)) for i in range(2)]
            qpb = [es.enter_context(nc.sbuf_tensor(f"qpb{i}", [128, 512], BF16)) for i in range(2)]
            kpb = [es.enter_context(nc.sbuf_tensor(f"kpb{i}", [128, 512], BF16)) for i in range(2)]
            qcosf = es.enter_context(nc.sbuf_tensor("qcosf", [128, 512], F32))
            rsinf = es.enter_context(nc.sbuf_tensor("rsinf", [128, 512], F32))
            qrb = [es.enter_context(nc.sbuf_tensor(f"qrb{i}", [128, 512], BF16)) for i in range(2)]
            krb = [es.enter_context(nc.sbuf_tensor(f"krb{i}", [128, 512], BF16)) for i in range(2)]
            vbt = [es.enter_context(nc.sbuf_tensor(f"vbt{i}", [128, 512], BF16)) for i in range(2)]
            krtok = [es.enter_context(nc.sbuf_tensor(f"krt{i}", [128, 128], BF16)) for i in range(4)]
            vtok = [es.enter_context(nc.sbuf_tensor(f"vt{i}", [128, 128], BF16)) for i in range(4)]
            kvsb = es.enter_context(nc.sbuf_tensor("kvsb", [128, 1024], F32))
            kmtmp = es.enter_context(nc.sbuf_tensor("kmtmp", [128, 32], F32))
            km16 = es.enter_context(nc.sbuf_tensor("km16", [128, 16], F32))
        PP = [es.enter_context(nc.psum_tensor(f"PP{i}", [128, 512], F32)) for i in range(2)]
        OZ = [es.enter_context(nc.psum_tensor(f"OZ{i}", [128, 512], F32)) for i in range(2)]
        if do_proj:
            KV = [es.enter_context(nc.psum_tensor(f"KV{i}", [128, 512], F32)) for i in range(2)]
            RQ = es.enter_context(nc.psum_tensor("RQ", [128, 512], F32))
            RK = es.enter_context(nc.psum_tensor("RK", [128, 512], F32))
        sch = Sch(nc, es)
        C = sch.cnt
        block = es.enter_context(nc.Block())

        t_ones = 0
        if do_fin:
            sch.emit("vector", lambda h: h.memset(onescol[:, :], 1.0))
            sch.emit("vector", lambda h: h.memset(ln8c[:, :], -float(np.log(8192.0))))
            t_ones = sch.emit("vector", lambda h: h.memset(onesrow[:, :], 1.0), inc="ones", fence=True)
            for s_, d_ in ((lnag, gs), (lnab, bs)):
                sch.emit("sync", lambda h, s_=s_, d_=d_: h.dma_start(out=d_[:, :], in_=s_[:, :]),
                         inc="ldc", k=16)
            sch.emit("gpsimd", lambda h: h.dma_start(out=e2s[:, :], in_=e2[:, :]), inc="ldc2", k=16)
        if do_proj:
            for w in "qkv":
                for i in range(NOC):
                    sch.emit("gpsimd", lambda h, w=w, i=i: h.dma_start(
                        out=wsb[w][i][:, :], in_=wts[w][i * 128:(i + 1) * 128, :]),
                        inc="ldw", k=16)
                sch.emit("sync", lambda h, w=w: h.dma_start(
                    out=bsb[w][:, :], in_=bcols[w][:, :]), inc="ldc3", k=16)
            sch.emit("gpsimd", lambda h: h.dma_start(out=rms[:, :], in_=rmat[:, :]), inc="ldw", k=16)
            sch.emit("gpsimd", lambda h: h.dma_start(out=idn[:, :], in_=ident[:, :]), inc="ldw", k=16)
            sch.emit("vector", lambda h: h.memset(kvsb[:, :], 0.0))
            sch.emit("vector", lambda h: h.memset(km16[:, :], 0.0), inc="z0", fence=True)

        t_zf = 0
        t_kv = 0
        if do_fin:
            for r in range(8):
                tpl = sch.emit("sync", lambda h, r=r: h.dma_start(
                    out=psb[0][:, :], in_=parts[r]),
                    waits=[("psum8", r)], inc="ldp", k=16)
                if r == 0:
                    sch.emit("vector", lambda h: h.tensor_copy(ptot[:, :], psb[0][:, :]),
                             waits=[("ldp", tpl)], inc="psum8", fence=True)
                else:
                    sch.emit("vector", lambda h, r=r: h.tensor_add(
                        ptot[:, :], ptot[:, :], psb[0][:, :]),
                        waits=[("ldp", tpl)], inc="psum8", fence=True)

            def mkkv(h):
                h.tensor_copy(kvbf[:, :], ptot[:, 0:1024])
                h.tensor_copy(kmbf[:, :], ptot[:, 1024:1040])
                return h.memset(km2[:, :], 0.0)
            t_kv = sch.emit("vector", mkkv, waits=[("psum8", C["psum8"])], inc="kvr", fence=True)

            def mkkm2(h):
                for b_ in range(2):
                    for oc in range(NOC):
                        c = b_ * 8 + oc
                        h.tensor_copy(km2[0:64, 2 * c:2 * c + 1], kmbf[0:64, c:c + 1])
                        inst = h.tensor_copy(km2[64:128, 2 * c + 1:2 * c + 2], kmbf[64:128, c:c + 1])
                return inst
            t_km2 = sch.emit("vector", mkkm2, waits=[("kvr", t_kv)], inc="km2d", fence=True)


        prev = {}
        pst = {"m": 0, "gel": {}}
        last_bc_readers = [("ones", t_ones)]
        last_stat_users = []
        for tt in range(4):
            b = tt // 2
            tok_lh = {}
            for ic in range(NOC):
                if do_fin:
                    w = [("sA", prev.get("sA", {}).get(ic, 0))]
                else:
                    w = [("hbc", prev.get("hbc", {}).get(ic, 0))]
                tok_lh[ic] = sch.emit("sync", lambda h, ic=ic, tt=tt: h.dma_start(
                    out=ht[ic][:, :], in_=hin[ic * 128:(ic + 1) * 128, tt * 512:(tt + 1) * 512]),
                    waits=w, inc="lh", k=16)
            if do_fin:
                tok_zmm = {}
                for oc in range(NOC):
                    tqp = sch.emit("sync", lambda h, oc=oc, tt=tt: h.dma_start(
                        out=qpt[oc % 2][:, :],
                        in_=qp_i[oc * 128:(oc + 1) * 128, tt * 512:(tt + 1) * 512]),
                        waits=[("zmm", max(0, C.get("zmm", 0) - 1))], inc="ldq", k=16)
                    c = b * 8 + oc
                    wz = [("km2d", t_km2), ("ldq", tqp),
                          ("zev", max(0, C.get("zev", 0) - 1)),
                          ("gvd", prev.get("gvd", 0)), ("ropk", prev.get("ropk", 0)),
                          ("apwA", prev.get("apwA", 0))]
                    tok_zmm[oc] = sch.emit("tensor", lambda h, c=c, oc=oc: h.matmul(
                        PP[oc % 2][0:2, :], km2[:, 2 * c:2 * c + 2],
                        qpt[oc % 2][:, :], start=True, stop=True),
                        waits=wz, inc="zmm")
                    sch.emit("vector", lambda h, oc=oc: h.tensor_copy(
                        zden[0:2, oc * 512:(oc + 1) * 512], PP[oc % 2][0:2, :]),
                        waits=[("zmm", tok_zmm[oc]), ("zbm", prev.get("zbm", 0))],
                        inc="zev", fence=True)

                t_zs = sch.emit("vector", lambda h: h.tensor_scalar(
                    zden[:, :], zden[:, :], 1.0 / 8192.0, 1e-6, ALU.mult, ALU.add),
                    waits=[("zev", C["zev"])], inc="zsc", fence=True)

                def mkzf(h):
                    h.activation(zden[:, :], zden[:, :], AF.Ln)
                    return h.activation(zfb[:, :], zden[:, :], AF.Exp,
                                        scale=-1.0, bias=ln8c[0:2, 0:1])
                t_zf = sch.emit("scalar", mkzf,
                                waits=[("zsc", t_zs), ("ones", t_ones)],
                                inc="zfd", fence=True)

                tok_ao = {}
                for oc in range(NOC):
                    tq = sch.emit("sync", lambda h, oc=oc, tt=tt: h.dma_start(
                        out=qrt[oc % 2][:, :],
                        in_=qr_i[oc * 128:(oc + 1) * 128, tt * 512:(tt + 1) * 512]),
                        waits=[("omm", tok_ao.get(oc - 2, 0) and 0) if False else
                               ("omm", max(0, C.get("omm", 0) - 1))], inc="lqr", k=16)
                    woz = [("stfA", prev.get("stfA", 0)),
                           ("trev", prev.get("trev", 0))]
                    if oc >= 2:
                        woz.append(("aoe", tok_ao[oc - 2]))
                    else:
                        woz.append(("aoe", prev.get("aoe", {}).get(oc + 6, 0)))

                    def mko(h, oc=oc, b=b):
                        h.matmul(OZ[oc % 2][0:64, :],
                                 kvbf[0:64, b * 512 + 64 * oc:b * 512 + 64 * oc + 64],
                                 qrt[oc % 2][0:64, :], start=True, stop=True)
                        return h.matmul(OZ[oc % 2][64:128, :],
                                        kvbf[64:128, b * 512 + 64 * oc:b * 512 + 64 * oc + 64],
                                        qrt[oc % 2][64:128, :], start=True, stop=True)
                    t_omm = sch.emit("tensor", mko,
                                     waits=[("kvr", t_kv), ("lqr", tq), ("zfd", t_zf)] + woz,
                                     inc="omm")
                    wzb = [("ldc2", 16), ("zfd", t_zf),
                           ("gvd", prev.get("gvd", 0)), ("ropk", prev.get("ropk", 0)),
                           ("apwA", prev.get("apwA", 0))]
                    if oc >= 2:
                        wzb.append(("aoe", tok_ao[oc - 2]))
                    t_zb = sch.emit("tensor", lambda h, oc=oc: h.matmul(
                        PP[oc % 2][:, :], e2s[:, :], zfb[0:2, oc * 512:(oc + 1) * 512],
                        start=True, stop=True), waits=wzb, inc="zbm")

                    def mkao(h, oc=oc):
                        h.tensor_copy(zbsb[:, :], PP[oc % 2][:, :])
                        return h.tensor_mul(sfull[oc][:, :], zbsb[:, :], OZ[oc % 2][:, :])
                    tok_ao[oc] = sch.emit("vector", mkao,
                                          waits=[("omm", t_omm), ("zbm", t_zb),
                                                 ("mstA", prev.get("mstA", 0))],
                                          inc="aoe", fence=True)
                prev["zbm"] = C["zbm"]
                tok_s, tok_q = {}, {}
                for oc in range(NOC):
                    def mks(h, oc=oc):
                        h.tensor_add(sfull[oc][:, :], sfull[oc][:, :], ht[oc][:, :])
                        return h.tensor_copy(sbf[oc][:, :], sfull[oc][:, :])
                    tok_s[oc] = sch.emit("vector", mks,
                                         waits=[("aoe", tok_ao[oc]), ("lh", tok_lh[oc])],
                                         inc="sA", fence=True)
                    tok_q[oc] = sch.emit("scalar", lambda h, oc=oc: h.activation(
                        s2bf[oc][:, :], sfull[oc][:, :], AF.Square),
                        waits=[("sA", tok_s[oc])], inc="qA", fence=True)
                pre = [("sA", tok_s[7]), ("qA", tok_q[7]), ("ones", t_ones),
                       ("aoe", C["aoe"]), ("ldc", 32)] + last_stat_users
                gcol = [gs[:, oc:oc + 1] for oc in range(NOC)]
                bcol = [bs[:, oc:oc + 1] for oc in range(NOC)]
                bcw = list(last_bc_readers) + [("aoe", C["aoe"]),
                                               ("gvd", prev.get("gvd", 0)),
                                               ("ropk", prev.get("ropk", 0))]
                ap_extra = [[("sthn", prev.get("sthn", {}).get(oc, 0)),
                             ("hbc", prev.get("hbc", {}).get(oc, 0))] for oc in range(NOC)]
                ps_stat = [OZ[0][0:1, :], OZ[1][0:1, :]]
                ps_b = [PP[0][:, :], PP[1][:, :]]
                apA = _emit_ln(sch, sbf, s2bf, sfull, onescol, onesrow, ps_stat, ps_b,
                               scratch, hnew, None, 1e-12, gcol=gcol, bcol=bcol,
                               pre_tokens=pre, bc_waits=bcw,
                               apply_waits=ap_extra, tag="A")
                last_bc_readers = [("apwA", apA[7])]
                last_stat_users = [("stfA", C["stfA"])]
                prev["mstA"] = C["mstA"]
                prev["stfA"] = C["stfA"]
                prev["apwA"] = C["apwA"]
                tok_hn = {}
                for oc in range(NOC):
                    tok_hn[oc] = sch.emit("sync", lambda h, oc=oc, tt=tt: h.dma_start(
                        out=hn[oc * 128:(oc + 1) * 128, tt * 512:(tt + 1) * 512],
                        in_=hnew[oc][:, :]),
                        waits=[("apwA", apA[oc])], inc="sthn", k=16)
                prev["sthn"] = tok_hn
                prev["aoe"] = tok_ao
                prev["sA"] = tok_s
                srcf = hnew
                src_gate = [("apwA", apA[oc]) for oc in range(NOC)]
            else:
                srcf = ht
                src_gate = [("lh", tok_lh[oc]) for oc in range(NOC)]

            if do_proj:
                tok_hb = {}
                for ic in range(NOC):
                    w = [src_gate[ic], ("pmv", prev.get("pmv", 0))]
                    tok_hb[ic] = sch.emit("vector", lambda h, ic=ic, srcf=srcf: h.tensor_copy(
                        hbnew[ic][:, :], srcf[ic][:, :]), waits=w, inc="hbc", fence=True)
                prev["hbc"] = tok_hb
                t_qpd, t_kpd, t_rok, t_gv2 = {}, {}, {}, {}
                for it in range(NOC + 2):
                    oc2 = it - 2
                    if oc2 >= 0:
                        for tb in range(4):
                            t_t1 = sch.emit("tensor", lambda h, tb=tb, oc2=oc2: h.transpose(
                                OZ[tb % 2].bitcast(BF16)[:, 0:128],
                                krb[oc2 % 2][:, tb * 128:(tb + 1) * 128], idn[:, :]),
                                waits=[("ropk", t_rok[oc2]), ("trev", max(0, C.get("trev", 0) - 1)),
                                       ("ldw", C["ldw"]), ("stfA", C.get("stfA", 0))],
                                inc="trm")
                            sch.emit("vector", lambda h, tb=tb: h.tensor_copy(
                                krtok[tb][:, :], OZ[tb % 2].bitcast(BF16)[:, 0:128]),
                                waits=[("trm", t_t1), ("kvm", C.get("kvm", 0))],
                                inc="trev", fence=True)
                        for tb in range(4):
                            t_t2 = sch.emit("tensor", lambda h, tb=tb, oc2=oc2: h.transpose(
                                OZ[tb % 2].bitcast(BF16)[:, 0:128],
                                vbt[oc2 % 2][:, tb * 128:(tb + 1) * 128], idn[:, :]),
                                waits=[("gvd", t_gv2[oc2]), ("trev", max(0, C.get("trev", 0) - 1))],
                                inc="trm")
                            sch.emit("vector", lambda h, tb=tb: h.tensor_copy(
                                vtok[tb][:, :], OZ[tb % 2].bitcast(BF16)[:, 0:128]),
                                waits=[("trm", t_t2), ("kvm", C.get("kvm", 0))],
                                inc="trev", fence=True)

                        def mkkvmm(h, oc=oc2, b=b):
                            for hh in range(2):
                                for tb in range(4):
                                    inst = h.matmul(KV[b][64 * hh:64 * hh + 64,
                                                          64 * oc:64 * oc + 64],
                                                    krtok[tb][:, 64 * hh:64 * hh + 64],
                                                    vtok[tb][:, 64 * hh:64 * hh + 64],
                                                    start=(tb == 0), stop=(tb == 3))
                            return inst
                        t_kvm = sch.emit("tensor", mkkvmm,
                                         waits=[("trev", C["trev"]), ("kve", C.get("kve", 0))],
                                         inc="kvm")

                        def mkkve(h, oc=oc2, b=b, tt=tt):
                            reg = kvsb[:, b * 512 + 64 * oc: b * 512 + 64 * oc + 64]
                            src = KV[b][:, 64 * oc:64 * oc + 64]
                            if tt % 2 == 0:
                                return h.tensor_copy(reg, src)
                            return h.tensor_add(reg, reg, src)
                        sch.emit("vector", mkkve, waits=[("kvm", t_kvm), ("z0", 1)],
                                 inc="kve", fence=True)
                    oc1 = it - 1
                    if 0 <= oc1 < NOC:
                        t_rq = sch.emit("tensor", lambda h, oc=oc1: h.matmul(
                            RQ[:, :], rms[:, :], qpb[oc % 2][:, :], start=True, stop=True),
                            waits=[("qpd", t_qpd[oc1]), ("ropq", C.get("ropq", 0))], inc="rq")

                        def mkrope_q(h, oc=oc1):
                            h.tensor_mul(qcosf[:, :], qpf[oc % 2][:, :], coss[oc % 2][:, :])
                            h.tensor_mul(rsinf[:, :], RQ[:, :], sins[oc % 2][:, :])
                            return h.tensor_add(qrb[oc % 2][:, :], qcosf[:, :], rsinf[:, :])
                        t_ro = sch.emit("vector", mkrope_q,
                                        waits=[("rq", t_rq), ("lcs", C["lcs"]),
                                               ("stqr", C.get("stqr", 0))],
                                        inc="ropq", fence=True)
                        sch.emit("sync", lambda h, oc=oc1, tt=tt: h.dma_start(
                            out=qr_o[oc * 128:(oc + 1) * 128, tt * 512:(tt + 1) * 512],
                            in_=qrb[oc % 2][:, :]), waits=[("ropq", t_ro)], inc="stqr", k=16)
                        sch.emit("sync", lambda h, oc=oc1, tt=tt: h.dma_start(
                            out=qp_o[oc * 128:(oc + 1) * 128, tt * 512:(tt + 1) * 512],
                            in_=qpb[oc % 2][:, :]), waits=[("qpd", t_qpd[oc1])], inc="stqp", k=16)
                        t_rk = sch.emit("tensor", lambda h, oc=oc1: h.matmul(
                            RK[:, :], rms[:, :], kpb[oc % 2][:, :], start=True, stop=True),
                            waits=[("kpd", t_kpd[oc1]), ("ropk", C.get("ropk", 0))], inc="rk")

                        def mkrope_k(h, oc=oc1):
                            h.tensor_mul(qcosf[:, :], kpf[oc % 2][:, :], coss[oc % 2][:, :])
                            h.tensor_mul(rsinf[:, :], RK[:, :], sins[oc % 2][:, :])
                            return h.tensor_add(krb[oc % 2][:, :], qcosf[:, :], rsinf[:, :])
                        t_rok[oc1] = sch.emit("vector", mkrope_k,
                                              waits=[("rk", t_rk), ("lcs", C["lcs"]),
                                                     ("trm", C.get("trm", 0))],
                                              inc="ropk", fence=True)
                    oc = it
                    if oc < NOC:
                        sch.emit("sync", lambda h, oc=oc, tt=tt: h.dma_start(
                            out=coss[oc % 2][:, :],
                            in_=cosb[oc * 128:(oc + 1) * 128, tt * 512:(tt + 1) * 512]),
                            waits=[("ropk", max(0, C.get("ropk", 0) - 1))], inc="lcs", k=16)
                        sch.emit("sync", lambda h, oc=oc, tt=tt: h.dma_start(
                            out=sins[oc % 2][:, :],
                            in_=sinb[oc * 128:(oc + 1) * 128, tt * 512:(tt + 1) * 512]),
                            inc="lcs", k=16)
                        m0 = pst["m"]

                        def mkproj(w_, m):
                            def mm(h, w_=w_, oc=oc, m=m):
                                for ic in range(NOC):
                                    inst = h.matmul(PP[m % 2][:, :],
                                                    wsb[w_][ic][:, oc * 128:(oc + 1) * 128],
                                                    hbnew[ic][:, :],
                                                    start=(ic == 0), stop=(ic == NOC - 1))
                                return inst
                            wts_ = [("ldw", C["ldw"]), ("hbc", tok_hb[7])]
                            if m - 2 in pst["gel"]:
                                wts_.append(pst["gel"][m - 2])
                            return sch.emit("tensor", mm, waits=wts_, inc="pm" + w_)
                        t_q = mkproj("q", m0)
                        t_gq = sch.emit("scalar", lambda h, oc=oc, m=m0: h.activation(
                            qpf[oc % 2][:, :], PP[m % 2][:, :], AF.Gelu,
                            bias=bsb["q"][:, oc:oc + 1]),
                            waits=[("pmq", t_q), ("ldc3", 48), ("qpd", C.get("qpd", 0)),
                                   ("ropq", C.get("ropq", 0))],
                            inc="gqd", fence=True)
                        pst["gel"][m0] = ("gqd", t_gq)

                        def mkqp2(h, oc=oc):
                            h.tensor_scalar_add(qpf[oc % 2][:, :], qpf[oc % 2][:, :], 0.21)
                            return h.tensor_copy(qpb[oc % 2][:, :], qpf[oc % 2][:, :])
                        t_qpd[oc] = sch.emit("vector", mkqp2,
                                             waits=[("gqd", t_gq), ("rq", C.get("rq", 0)),
                                                    ("stqp", C.get("stqp", 0))],
                                             inc="qpd", fence=True)
                        t_k = mkproj("k", m0 + 1)
                        t_gk = sch.emit("scalar", lambda h, oc=oc, m=m0 + 1: h.activation(
                            kpf[oc % 2][:, :], PP[m % 2][:, :], AF.Gelu,
                            bias=bsb["k"][:, oc:oc + 1]),
                            waits=[("pmk", t_k), ("kpd", C.get("kpd", 0)),
                                   ("ropk", C.get("ropk", 0))],
                            inc="gkd", fence=True)
                        pst["gel"][m0 + 1] = ("gkd", t_gk)

                        def mkkp(h, oc=oc, tt=tt):
                            h.tensor_scalar_add(kpf[oc % 2][:, :], kpf[oc % 2][:, :], 0.21)
                            h.tensor_copy(kpb[oc % 2][:, :], kpf[oc % 2][:, :])
                            return h.reduce_sum(kmtmp[:, tt * 8 + oc:tt * 8 + oc + 1],
                                                kpf[oc % 2][:, :], axis=AX.X)
                        t_kpd[oc] = sch.emit("vector", mkkp,
                                             waits=[("gkd", t_gk), ("rk", C.get("rk", 0))],
                                             inc="kpd", fence=True)
                        t_v = mkproj("v", m0 + 2)
                        t_gv2[oc] = sch.emit("scalar", lambda h, oc=oc, m=m0 + 2: h.activation(
                            vbt[oc % 2][:, :], PP[m % 2][:, :], AF.Identity,
                            bias=bsb["v"][:, oc:oc + 1]),
                            waits=[("pmv", t_v), ("trm", C.get("trm", 0))],
                            inc="gvd", fence=True)
                        pst["gel"][m0 + 2] = ("gvd", t_gv2[oc])
                        pst["m"] = m0 + 3
                prev["pmv"] = C["pmv"]
                prev["gvd"] = C["gvd"]
                prev["ropk"] = C["ropk"]
                prev["trev"] = C["trev"]
        if do_proj:
            def mkpart(h):
                for b_ in range(2):
                    for oc in range(NOC):
                        c = b_ * 8 + oc
                        inst = h.tensor_add(km16[:, c:c + 1],
                                            kmtmp[:, (2 * b_) * 8 + oc:(2 * b_) * 8 + oc + 1],
                                            kmtmp[:, (2 * b_ + 1) * 8 + oc:(2 * b_ + 1) * 8 + oc + 1])
                return inst
            t_pt = sch.emit("vector", mkpart,
                            waits=[("kpd", C["kpd"]), ("z0", 1)],
                            inc="ptd", fence=True)
            sch.emit("sync", lambda h: h.dma_start(out=part_o[:, 0:1024], in_=kvsb[:, :]),
                     waits=[("kve", C["kve"])], inc="stpt", k=16)
            sch.emit("sync", lambda h: h.dma_start(out=part_o[:, 1024:1040], in_=km16[:, :]),
                     waits=[("ptd", t_pt)], inc="stpt", k=16)
            sch.emit("sync", None, waits=[("stpt", 32), ("stqr", C["stqr"]),
                                          ("stqp", C["stqp"])])
        if do_fin:
            sch.emit("sync", None, waits=[("sthn", C["sthn"])])
        sch.build(block)
    return nc


def _rope_tables(j):
    inv = (1.0 / 10000.0 ** (np.arange(0, D, 2, dtype=np.float32) / np.float32(D))).astype(np.float32)
    inv2 = np.repeat(inv, 2)
    cos = np.empty((D, TOK), np.float32)
    sin = np.empty((D, TOK), np.float32)
    for i, (b, c) in enumerate(_chunks_of(j)):
        pos = np.arange(c * CH, (c + 1) * CH, dtype=np.float32)
        fr = pos[None, :] * inv2[:, None]
        cos[:, i * CH:(i + 1) * CH] = np.cos(fr)
        sin[:, i * CH:(i + 1) * CH] = np.sin(fr)
    return cos.astype(BF), sin.astype(BF)


def _rmat():
    r = np.zeros((128, 128), np.float32)
    for i in range(64):
        r[2 * i + 1, 2 * i] = -1.0
        r[2 * i, 2 * i + 1] = 1.0
    return r


def _e2():
    e = np.zeros((2, 128), np.float32)
    e[0, 0:64] = 1.0
    e[1, 64:128] = 1.0
    return e


def _cols128(v):
    n = v.shape[0] // 128
    return np.ascontiguousarray(np.asarray(v).reshape(n, 128).T)



def kernel(**inputs):
    g = {k: np.ascontiguousarray(np.asarray(v)) for k, v in inputs.items()}
    cores = list(range(NCORES))
    ident = np.eye(128, dtype=np.float32)

    xs = _stage_xT(g["L0_attn_in_features"])
    wq = np.ascontiguousarray(g["L0_attn_Wq"].T)
    wk = np.ascontiguousarray(g["L0_attn_Wk"].T)
    wv = np.ascontiguousarray(g["L0_attn_Wv"].T)
    res = run_bass_kernel_spmd(build_p1(), [
        {"x": xs[j], "wqt": wq, "wkt": wk, "wvt": wv} for j in cores], cores)
    qt = [np.asarray(res.results[j]["qt"]) for j in cores]
    kt = [np.asarray(res.results[j]["kt"]) for j in cores]
    vt = [np.asarray(res.results[j]["vt"]) for j in cores]

    qh, kh, vh = _assemble_bh(qt, kt, vt)
    mask = _causal_mask_strips()
    ones1 = np.ones((1, 64), np.float32)
    ident2 = np.concatenate([np.eye(64), np.eye(64)], axis=0).astype(np.float32)
    res = run_bass_kernel_spmd(build_p2(), [
        {"qh": qh[j], "kh": kh[j], "vh": vh[j], "mask": mask, "ones1": ones1,
         "ident2": ident2} for j in cores], cores)
    ao = [np.asarray(res.results[j]["ao"]) for j in cores]

    ats = _assemble_attnT(ao)

    def _pan_t(wT, nhc):
        n_ic = wT.shape[0] // 128
        return np.ascontiguousarray(
            wT.reshape(n_ic, 128, nhc, 128).transpose(1, 2, 0, 3).reshape(128, nhc * n_ic * 128))

    res = run_bass_kernel_spmd(build_p3(), [
        {"x": xs[j], "at": ats[j], "wot": np.ascontiguousarray(g["L0_attn_Wo"].T),
         "wgt": _pan_t(g["L0_ffn_w_gate"].T, NHC),
         "wut": _pan_t(g["L0_ffn_w_up"].T, NHC),
         "wdt": _pan_t(g["L0_ffn_w_down"].T, NOC)} for j in cores], cores)
    x1s = [np.asarray(res.results[j]["x1"], np.float32) for j in cores]
    x1full = _unstage_xT(x1s)

    x1h = _stage_halo(x1full)
    w1m, w2m = _conv_mats(g["L1_attn_conv1_w"], g["L1_attn_conv2_w"])
    emks = []
    for j in cores:
        emk = np.ones((128, 32), np.float32)
        for stg in range(8):
            i, st = divmod(stg, 2)
            b, c = _chunks_of(j)[i]
            if c == 0 and st == 0:
                emk[:, stg * 4:stg * 4 + 2] = 0.0
            if c == 15 and st == 1:
                emk[:, stg * 4 + 2:stg * 4 + 4] = 0.0
        emks.append(emk)
    res = run_bass_kernel_spmd(build_p5(), [
        {"x1h": x1h[j], "x1": x1s[j], "w1m": w1m, "w2m": w2m,
         "b1c": _cols128(g["L1_attn_conv1_b"]), "b2c": _cols128(g["L1_attn_conv2_b"]),
         "g0c": _cols128(g["L1_attn_ln0_g"]), "b0c": _cols128(g["L1_attn_ln0_b"]),
         "emk": emks[j]} for j in cores], cores)
    hs = [np.asarray(res.results[j]["h0"], np.float32) for j in cores]

    tables = [_rope_tables(j) for j in cores]
    rm = _rmat()
    e2 = _e2()

    def proj_in(r):
        return {"wq": np.ascontiguousarray(g["L1_attn_Wq"][r]),
                "wk": np.ascontiguousarray(g["L1_attn_Wk"][r]),
                "wv": np.ascontiguousarray(g["L1_attn_Wv"][r]),
                "bq": _cols128(g["L1_attn_bq"][r]), "bk": _cols128(g["L1_attn_bk"][r]),
                "bv": _cols128(g["L1_attn_bv"][r]), "rmat": rm, "ident": ident}

    def fin_in(r, parts, qp_j, qr_j):
        return {"parts": parts, "qp_i": qp_j, "qr_i": qr_j,
                "lnag": _cols128(g["L1_attn_lnA_g"][r]),
                "lnab": _cols128(g["L1_attn_lnA_b"][r]), "e2": e2}

    res = run_bass_kernel_spmd(build_p6("first"), [
        {"hin": hs[j], "cosb": tables[j][0], "sinb": tables[j][1], **proj_in(0)}
        for j in cores], cores)
    qp = [np.asarray(res.results[j]["qp_o"]) for j in cores]
    qr = [np.asarray(res.results[j]["qr_o"]) for j in cores]
    parts = np.stack([np.asarray(res.results[j]["part_o"], np.float32) for j in cores])

    nc_mid = None
    for r in range(4):
        if r < 3:
            if nc_mid is None:
                nc_mid = build_p6("mid")
            res = run_bass_kernel_spmd(nc_mid, [
                {"hin": hs[j], "cosb": tables[j][0], "sinb": tables[j][1],
                 **fin_in(r, parts, qp[j], qr[j]), **proj_in(r + 1)} for j in cores], cores)
            hs = [np.asarray(res.results[j]["hn"], np.float32) for j in cores]
            qp = [np.asarray(res.results[j]["qp_o"]) for j in cores]
            qr = [np.asarray(res.results[j]["qr_o"]) for j in cores]
            parts = np.stack([np.asarray(res.results[j]["part_o"], np.float32) for j in cores])
        else:
            res = run_bass_kernel_spmd(build_p6("last"), [
                {"hin": hs[j], **fin_in(r, parts, qp[j], qr[j])} for j in cores], cores)
            hs = [np.asarray(res.results[j]["hn"], np.float32) for j in cores]

    w1t = np.ascontiguousarray(
        g["L1_attn_W1"].reshape(8, 128, 32, 128).transpose(1, 2, 0, 3).reshape(128, 32768))
    w2t = np.ascontiguousarray(
        g["L1_attn_W2"].reshape(32, 128, 8, 128).transpose(1, 2, 0, 3).reshape(128, 32768))
    res = run_bass_kernel_spmd(build_p7(), [
        {"hin": hs[j], "x1": x1s[j], "w1": w1t, "w2": w2t,
         "b1r": _cols128(g["L1_attn_b1"]), "b2r": _cols128(g["L1_attn_b2"]),
         "g2r": _cols128(g["L1_attn_ln2_g"]), "bb2r": _cols128(g["L1_attn_ln2_b"])}
        for j in cores], cores)
    x2s = [np.asarray(res.results[j]["x2"], np.float32) for j in cores]

    res = run_bass_kernel_spmd(build_p8(), [
        {"x2": x2s[j], "wst": np.ascontiguousarray(g["L1_ffn_weight"].T)}
        for j in cores], cores)
    outs = [np.asarray(res.results[j]["out"], np.float32) for j in cores]
    return _unstage_xT(outs)
